# version 06
# baseline (speedup 1.0000x reference)
import os
import contextlib
import numpy as np
import ml_dtypes
import concourse.bass as bass
import concourse.mybir as mybir
from concourse.bass_utils import run_bass_kernel_spmd

F32 = mybir.dt.float32
BF16 = mybir.dt.bfloat16
I32 = mybir.dt.int32
AF = mybir.ActivationFunctionType
ALU = mybir.AluOpType
AX = mybir.AxisListType

L, DM, DIN, H, Q = 2048, 2048, 4096, 64, 256
NCH, NT, KD = L // Q, L // 128, DM // 128
NOT_Z, NOT_XBC = 32, 34
NOT_IN = NOT_Z + NOT_XBC + 1
FF, NOT_F, KF, OT2 = 8192, 64, 64, 16
RMS_EPS = 1.1920929e-07
LN_EPS = 1e-5
MASK_NEG = -1e30
MAGICR = 12582912.0
ENGS = ("sync", "scalar", "vector", "tensor", "gpsimd")
_PH = int(os.environ.get("MK_PH", "9"))
_SKIP = set(x for x in os.environ.get("MK_SKIP", "").split(",") if x)


def build_nc():
    nc = bass.Bass("TRN2", target_bir_lowering=False, detect_race_conditions=False)

    u = nc.dram_tensor("u", [L, DM], F32, kind="ExternalInput").ap()
    winT = nc.dram_tensor("winT", [NOT_IN, 128, KD, 128], F32, kind="ExternalInput").ap()
    w1T = nc.dram_tensor("w1T", [NOT_F, 128, KD, 128], F32, kind="ExternalInput").ap()
    w2T = nc.dram_tensor("w2T", [OT2, 128, KF, 128], F32, kind="ExternalInput").ap()
    woutT = nc.dram_tensor("woutT", [DIN, DM], F32, kind="ExternalInput").ap()
    convw = nc.dram_tensor("convw", [128, NOT_XBC * 4], F32, kind="ExternalInput").ap()
    convb = nc.dram_tensor("convb", [128, NOT_XBC], F32, kind="ExternalInput").ap()
    a_log = nc.dram_tensor("a_log", [64, 1], F32, kind="ExternalInput").ap()
    d_in = nc.dram_tensor("d_in", [1, 64], F32, kind="ExternalInput").ap()
    dt_bias = nc.dram_tensor("dt_bias", [64, 1], F32, kind="ExternalInput").ap()
    norm_w = nc.dram_tensor("norm_w", [128, 32], F32, kind="ExternalInput").ap()
    b1 = nc.dram_tensor("b1", [128, NOT_F], F32, kind="ExternalInput").ap()
    ln_g = nc.dram_tensor("ln_g", [128, NOT_F], F32, kind="ExternalInput").ap()
    ln_b = nc.dram_tensor("ln_b", [128, NOT_F], F32, kind="ExternalInput").ap()
    b2 = nc.dram_tensor("b2", [128, OT2], F32, kind="ExternalInput").ap()
    identb = nc.dram_tensor("identb", [128, 128], BF16, kind="ExternalInput").ap()
    identf = nc.dram_tensor("identf", [128, 128], F32, kind="ExternalInput").ap()
    onesrow_b = nc.dram_tensor("onesrow_b", [1, 128], BF16, kind="ExternalInput").ap()
    onescol_b = nc.dram_tensor("onescol_b", [128, 1], BF16, kind="ExternalInput").ap()
    onesrow_f = nc.dram_tensor("onesrow_f", [1, 128], F32, kind="ExternalInput").ap()
    maskt = nc.dram_tensor("maskt", [128, 512], F32, kind="ExternalInput").ap()
    sel2in = nc.dram_tensor("sel2in", [2, 256], F32, kind="ExternalInput").ap()

    out = nc.dram_tensor("out", [L, DM], F32, kind="ExternalOutput").ap()
    _SCRK = "ExternalOutput" if os.environ.get("MK_DBGOUT") else "Internal"
    xq1_sp = nc.dram_tensor("xq1_sp", [KD, 128, L], BF16, kind=_SCRK).ap()
    z_sp = nc.dram_tensor("z_sp", [NOT_Z, 128, L], BF16, kind=_SCRK).ap()
    xfm_sp = nc.dram_tensor("xfm_sp", [32, 128, L], BF16, kind=_SCRK).ap()
    xtm_sp = nc.dram_tensor("xtm_sp", [32, 128, 2048], BF16, kind=_SCRK).ap()
    yg_sp = nc.dram_tensor("yg_sp", [NCH, 128, 8192], BF16, kind=_SCRK).ap()
    aacc_sp = nc.dram_tensor("aacc_sp", [NT, 128, DM], BF16, kind=_SCRK).ap()
    h_sp = nc.dram_tensor("h_sp", [NOT_F, 128, L], BF16, kind=_SCRK).ap()
    lnh_sp = nc.dram_tensor("lnh_sp", [NOT_F, 128, L], BF16, kind=_SCRK).ap()
    w2q_sp = nc.dram_tensor("w2q_sp", [OT2, 128, KF * 128], BF16, kind=_SCRK).ap()

    stack = contextlib.ExitStack()
    with stack:
        E = stack.enter_context

        wf = [E(nc.sbuf_tensor(f"wf{i}", [128, 2048], F32)) for i in range(2)]
        wq_b = [E(nc.sbuf_tensor(f"wq_b{i}", [128, 2048], BF16)) for i in range(2)]
        sq_scr = E(nc.sbuf_tensor("sq_scr", [128, 2048], F32))
        csum = E(nc.sbuf_tensor("csum", [128, 16], F32))
        csq = E(nc.sbuf_tensor("csq", [128, 16], F32))
        cmean = E(nc.sbuf_tensor("cmean", [128, 16], F32))
        crstd = E(nc.sbuf_tensor("crstd", [128, 16], F32))
        cr2 = E(nc.sbuf_tensor("cr2", [128, 16], F32))
        camax = E(nc.sbuf_tensor("camax", [128, 16], F32))
        cr2s1 = E(nc.sbuf_tensor("cr2s1", [128, 16], F32))
        ctmp = E(nc.sbuf_tensor("ctmp", [128, 16], F32))
        acol = E(nc.sbuf_tensor("acol", [128, 4], F32))
        swc = E(nc.sbuf_tensor("swc", [128, 4], F32))
        absacc = E(nc.sbuf_tensor("absacc", [128, 2], F32))
        pcol = E(nc.sbuf_tensor("pcol", [64, 4], F32))
        d_row = E(nc.sbuf_tensor("d_row", [1, 64], F32))
        d_rep = E(nc.sbuf_tensor("d_rep", [128, 64], F32))
        normw_sb = E(nc.sbuf_tensor("normw_sb", [128, 32], F32))
        bcols = E(nc.sbuf_tensor("bcols", [128, NOT_F], F32))
        gcols = E(nc.sbuf_tensor("gcols", [128, NOT_F], F32))
        lbcols = E(nc.sbuf_tensor("lbcols", [128, NOT_F], F32))
        b2cols = E(nc.sbuf_tensor("b2cols", [128, OT2], F32))
        identb_sb = E(nc.sbuf_tensor("identb_sb", [128, 128], BF16))
        identf_sb = E(nc.sbuf_tensor("identf_sb", [128, 128], F32))
        ones_r_b = E(nc.sbuf_tensor("ones_r_b", [1, 128], BF16))
        ones_c_b = E(nc.sbuf_tensor("ones_c_b", [128, 1], BF16))
        ones_r_f = E(nc.sbuf_tensor("ones_r_f", [1, 128], F32))
        sel2 = E(nc.sbuf_tensor("sel2", [2, 256], F32))
        mask_sb = E(nc.sbuf_tensor("mask_sb", [128, 512], F32))

        ps = [E(nc.psum_tensor(f"ps{i}", [128, 512], F32)) for i in range(6)]
        psA = E(nc.psum_tensor("psA", [128, 1024], BF16))
        psB = E(nc.psum_tensor("psB", [128, 1024], BF16))

        class Phys:
            __slots__ = ("h", "n", "name")
            def __init__(self, name):
                self.h = E(nc.semaphore(name)); self.n = 0; self.name = name

        class Sem:
            __slots__ = ("p", "base", "lname")
            def __init__(self, phys, lname):
                self.p = phys; self.base = phys.n; self.lname = lname
            @property
            def n(self):
                return self.p.n - self.base
            def inc(self, instr, k=1):
                instr.then_inc(self.p.h, k); self.p.n += k
            def dinc(self, instr):
                self.inc(instr, 16)

        SEMS = {}
        _pool = []
        _pool_next = {}
        SW_SEMS = {"p0_xq1w", "p1_wld", "p1_zw", "p1_xwa", "p1_xwb", "p1_xtw", "p2_ch", "p2_row",
                   "p2_ygw", "pb_wo", "pb_ygla", "pb_yglb", "pb_u2a", "pb_u2b", "pb_aw", "pb_w2wa", "pb_w2wb",
                   "p3_xq1l", "p3_ivd", "p3_hwa", "p3_hwb", "p4_lnwa", "p4_lnwb", "p4_flt", "p4_flt2",
                   "p4_flt3", "p5_w2ra", "p5_w2rb", "p5_aal", "p5_out"}
        def sem(name):
            if name in SEMS:
                return SEMS[name]
            if (name.startswith("bar_") or name.startswith("chn_")
                    or name in ("pre", "prep") or name in SW_SEMS):
                SEMS[name] = Sem(Phys(name), name)
                return SEMS[name]
            pref = name.split("_")[0]
            idx = _pool_next.get(pref, 0)
            _pool_next[pref] = idx + 1
            while len(_pool) <= idx:
                _pool.append(Phys(f"pool{len(_pool)}"))
            SEMS[name] = Sem(_pool[idx], name)
            return SEMS[name]

        waited = {}
        def wait(eng, engname, s, val=None):
            v = (s.p.n) if val is None else (s.base + val)
            if v <= s.base and val is not None and val <= 0:
                return
            if v <= 0:
                return
            k = (engname, s.p.name)
            if waited.get(k, -1) >= v:
                return
            eng.wait_ge(s.p.h, v)
            waited[k] = v

        def ppsem(name):
            return [sem(name + "a"), sem(name + "b")]

        def ppwait(e, en, ss, t):
            wait(e, en, ss[t % 2], 16 * (t // 2 + 1))

        def ppissue(e, en, ss, t, instr):
            ss[t % 2].inc(instr, 16)

        blk = E(nc.Block())
        _chain = sem_holder = {}
        def schain(e, engname, instr):
            cs = sem("chain_" + engname) if False else None
            s_ = SEMS.get("chn_" + engname)
            if s_ is None:
                s_ = sem("chn_" + engname)
            s_.inc(instr)
            e.wait_ge(s_.p.h, s_.p.n)
        _cur = [0]

        def phase_mark(k):
            _cur[0] = k

        def on(engname):
            def deco(f):
                if _cur[0] > _PH:
                    return
                getattr(blk, engname)(lambda e: f(e, engname))
            return deco

        def barrier(name, extra=()):
            s = sem("bar_" + name)
            for engname in ENGS:
                @on(engname)
                def _(e, en, engname=engname):
                    for (wn, ws, wv) in extra:
                        if wn == engname:
                            wait(e, en, ws, wv)
                    s.inc(e.nop())
            for engname in ENGS:
                @on(engname)
                def _(e, en):
                    wait(e, en, s, 5)
            return s

        s_pre = sem("pre")
        s_prep = sem("prep")

        @on("sync")
        def _(e, en):
            for (dst, src) in ((identb_sb[:, :], identb), (identf_sb[:, :], identf),
                               (ones_r_b[:, :], onesrow_b), (ones_c_b[:, :], onescol_b),
                               (ones_r_f[:, :], onesrow_f), (mask_sb[:, :], maskt),
                               (sel2[:, :], sel2in), (pcol[:, 0:1], a_log),
                               (pcol[:, 1:2], dt_bias), (d_row[:, :], d_in),
                               (normw_sb[:, :], norm_w), (bcols[:, :], b1),
                               (gcols[:, :], ln_g), (lbcols[:, :], ln_b),
                               (b2cols[:, :], b2)):
                s_pre.dinc(e.dma_start(out=dst, in_=src))

        @on("scalar")
        def _(e, en):
            wait(e, en, s_pre)
            schain(e, en, e.activation(pcol[:, 2:3], pcol[:, 0:1], AF.Exp))
            instr = e.mul(pcol[:, 2:3], pcol[:, 2:3], -1.0)
            s_prep.inc(instr)

        @on("tensor")
        def _(e, en):
            wait(e, en, s_pre)
            instr = e.matmul(ps[0][:, 0:64], ones_r_f[0:1, :], d_row[0:1, :], start=True, stop=True)
            s_prep.inc(instr)

        @on("vector")
        def _(e, en):
            wait(e, en, s_prep, 2)
            e.memset(acol[:, :], 0.0)
            e.memset(absacc[:, :], 0.0)
            instr = e.tensor_copy(d_rep[:, :], ps[0][:, 0:64])
            s_prep.inc(instr)

        barrier("pro")

        phase_mark(1)
        scspan = contextlib.ExitStack()
        Esp = scspan.enter_context
        B_fm = Esp(nc.sbuf_tensor("B_fm", [128, L], BF16))
        C_fm = Esp(nc.sbuf_tensor("C_fm", [128, L], BF16))
        B_tm = Esp(nc.sbuf_tensor("B_tm", [128, NT * 128], BF16))
        dt_fm = Esp(nc.sbuf_tensor("dt_fm", [64, L], F32))
        acum_fm = Esp(nc.sbuf_tensor("acum_fm", [64, L], F32))
        acum_hi = Esp(nc.sbuf_tensor("acum_hi", [64, L], BF16))
        acum_lo = Esp(nc.sbuf_tensor("acum_lo", [64, L], BF16))
        acum_tm = Esp(nc.sbuf_tensor("acum_tm", [128, NT * 64], F32))
        nacum_tm = Esp(nc.sbuf_tensor("nacum_tm", [128, NT * 64], F32))
        dt_tm = Esp(nc.sbuf_tensor("dt_tm", [128, NT * 64], F32))
        dtdec_fm = Esp(nc.sbuf_tensor("dtdec_fm", [64, Q], F32))
        dtdec_tm = Esp(nc.sbuf_tensor("dtdec_tm", [128, 2 * 64], F32))

        with contextlib.ExitStack() as sc0:
            scx = contextlib.ExitStack()
            E0 = scx.enter_context
            xn_fm = E0(nc.sbuf_tensor("xn_fm", [128, KD * L], BF16))
            trst = E0(nc.sbuf_tensor("trst", [128, 2048], BF16))
            sc0i = contextlib.ExitStack()
            E0i = sc0i.enter_context
            u_tl = [E0i(nc.sbuf_tensor(f"u_tl{i}", [128, DM], F32)) for i in range(2)]
            xn_tm = E0i(nc.sbuf_tensor("xn_tm", [128, DM], F32))
            xn_tm_b = E0i(nc.sbuf_tensor("xn_tm_b", [128, DM], BF16))
            xq_tm_i = E0i(nc.sbuf_tensor("xq_tm_i", [128, DM], I32))
            xq_tm_b = E0i(nc.sbuf_tensor("xq_tm_b", [128, DM], BF16))

            s_u = ppsem("p0_u")
            s_st0 = sem("p0_st")
            s_sq0 = sem("p0_sq")
            s_c0 = sem("p0_c")
            s_u2p = ppsem("p0_u2")
            s_x2 = sem("p0_x2")
            s_tr0 = sem("p0_tr")
            s_ev0 = sem("p0_ev")
            s_xq1w = sem("p0_xq1w")

            @on("sync")
            def _(e, en):
                for t in range(NT):
                    if t >= 2:
                        wait(e, en, s_st0, t - 1)
                        wait(e, en, sem("p0_sqs"), t - 1)
                        wait(e, en, s_u[t % 2], 16 * (t // 2))
                    instr = e.dma_start(out=u_tl[t % 2][:, :], in_=u[t * 128:(t + 1) * 128, :])
                    ppissue(e, en, s_u, t, instr)
                for t in range(NT):
                    wait(e, en, s_c0, 3)
                    if t >= 2:
                        wait(e, en, s_x2, 2 * (t - 1))
                        wait(e, en, s_u2p[t % 2], 16 * (t // 2))
                    instr = e.dma_start(out=u_tl[t % 2][:, :], in_=u[t * 128:(t + 1) * 128, :])
                    ppissue(e, en, s_u2p, t, instr)

            @on("vector")
            def _(e, en):
                for t in range(NT):
                    ppwait(e, en, s_u, t)
                    wait(e, en, sem("p0_sqs"), t)
                    instr = e.tensor_reduce(csum[:, t:t + 1], u_tl[t % 2][:, :], AX.X, ALU.add)
                    s_st0.inc(instr)
                wait(e, en, sem("p0_sqs"), NT)
                wait(e, en, s_st0, NT)
                schain(e, en, e.tensor_scalar(cmean[:, :], csum[:, :], 1.0 / DM, None, ALU.mult))
                schain(e, en, e.tensor_scalar(ctmp[:, :], csq[:, :], 1.0 / DM, None, ALU.mult))
                schain(e, en, e.scalar_tensor_tensor(csq[:, :], cmean[:, :], acol[:, 0:1], cmean[:, :],
                                       ALU.bypass, ALU.mult))
                schain(e, en, e.tensor_tensor(csq[:, :], ctmp[:, :], csq[:, :], ALU.subtract))
                instr = e.tensor_scalar(csq[:, :], csq[:, :], 1.0, LN_EPS, ALU.mult, ALU.add)
                s_c0.inc(instr)
                wait(e, en, s_sq0, 1)
                schain(e, en, e.reciprocal(crstd[:, :], ctmp[:, :]))
                schain(e, en, e.scalar_tensor_tensor(cr2[:, :], crstd[:, :], acol[:, 0:1], crstd[:, :],
                                       ALU.bypass, ALU.mult))
                instr = e.tensor_scalar(cr2[:, :], cr2[:, :], -LN_EPS, 1.0 + LN_EPS, ALU.mult, ALU.add)
                s_c0.inc(instr)
                wait(e, en, s_sq0, 2)
                instr = e.reciprocal(cr2[:, :], ctmp[:, :])
                s_c0.inc(instr)
                for t in range(NT):
                    ppwait(e, en, s_u2p, t)
                    ut = u_tl[t % 2]
                    schain(e, en, e.tensor_scalar(xn_tm[:, :], ut[:, :], cmean[:, t:t + 1], crstd[:, t:t + 1],
                                    ALU.subtract, ALU.mult))
                    schain(e, en, e.tensor_reduce(ctmp[:, t:t + 1], xn_tm[:, :], AX.X, ALU.max,
                                    apply_absolute_value=True))
                    schain(e, en, e.tensor_scalar(camax[:, t:t + 1], ctmp[:, t:t + 1], cr2[:, t:t + 1], 1e-5,
                                    ALU.mult, ALU.max))
                    schain(e, en, e.reciprocal(ctmp[:, t:t + 1], camax[:, t:t + 1]))
                    schain(e, en, e.scalar_tensor_tensor(cr2s1[:, t:t + 1], ctmp[:, t:t + 1], cr2[:, t:t + 1],
                                           acol[:, 0:1], ALU.mult, ALU.add))
                    schain(e, en, e.tensor_scalar(cr2s1[:, t:t + 1], cr2s1[:, t:t + 1], 127.0, None, ALU.mult))
                    schain(e, en, e.tensor_scalar(sq_scr[:, :], xn_tm[:, :], cr2s1[:, t:t + 1], None, ALU.mult))
                    schain(e, en, e.tensor_scalar(sq_scr[:, :], sq_scr[:, :], MAGICR, MAGICR, ALU.add, ALU.subtract))
                    schain(e, en, e.tensor_copy(xq_tm_i[:, :], sq_scr[:, :]))
                    if t >= 1:
                        wait(e, en, s_tr0, 4 * t)
                    e.tensor_copy(xq_tm_b[:, :], xq_tm_i[:, :])
                    instr = e.tensor_copy(xn_tm_b[:, :], xn_tm[:, :])
                    s_x2.inc(instr, 2)
                    wait(e, en, s_tr0, 4 * t + 2)
                    e.tensor_copy(
                        bass.AP(xn_fm, t * 128, [[KD * L, 128], [L, 8], [1, 128]]),
                        psA[:, :].rearrange("p (k c) -> p k c", k=8))
                    instr = e.tensor_copy(
                        bass.AP(xn_fm, 8 * L + t * 128, [[KD * L, 128], [L, 8], [1, 128]]),
                        psB[:, :].rearrange("p (k c) -> p k c", k=8))
                    s_ev0.inc(instr)
                    wait(e, en, s_tr0, 4 * t + 4)
                    if t >= 1:
                        wait(e, en, s_xq1w, 16 * t)
                    e.tensor_copy(trst[:, 0:1024], psA[:, :])
                    instr = e.tensor_copy(trst[:, 1024:2048], psB[:, :])
                    s_ev0.inc(instr)

            @on("scalar")
            def _(e, en):
                for t in range(NT):
                    ppwait(e, en, s_u, t)
                    wait(e, en, sem("p0_sqs"), t)
                    instr = e.activation(sq_scr[:, :], u_tl[t % 2][:, :], AF.Square,
                                         accum_out=csq[:, t:t + 1])
                    sem("p0_sqs").inc(instr)
                wait(e, en, s_c0, 1)
                instr = e.activation(ctmp[:, :], csq[:, :], AF.Sqrt)
                s_sq0.inc(instr)
                wait(e, en, s_c0, 2)
                instr = e.activation(ctmp[:, :], cr2[:, :], AF.Sqrt)
                s_sq0.inc(instr)
                for t in range(NT):
                    wait(e, en, s_ev0, 2 * t + 2)
                    wait(e, en, s_xq1w, 16 * t)
                    instr = e.dma_start(
                        out=xq1_sp.rearrange("k p t -> p k t")[:, :, t * 128:(t + 1) * 128],
                        in_=trst[:, :].rearrange("p (k c) -> p k c", k=16))
                    s_xq1w.dinc(instr)

            @on("tensor")
            def _(e, en):
                for t in range(NT):
                    wait(e, en, s_x2, 2 * t + 2)
                    if t >= 1:
                        wait(e, en, s_ev0, 2 * t)
                    for half, tgt in ((0, psA), (1, psB)):
                        for j in range(8):
                            k = half * 8 + j
                            instr = e.transpose(tgt[:, j * 128:(j + 1) * 128],
                                                xn_tm_b[:, k * 128:(k + 1) * 128], identb_sb[:, :])
                        s_tr0.inc(instr)
                    wait(e, en, s_ev0, 2 * t + 1)
                    for half, tgt in ((0, psA), (1, psB)):
                        for j in range(8):
                            k = half * 8 + j
                            instr = e.transpose(tgt[:, j * 128:(j + 1) * 128],
                                                xq_tm_b[:, k * 128:(k + 1) * 128], identb_sb[:, :])
                        s_tr0.inc(instr)

            barrier("p0", extra=[("scalar", s_xq1w, 16 * NT)])
            sc0i.close()

            phase_mark(2)
            with contextlib.ExitStack() as sc1:
                sc1m = contextlib.ExitStack()
                E1 = sc1m.enter_context
                wb = E1(nc.sbuf_tensor("wb", [128, KD * 128], BF16))
                convbuf = E1(nc.sbuf_tensor("convbuf", [128, 2051], F32))
                convout = E1(nc.sbuf_tensor("convout", [128, DM], F32))
                xfm_t = [E1(nc.sbuf_tensor(f"xfm_t{i}", [128, DM], BF16)) for i in range(2)]
                convw_sb = E1(nc.sbuf_tensor("convw_sb", [128, NOT_XBC * 4], F32))
                convb_sb = E1(nc.sbuf_tensor("convb_sb", [128, NOT_XBC], F32))
                dtraw = E1(nc.sbuf_tensor("dtraw", [64, L], F32))
                scrA = sq_scr
                zst = E1(nc.sbuf_tensor("zst", [128, DM], BF16))

                s_wld = sem("p1_wld")
                s_mm1 = sem("p1_mm")
                s_ev1 = sem("p1_ev")
                s_zw = sem("p1_zw")
                s_cv = sem("p1_cv")
                s_cvs = sem("p1_cvs")
                s_xw = ppsem("p1_xw")
                s_xtw = sem("p1_xtw")
                s_tr1 = sem("p1_tr")
                s_te1 = sem("p1_te")
                s_ps1 = ppsem("p1_ps")
                s_pa1 = sem("p1_pa")
                s_dt = sem("p1_dt")
                s_dtt = sem("p1_dtt")
                s_att = sem("p1_att")
                s_ate = sem("p1_ate")
                s_sw = sem("p1_sw")
                NSLAB = 128

                @on("sync")
                def _(e, en):
                    if "nocp" not in _SKIP:
                        sem("p1_cb").dinc(e.dma_start(out=convw_sb[:, :], in_=convw))
                        sem("p1_cb").dinc(e.dma_start(out=convb_sb[:, :], in_=convb))
                    for i in range(NSLAB if "prescan" not in _SKIP else 0):
                        if i >= 2:
                            wait(e, en, s_pa1, i - 1)
                            wait(e, en, s_ps1[i % 2], 16 * (i // 2))
                        if i < 64:
                            src = w1T[i]
                        else:
                            j = i - 64
                            src = w2T[j // 4][:, (j % 4) * 16:(j % 4 + 1) * 16]
                        instr = e.dma_start(out=wf[i % 2][:, :].rearrange("p (k c) -> p k c", k=KD),
                                            in_=src)
                        ppissue(e, en, s_ps1, i, instr)

                if "nogpcast" not in _SKIP:
                    @on("gpsimd")
                    def _(e, en):
                        for ot in range(NOT_IN if "noload" not in _SKIP else 0):
                            if ot >= 1 and "nomm" not in _SKIP:
                                wait(e, en, s_mm1, 4 * ot)
                            instr = e.dma_start(out=wb[:, :].rearrange("p (k c) -> p k c", k=KD),
                                                in_=winT[ot])
                            s_wld.dinc(instr)
                else:
                    @on("gpsimd")
                    def _(e, en):
                        for ot in range(NOT_IN if "noload" not in _SKIP else 0):
                            if ot >= 1 and "nomm" not in _SKIP:
                                wait(e, en, sem("p1_wcst"), ot)
                            instr = e.dma_start(out=sq_scr[:, :].rearrange("p (k c) -> p k c", k=KD),
                                                in_=winT[ot])
                            s_wld.dinc(instr)

                    @on("vector")
                    def _(e, en):
                        for ot in range(NOT_IN if "nomm" not in _SKIP else 0):
                            wait(e, en, s_wld, 16 * (ot + 1))
                            if ot >= 1:
                                wait(e, en, s_mm1, 4 * ot)
                            instr = e.tensor_copy(wb[:, :], sq_scr[:, :])
                            sem("p1_wcst").inc(instr)

                @on("vector")
                def _(e, en):
                    for i in range(NSLAB if "prescan" not in _SKIP else 0):
                        ppwait(e, en, s_ps1, i)
                        col = 0 if i < 64 else 1
                        wait(e, en, s_pa1, i)
                        schain(e, en, e.tensor_reduce(acol[:, 2:3], wf[i % 2][:, :], AX.X, ALU.add,
                                        apply_absolute_value=True))
                        instr = e.tensor_tensor(absacc[:, col:col + 1], absacc[:, col:col + 1],
                                                acol[:, 2:3], ALU.add)
                        s_pa1.inc(instr)

                def pe_xpose(e, en, o):
                    wait(e, en, s_cvs, o + 1)
                    if o >= 1:
                        wait(e, en, s_te1, o)
                    for half, tgt in ((0, psA), (1, psB)):
                        for j in range(8):
                            k = half * 8 + j
                            instr = e.transpose(tgt[:, j * 128:(j + 1) * 128],
                                                xfm_t[o % 2][:, k * 128:(k + 1) * 128],
                                                identb_sb[:, :])
                        s_tr1.inc(instr)

                @on("tensor")
                def _(e, en):
                    for ot in range(NOT_IN if "nomm" not in _SKIP else 0):
                        wait(e, en, s_wld, 16 * (ot + 1))
                        for tg in range(4):
                            if ot >= 1:
                                wait(e, en, s_ev1, 4 * (ot - 1) + tg + 1)
                            for kt in range(KD):
                                instr = e.matmul(ps[tg][:, :],
                                                 wb[:, kt * 128:(kt + 1) * 128],
                                                 bass.AP(xn_fm, kt * L + tg * 512,
                                                         [[KD * L, 128], [1, 512]]),
                                                 start=(kt == 0), stop=(kt == KD - 1))
                            s_mm1.inc(instr)
                        if "conv" not in _SKIP and "nomm" not in _SKIP and ot >= 34:
                            pe_xpose(e, en, ot - 34)

                @on("scalar")
                def _(e, en):
                    for ot in range(NOT_Z if "nomm" not in _SKIP else 0):
                        for tg in range(4):
                            wait(e, en, s_mm1, 4 * ot + tg + 1)
                            if ot >= 1 and "zspill" not in _SKIP:
                                wait(e, en, s_zw, 16 * ot)
                            instr = e.activation(zst[:, tg * 512:(tg + 1) * 512], ps[tg][:, :], AF.Silu)
                            s_ev1.inc(instr)
                        if "zspill" not in _SKIP:
                            wait(e, en, s_ev1, 4 * (ot + 1))
                            instr = e.dma_start(out=z_sp[ot], in_=zst[:, :])
                            s_zw.dinc(instr)
                    for o in range(NOT_XBC if "conv" not in _SKIP else 0):
                        wait(e, en, s_cv, o + 1)
                        if o >= 2:
                            wait(e, en, s_te1, o - 1)
                            ppwait(e, en, s_xw, o - 2)
                        instr = e.activation(xfm_t[o % 2][:, :], convout[:, :], AF.Silu,
                                             bias=convb_sb[:, o:o + 1])
                        s_cvs.inc(instr)
                        if o < 32:
                            wait(e, en, s_cvs, o + 1)
                            if o >= 2:
                                wait(e, en, s_xw[o % 2], 16 * (o // 2))
                            instr = e.dma_start(out=xfm_sp[o], in_=xfm_t[o % 2][:, :])
                            ppissue(e, en, s_xw, o, instr)
                            wait(e, en, s_te1, o + 1)
                            wait(e, en, s_xtw, 16 * o)
                            instr = e.dma_start(out=xtm_sp[o], in_=trst[:, :])
                            s_xtw.dinc(instr)

                def v_tb_evac(e, en, o):
                    if o < 32:
                        wait(e, en, s_tr1, 2 * o + 2)
                        if o >= 1:
                            wait(e, en, s_xtw, 16 * o)
                        e.tensor_copy(trst[:, 0:1024], psA[:, :])
                        instr = e.tensor_copy(trst[:, 1024:2048], psB[:, :])
                        s_te1.inc(instr)
                    elif o == 32:
                        e.tensor_copy(B_fm[:, :], xfm_t[o % 2][:, :])
                        wait(e, en, s_tr1, 2 * 32 + 2)
                        e.tensor_copy(B_tm[:, 0:1024], psA[:, :])
                        instr = e.tensor_copy(B_tm[:, 1024:2048], psB[:, :])
                        s_te1.inc(instr)
                    else:
                        wait(e, en, s_cvs, 34)
                        instr = e.tensor_copy(C_fm[:, :], xfm_t[o % 2][:, :])
                        s_te1.inc(instr)

                @on("vector")
                def _(e, en):
                    schain(e, en, e.memset(convbuf[:, 0:3], 0.0))
                    for ot in range(NOT_Z, NOT_IN if "nomm" not in _SKIP else NOT_Z):
                        o = ot - NOT_Z
                        if ot == NOT_Z:
                            wait(e, en, s_ev1, 4 * NOT_Z)
                        if o >= 1:
                            wait(e, en, s_cv, o)
                        for tg in range(4):
                            wait(e, en, s_mm1, 4 * ot + tg + 1)
                            if ot < NOT_Z + NOT_XBC:
                                instr = e.tensor_copy(convbuf[:, 3 + tg * 512:3 + (tg + 1) * 512],
                                                      ps[tg][:, :])
                            else:
                                instr = e.tensor_copy(dtraw[:, tg * 512:(tg + 1) * 512], ps[tg][0:64, :])
                            s_ev1.inc(instr)
                        if ot < NOT_Z + NOT_XBC and "conv" not in _SKIP:
                            wait(e, en, sem("p1_cb"), 32)
                            wait(e, en, s_ev1, 4 * (ot + 1))
                            if o >= 1:
                                wait(e, en, s_cvs, o)
                            schain(e, en, e.tensor_scalar(convout[:, :], convbuf[:, 0:DM],
                                            convw_sb[:, o * 4:o * 4 + 1], None, ALU.mult))
                            for k in range(1, 4):
                                instr = e.scalar_tensor_tensor(convout[:, :], convbuf[:, k:k + DM],
                                                               convw_sb[:, o * 4 + k:o * 4 + k + 1],
                                                               convout[:, :], ALU.mult, ALU.add)
                                if k < 3:
                                    schain(e, en, instr)
                            s_cv.inc(instr)
                        if "conv" not in _SKIP and o >= 1 and (o - 1) <= 33:
                            v_tb_evac(e, en, o - 1)

                @on("scalar")
                def _(e, en):
                    if "dt" in _SKIP:
                        return
                    wait(e, en, s_ev1, 4 * NOT_IN)
                    schain(e, en, e.activation(scrA[0:64, :], dtraw[:, :], AF.Abs, bias=pcol[:, 1:2]))
                    schain(e, en, e.activation(scrA[0:64, :], scrA[0:64, :], AF.Exp, scale=-1.0))
                    schain(e, en, e.activation(scrA[0:64, :], scrA[0:64, :], AF.Ln, bias=1.0))
                    instr = e.activation(dt_fm[:, :], dtraw[:, :], AF.Relu, bias=pcol[:, 1:2])
                    s_dt.inc(instr)

                @on("vector")
                def _(e, en):
                    if "dt" in _SKIP:
                        return
                    wait(e, en, s_dt, 1)
                    schain(e, en, e.tensor_tensor(dt_fm[:, :], dt_fm[:, :], scrA[0:64, :], ALU.add))
                    wait(e, en, s_prep, 3)
                    schain(e, en, e.tensor_scalar(scrA[0:64, :], dt_fm[:, :], pcol[:, 2:3], None, ALU.mult))
                    for c in range(NCH):
                        instr = e.tensor_tensor_scan(acum_fm[:, c * Q:(c + 1) * Q], scrA[0:64, c * Q:(c + 1) * Q],
                                             scrA[0:64, c * Q:(c + 1) * Q], 0.0, ALU.add, ALU.bypass)
                    schain(e, en, instr)
                    schain(e, en, e.tensor_copy(acum_hi[:, :], acum_fm[:, :]))
                    schain(e, en, e.tensor_copy(scrA[0:64, :], acum_hi[:, :]))
                    instr = e.tensor_tensor(acum_lo[:, :], acum_fm[:, :], scrA[0:64, :], ALU.subtract)
                    s_dt.inc(instr)

                @on("tensor")
                def _(e, en):
                    if "dt" in _SKIP:
                        return
                    wait(e, en, s_dt, 2)
                    for i2 in range(2 * NT):
                        lt = i2 % NT
                        src = dt_fm if i2 < NT else acum_fm
                        bank = ps[4] if (i2 % 2 == 0) else ps[5]
                        if i2 >= 2:
                            wait(e, en, s_ate, i2 - 1)
                        instr = e.transpose(bank[:, 0:64], src[:, lt * 128:(lt + 1) * 128],
                                            identf_sb[0:64, 0:64])
                        s_att.inc(instr)

                @on("vector")
                def _(e, en):
                    if "dt" in _SKIP:
                        return
                    for i2 in range(2 * NT):
                        lt = i2 % NT
                        wait(e, en, s_att, i2 + 1)
                        bank = ps[4] if (i2 % 2 == 0) else ps[5]
                        if i2 < NT:
                            instr = e.tensor_copy(dt_tm[:, lt * 64:(lt + 1) * 64], bank[:, 0:64])
                        else:
                            e.tensor_copy(acum_tm[:, lt * 64:(lt + 1) * 64], bank[:, 0:64])
                            instr = e.tensor_scalar(nacum_tm[:, lt * 64:(lt + 1) * 64], bank[:, 0:64],
                                                    -1.0, None, ALU.mult)
                        s_ate.inc(instr)

                @on("tensor")
                def _(e, en):
                    if "prescan" in _SKIP or "dt" in _SKIP:
                        return
                    wait(e, en, s_pa1, NSLAB)
                    wait(e, en, s_ate, 2 * NT)
                    instr = e.transpose(ps[4][0:2, 64:192], absacc[:, 0:2], identf_sb[:, :])
                    s_sw.inc(instr)

                @on("vector")
                def _(e, en):
                    if "prescan" in _SKIP or "dt" in _SKIP:
                        return
                    wait(e, en, s_sw, 1)
                    schain(e, en, e.tensor_reduce(acol[0:2, 3:4], ps[4][0:2, 64:192], AX.X, ALU.add))
                    schain(e, en, e.tensor_scalar(acol[0:2, 3:4], acol[0:2, 3:4], 1.0 / (FF * DM), 1e-5,
                                    ALU.mult, ALU.max))
                    instr = e.reciprocal(acol[0:2, 3:4], acol[0:2, 3:4])
                    s_sw.inc(instr)

                @on("tensor")
                def _(e, en):
                    if "prescan" in _SKIP or "dt" in _SKIP:
                        return
                    wait(e, en, s_sw, 2)
                    e.matmul(ps[4][:, 66:67], sel2[:, 0:128], acol[0:2, 3:4], start=True, stop=False,
                             skip_group_check=True)
                    instr = e.matmul(ps[4][:, 67:68], sel2[:, 128:256], acol[0:2, 3:4],
                                     start=False, stop=True, skip_group_check=True)
                    s_sw.inc(instr)

                @on("vector")
                def _(e, en):
                    if "prescan" in _SKIP or "dt" in _SKIP:
                        return
                    wait(e, en, s_sw, 3)
                    schain(e, en, e.tensor_copy(swc[:, 0:1], ps[4][:, 66:67]))
                    schain(e, en, e.tensor_copy(swc[:, 1:2], ps[4][:, 67:68]))
                    schain(e, en, e.tensor_scalar(swc[:, 2:3], swc[:, 0:1], 127.0, None, ALU.mult))
                    schain(e, en, e.tensor_scalar(swc[:, 3:4], swc[:, 1:2], 127.0, None, ALU.mult))
                    schain(e, en, e.reciprocal(swc[:, 2:3], swc[:, 2:3]))
                    instr = e.reciprocal(swc[:, 3:4], swc[:, 3:4])
                    s_sw.inc(instr)

                barrier("p1", extra=(([("scalar", s_xw[0], 16 * 16), ("scalar", s_xw[1], 16 * 16),
                                      ("scalar", s_xtw, 16 * 32)]) if "conv" not in _SKIP else []) +
                        ([("scalar", s_zw, 16 * NOT_Z)] if "zspill" not in _SKIP else []))
                sc1m.close()
                scx.close()

                phase_mark(3)
                with contextlib.ExitStack() as sc2:
                    E2 = sc2.enter_context
                    xtm_ch = E2(nc.sbuf_tensor("xtm_ch", [128, 8192], BF16))
                    xfm_ch = E2(nc.sbuf_tensor("xfm_ch", [128, 8192], BF16))
                    zs_ch = E2(nc.sbuf_tensor("zs_ch", [128, 8192], BF16))
                    yg_sb = E2(nc.sbuf_tensor("yg_sb", [128, 8192], BF16))
                    y_sb = E2(nc.sbuf_tensor("y_sb", [128, 8192], BF16))
                    _achr_hi = E2(nc.sbuf_tensor("achr_hi", [1, 2048], BF16))
                    _achr_lo = E2(nc.sbuf_tensor("achr_lo", [1, 2048], BF16))
                    achr_hi = [_achr_hi, _achr_hi]
                    achr_lo = [_achr_lo, _achr_lo]
                    cbt = E2(nc.sbuf_tensor("cbt", [128, 512], F32))
                    scombs = [E2(nc.sbuf_tensor(f"scomb{i}", [128, 512], F32)) for i in range(2)]
                    esdo = E2(nc.sbuf_tensor("esdo", [128, 256], BF16))
                    arep_sb = E2(nc.sbuf_tensor("arep_sb", [128, 256], F32))
                    cbm = E2(nc.sbuf_tensor("cbm", [128, 512], BF16))
                    cbmt = E2(nc.sbuf_tensor("cbmt", [128, 512], BF16))
                    csdo = E2(nc.sbuf_tensor("csdo", [128, 256], BF16))
                    bdd = E2(nc.sbuf_tensor("bdd", [128, 256], BF16))
                    st_f = E2(nc.sbuf_tensor("st_f", [128, H * 64], F32))
                    st_hb = E2(nc.sbuf_tensor("st_hb", [128, 64], BF16))
                    etrep = E2(nc.sbuf_tensor("etrep", [128, 64], F32))
                    rrow = E2(nc.sbuf_tensor("rrow", [1, Q], F32))
                    rrow_b = E2(nc.sbuf_tensor("rrow_b", [1, Q], BF16))
                    rstd_rep = E2(nc.sbuf_tensor("rstd_rep", [128, Q], F32))
                    ygt = E2(nc.sbuf_tensor("ygt", [128, 256], F32))
                    et_hi = E2(nc.sbuf_tensor("et_hi", [1, 64], BF16))
                    et_lo = E2(nc.sbuf_tensor("et_lo", [1, 64], BF16))
                    lndt_ch = E2(nc.sbuf_tensor("lndt_ch", [128, 128], F32))

                    s_ch = sem("p2_ch")
                    s_row2 = sem("p2_row")
                    s_cb = sem("p2_cb")
                    s_cbe = sem("p2_cbe")
                    s_et = sem("p2_et")
                    s_dd = sem("p2_dd")
                    s_ddt = sem("p2_ddt")
                    s_dde = sem("p2_dde")
                    s_rep = sem("p2_rep")
                    s_cmb = sem("p2_cmb")
                    s_exp = sem("p2_exp")
                    s_cbml = sem("p2_cbm")
                    s_bddp = sem("p2_bdd")
                    s_csd = sem("p2_csd")
                    s_ymm = sem("p2_ymm")
                    s_smm = sem("p2_smm")
                    s_yev = sem("p2_yev")
                    s_stb = sem("p2_stb")
                    s_rec = sem("p2_rec")
                    s_y2 = sem("p2_y2")
                    s_rs = sem("p2_rs")
                    s_rr = sem("p2_rr")
                    s_gt = sem("p2_gt")
                    s_ygw = sem("p2_ygw")

                    @on("vector")
                    def _(e, en):
                        e.memset(st_f[:, :], 0.0)

                    def achr_load(e, en, c, g):
                        if c * 8 + g >= 1:
                            wait(e, en, s_exp, 16 * (c * 8 + g))
                        s_row2.dinc(e.dma_start(
                            out=achr_hi[g % 2][0:1, :].rearrange("o (h q) -> o h q", h=8),
                            in_=acum_hi[g * 8:(g + 1) * 8, c * Q:(c + 1) * Q]))
                        s_row2.dinc(e.dma_start(
                            out=achr_lo[g % 2][0:1, :].rearrange("o (h q) -> o h q", h=8),
                            in_=acum_lo[g * 8:(g + 1) * 8, c * Q:(c + 1) * Q]))

                    def gp_chunk(e, en, c):
                        if c >= 1:
                            wait(e, en, s_ymm, c * H)
                            wait(e, en, s_yev, c * H)
                            wait(e, en, s_ygw, 16 * c)
                            wait(e, en, s_gt, c)
                        for ot in range(32):
                            s_ch.dinc(e.dma_start(out=bass.AP(xtm_ch, ot * 128,
                                                  [[8192, 128], [4096, 2], [1, 128]]),
                                      in_=bass.AP(xtm_sp.tensor, ot * 128 * 2048 + 2 * c * 128,
                                                  [[2048, 128], [128, 2], [1, 128]])))
                            s_ch.dinc(e.dma_start(out=xfm_ch[:, ot * 256:(ot + 1) * 256],
                                      in_=xfm_sp[ot][:, c * Q:(c + 1) * Q]))
                            s_ch.dinc(e.dma_start(out=zs_ch[:, ot * 256:(ot + 1) * 256],
                                      in_=z_sp[ot][:, c * Q:(c + 1) * Q]))
                        for g in range(8):
                            achr_load(e, en, c, g)
                        wait(e, en, s_yev, (c + 1) * H)
                        instr = e.tensor_tensor(yg_sb[:, :], y_sb[:, :], y_sb[:, :], ALU.mult)
                        s_y2.inc(instr)

                    def pe_chunk(e, en, c):
                        if c >= 1:
                            wait(e, en, s_cbe, c)
                        for sblk in range(2):
                            instr = e.matmul(ps[0][:, sblk * 256:(sblk + 1) * 256],
                                             B_fm[:, c * Q + sblk * 128:c * Q + (sblk + 1) * 128],
                                             C_fm[:, c * Q:(c + 1) * Q], start=True, stop=True,
                                             skip_group_check=True)
                        s_cb.inc(instr)
                        wait(e, en, s_et, c)
                        instr = e.transpose(ps[1][0:1, 64:128],
                                            acum_fm[:, c * Q + Q - 1:c * Q + Q],
                                            identf_sb[0:64, 0:64])
                        s_cb.inc(instr)
                        wait(e, en, sem("p2_ehl"), c + 1)
                        e.matmul(ps[1][:, 0:64], ones_r_b[0:1, :], et_hi[0:1, :],
                                 start=True, stop=False, skip_group_check=True)
                        instr = e.matmul(ps[1][:, 0:64], ones_r_b[0:1, :], et_lo[0:1, :],
                                         start=False, stop=True, skip_group_check=True)
                        s_cb.inc(instr)
                        wait(e, en, s_dd, 2 * c + 2)
                        if c >= 1:
                            wait(e, en, s_dde, c)
                            wait(e, en, s_gt, c)
                        for j in range(2):
                            instr = e.transpose(ps[2][:, j * 64:(j + 1) * 64],
                                                dtdec_fm[:, j * 128:(j + 1) * 128],
                                                identf_sb[0:64, 0:64])
                        s_ddt.inc(instr)
                        def pe_rep(h):
                            hd = c * H + h
                            g = h // 8
                            hg = h % 8
                            rbank = ps[3] if h % 2 == 0 else ps[0]
                            if hd >= 2:
                                wait(e, en, s_cmb, hd - 1)
                            if h == 1:
                                wait(e, en, s_cbe, c + 1)
                            wait(e, en, s_row2, (c * 8 + g + 1) * 32)
                            e.matmul(rbank[:, 0:256], ones_r_b[0:1, :],
                                     achr_hi[g % 2][0:1, hg * Q:(hg + 1) * Q],
                                     start=True, stop=False, skip_group_check=True)
                            instr = e.matmul(rbank[:, 0:256], ones_r_b[0:1, :],
                                             achr_lo[g % 2][0:1, hg * Q:(hg + 1) * Q],
                                             start=False, stop=True, skip_group_check=True)
                            s_rep.inc(instr)

                        def pe_ydiag(h):
                            hd = c * H + h
                            wait(e, en, s_ch, (c + 1) * 16 * 96)
                            wait(e, en, s_cbml, hd + 1)
                            wait(e, en, s_csd, hd + 1)
                            wait(e, en, s_stb, hd + 1)
                            wait(e, en, s_bddp, hd + 1)
                            if hd >= 1:
                                wait(e, en, s_yev, hd)
                            e.matmul(ps[5][0:64, 0:256], xtm_ch[:, h * 64:(h + 1) * 64],
                                     cbmt[:, 0:256], start=True, stop=False)
                            e.matmul(ps[5][0:64, 0:256],
                                     xtm_ch[:, 4096 + h * 64:4096 + (h + 1) * 64],
                                     cbmt[:, 256:512], start=False, stop=False)
                            instr = e.matmul(ps[5][0:64, 0:256], st_hb[:, :], csdo[:, :],
                                             start=False, stop=True)
                            s_ymm.inc(instr)
                            if hd >= 1:
                                wait(e, en, s_rec, hd)
                            e.matmul(ps[4][:, 0:64], bdd[:, 0:128],
                                     xtm_ch[:, h * 64:(h + 1) * 64], start=True, stop=False)
                            instr = e.matmul(ps[4][:, 0:64], bdd[:, 128:256],
                                             xtm_ch[:, 4096 + h * 64:4096 + (h + 1) * 64],
                                             start=False, stop=True)
                            s_smm.inc(instr)

                        for h in range(H):
                            pe_rep(h)
                            if h >= 1:
                                pe_ydiag(h - 1)
                        pe_ydiag(H - 1)
                        wait(e, en, s_y2, c + 1)
                        if c >= 1:
                            wait(e, en, s_rs, 2 * c)
                        for ob in range(32):
                            instr = e.matmul(ps[2][0:1, 0:256], ones_c_b[:, 0:1],
                                             yg_sb[:, ob * 256:(ob + 1) * 256],
                                             start=(ob == 0), stop=(ob == 31),
                                             skip_group_check=True)
                        s_rs.inc(instr)
                        wait(e, en, s_rr, 3 * c + 3)
                        instr = e.matmul(ps[2][:, 256:512], ones_r_b[0:1, :], rrow_b[0:1, :],
                                         start=True, stop=True, skip_group_check=True)
                        s_rs.inc(instr)

                    def v_chunk(e, en, c):
                        wait(e, en, s_cb, 3 * c + 1)
                        instr = e.tensor_copy(cbt[:, :], ps[0][:, :])
                        s_cbe.inc(instr)
                        wait(e, en, s_cb, 3 * c + 2)
                        schain(e, en, e.tensor_copy(et_hi[0:1, :], ps[1][0:1, 64:128]))
                        instr = e.tensor_tensor(et_lo[0:1, :], ps[1][0:1, 64:128],
                                                et_hi[0:1, :], ALU.subtract)
                        sem("p2_ehl").inc(instr)
                        wait(e, en, s_dd, 2 * c + 1)
                        instr = e.tensor_tensor(dtdec_fm[:, :], dtdec_fm[:, :],
                                                dt_fm[:, c * Q:(c + 1) * Q], ALU.mult)
                        s_dd.inc(instr)
                        wait(e, en, s_ddt, c + 1)
                        instr = e.tensor_copy(dtdec_tm[:, :], ps[2][:, 0:128])
                        s_dde.inc(instr)
                        def v_a(h):
                            hd = c * H + h
                            rbank = ps[3] if h % 2 == 0 else ps[0]
                            wait(e, en, s_rep, hd + 1)
                            if hd >= 2:
                                wait(e, en, s_exp, 2 * (hd - 1))
                            for j in range(2):
                                schain(e, en, e.scalar_tensor_tensor(
                                    scombs[h % 2][:, j * 256:(j + 1) * 256], rbank[:, 0:256],
                                    nacum_tm[:, (2 * c + j) * 64 + h:(2 * c + j) * 64 + h + 1],
                                    mask_sb[:, j * 256:(j + 1) * 256], ALU.add, ALU.add))
                            if hd >= 1:
                                wait(e, en, s_exp, 2 * hd)
                            instr = e.tensor_copy(arep_sb[:, :], rbank[:, 0:256])
                            s_cmb.inc(instr)

                        def v_b(h):
                            hd = c * H + h
                            wait(e, en, s_exp, 2 * hd + 1)
                            wait(e, en, s_cbe, c + 1)
                            if hd >= 1:
                                wait(e, en, s_ymm, hd)
                            instr = e.tensor_tensor(cbmt[:, :], cbm[:, :], cbt[:, :], ALU.mult)
                            s_cbml.inc(instr)
                            wait(e, en, s_exp, 2 * hd + 2)
                            instr = e.tensor_tensor(csdo[:, :], C_fm[:, c * Q:(c + 1) * Q],
                                                    esdo[:, :], ALU.mult)
                            s_csd.inc(instr)
                            instr = e.tensor_copy(st_hb[:, :], st_f[:, h * 64:(h + 1) * 64])
                            s_stb.inc(instr)
                            wait(e, en, s_dde, c + 1)
                            if hd >= 1:
                                wait(e, en, s_smm, hd)
                            for j in range(2):
                                instr = e.tensor_scalar(bdd[:, j * 128:(j + 1) * 128],
                                                        B_tm[:, (2 * c + j) * 128:(2 * c + j + 1) * 128],
                                                        dtdec_tm[:, j * 64 + h:j * 64 + h + 1],
                                                        None, ALU.mult)
                            s_bddp.inc(instr)

                        def v_c(h):
                            hd = c * H + h
                            wait(e, en, s_ymm, hd + 1)
                            ot = h // 2
                            half = h % 2
                            instr = e.scalar_tensor_tensor(
                                y_sb[half * 64:(half + 1) * 64, ot * 256:(ot + 1) * 256],
                                xfm_ch[half * 64:(half + 1) * 64, ot * 256:(ot + 1) * 256],
                                d_rep[half * 64:(half + 1) * 64, h:h + 1],
                                ps[5][0:64, 0:256], ALU.mult, ALU.add)
                            s_yev.inc(instr)
                            wait(e, en, s_smm, hd + 1)
                            wait(e, en, s_et, c + 1)
                            instr = e.scalar_tensor_tensor(st_f[:, h * 64:(h + 1) * 64],
                                                           st_f[:, h * 64:(h + 1) * 64],
                                                           etrep[:, h:h + 1], ps[4][:, 0:64],
                                                           ALU.mult, ALU.add)
                            s_rec.inc(instr)

                        for h in range(H):
                            v_a(h)
                            if h >= 1:
                                v_b(h - 1)
                            if h >= 2:
                                v_c(h - 2)
                        v_b(H - 1)
                        v_c(H - 2)
                        v_c(H - 1)
                        wait(e, en, s_rs, 2 * c + 1)
                        instr = e.tensor_scalar(rrow[0:1, :], ps[2][0:1, 0:256], 1.0 / DIN,
                                                RMS_EPS, ALU.mult, ALU.add)
                        s_rr.inc(instr)
                        wait(e, en, s_rr, 3 * c + 2)
                        schain(e, en, e.reciprocal(rrow[0:1, :], rrow[0:1, :]))
                        instr = e.tensor_copy(rrow_b[0:1, :], rrow[0:1, :])
                        s_rr.inc(instr)
                        wait(e, en, s_rs, 2 * c + 2)
                        schain(e, en, e.tensor_copy(rstd_rep[:, :], ps[2][:, 256:512]))
                        for ob in range(32):
                            schain(e, en, e.scalar_tensor_tensor(ygt[:, :], y_sb[:, ob * 256:(ob + 1) * 256],
                                                   normw_sb[:, ob:ob + 1], rstd_rep[:, :],
                                                   ALU.mult, ALU.mult))
                            instr = e.tensor_tensor(yg_sb[:, ob * 256:(ob + 1) * 256], ygt[:, :],
                                                    zs_ch[:, ob * 256:(ob + 1) * 256], ALU.mult)
                            if ob < 31:
                                schain(e, en, instr)
                        s_gt.inc(instr)

                    def act_chunk(e, en, c):
                        if c >= 1:
                            wait(e, en, s_exp, 2 * c * H)
                        instr = e.activation(lndt_ch[:, :], dt_tm[:, 2 * c * 64:(2 * c + 2) * 64], AF.Ln)
                        sem("p2_ldt").inc(instr)
                        wait(e, en, s_cb, 3 * c + 3)
                        instr = e.activation(etrep[:, :], ps[1][:, 0:64], AF.Exp,
                                             bias=acol[:, 0:1])
                        s_et.inc(instr)
                        instr = e.activation(dtdec_fm[:, :], acum_fm[:, c * Q:(c + 1) * Q], AF.Exp,
                                             scale=-1.0, bias=acum_fm[:, c * Q + Q - 1:c * Q + Q])
                        s_dd.inc(instr)
                        for h in range(H):
                            hd = c * H + h
                            wait(e, en, s_cmb, hd + 1)
                            wait(e, en, sem("p2_ldt"), c + 1)
                            if hd >= 1:
                                wait(e, en, s_cbml, hd)
                            for j in range(2):
                                instr = e.activation(cbm[:, j * 256:(j + 1) * 256],
                                                     scombs[h % 2][:, j * 256:(j + 1) * 256], AF.Exp,
                                                     bias=lndt_ch[:, j * 64 + h:j * 64 + h + 1])
                                if j == 0:
                                    schain(e, en, instr)
                            s_exp.inc(instr)
                            if hd >= 1:
                                wait(e, en, s_csd, hd)
                            instr = e.activation(esdo[:, :], arep_sb[:, :], AF.Exp,
                                                 bias=acol[:, 0:1])
                            s_exp.inc(instr)
                        wait(e, en, s_rr, 3 * c + 1)
                        instr = e.activation(rrow[0:1, :], rrow[0:1, :], AF.Sqrt)
                        s_rr.inc(instr)
                        wait(e, en, s_gt, c + 1)
                        instr = e.dma_start(out=yg_sp[c], in_=yg_sb[:, :])
                        s_ygw.dinc(instr)

                    @on("gpsimd")
                    def _(e, en):
                        for c in range(NCH):
                            gp_chunk(e, en, c)

                    @on("tensor")
                    def _(e, en):
                        for c in range(NCH):
                            pe_chunk(e, en, c)

                    @on("vector")
                    def _(e, en):
                        for c in range(NCH):
                            v_chunk(e, en, c)

                    @on("scalar")
                    def _(e, en):
                        for c in range(NCH):
                            act_chunk(e, en, c)

                    barrier("p2", extra=[("scalar", s_ygw, 16 * NCH)])
                scspan.close()

        phase_mark(4)
        with contextlib.ExitStack() as scb:
            Eb = scb.enter_context
            wout_h = Eb(nc.sbuf_tensor("wout_h", [128, 32 * 1024], BF16))
            yg_in = [Eb(nc.sbuf_tensor(f"yg_in{i}", [128, 8192], BF16)) for i in range(2)]
            att_st = Eb(nc.sbuf_tensor("att_st", [128, 1024], BF16))
            wq_i = Eb(nc.sbuf_tensor("wq_ib", [128, 2048], I32))
            u2_tl = [Eb(nc.sbuf_tensor(f"u2_tl{i}", [128, 1024], F32)) for i in range(2)]

            s_wo = sem("pb_wo")
            s_ygl = ppsem("pb_ygl")
            s_u2 = ppsem("pb_u2")
            s_mm2 = sem("pb_mm")
            s_ev2 = sem("pb_ev")
            s_aw = sem("pb_aw")
            s_w2l = ppsem("pb_w2l")
            s_w2q = sem("pb_w2q")
            s_w2w = ppsem("pb_w2w")

            @on("gpsimd")
            def _(e, en):
                for half in range(2):
                    if half == 1:
                        wait(e, en, s_mm2, 32)
                    instr = e.dma_start(
                        out=wout_h[:, :].rearrange("p (k c2) -> p k c2", k=32),
                        in_=woutT.rearrange("(k p) o -> p k o", p=128)[:, :, half * 1024:(half + 1) * 1024])
                    s_wo.dinc(instr)
                    for c in range(NCH):
                        hc = half * 8 + c
                        if hc >= 2:
                            wait(e, en, s_mm2, 4 * (hc - 1))
                            wait(e, en, s_ygl[hc % 2], 16 * (hc // 2))
                        instr = e.dma_start(out=yg_in[c % 2][:, :], in_=yg_sp[c])
                        ppissue(e, en, s_ygl, hc, instr)

            def aacc_spill(e, en, ht):
                wait(e, en, s_ev2, 2 * (ht + 1))
                instr = e.dma_start(out=aacc_sp[ht % NT][:, (ht // NT) * 1024:(ht // NT + 1) * 1024],
                                    in_=att_st[:, :])
                s_aw.dinc(instr)

            @on("scalar")
            def _(e, en):
                for half in range(2):
                    for t in range(NT):
                        ht = half * NT + t
                        if ht >= 2:
                            wait(e, en, s_ev2, 2 * (ht - 1))
                            wait(e, en, s_u2[ht % 2], 16 * (ht // 2))
                        instr = e.dma_start(out=u2_tl[t % 2][:, :],
                                            in_=u[t * 128:(t + 1) * 128, half * 1024:(half + 1) * 1024])
                        ppissue(e, en, s_u2, ht, instr)
                        if ht >= 1:
                            aacc_spill(e, en, ht - 1)
                aacc_spill(e, en, 2 * NT - 1)

            @on("tensor")
            def _(e, en):
                for half in range(2):
                    wait(e, en, s_wo, 16 * (half + 1))
                    for c in range(NCH):
                        hc = half * 8 + c
                        ppwait(e, en, s_ygl, hc)
                        for tt in range(2):
                            tgl = half * 16 + c * 2 + tt
                            for og in range(2):
                                if tgl >= 1:
                                    wait(e, en, s_ev2, 2 * (tgl - 1) + og + 1)
                                for kt in range(32):
                                    instr = e.matmul(
                                        ps[og][:, :],
                                        yg_in[c % 2][:, kt * 256 + tt * 128:kt * 256 + tt * 128 + 128],
                                        bass.AP(wout_h, kt * 1024 + og * 512, [[32 * 1024, 128], [1, 512]]),
                                        start=(kt == 0), stop=(kt == 31))
                                s_mm2.inc(instr)

            @on("vector")
            def _(e, en):
                for half in range(2):
                    for t in range(NT):
                        ht = half * NT + t
                        ppwait(e, en, s_u2, ht)
                        if ht >= 1:
                            wait(e, en, s_aw, 16 * ht)
                        for og in range(2):
                            wait(e, en, s_mm2, 2 * ht + og + 1)
                            instr = e.tensor_tensor(att_st[:, og * 512:(og + 1) * 512], ps[og][:, :],
                                                    u2_tl[t % 2][:, og * 512:(og + 1) * 512], ALU.add)
                            s_ev2.inc(instr)

            @on("sync")
            def _(e, en):
                for pidx in range(64):
                    if pidx >= 2:
                        wait(e, en, s_w2q, pidx - 1)
                        wait(e, en, s_w2l[pidx % 2], 16 * (pidx // 2))
                    instr = e.dma_start(out=wf[pidx % 2][:, :].rearrange("p (k c2) -> p k c2", k=KD),
                                        in_=w2T[pidx // 4][:, (pidx % 4) * 16:(pidx % 4 + 1) * 16])
                    ppissue(e, en, s_w2l, pidx, instr)

            @on("vector")
            def _(e, en):
                for pidx in range(64):
                    ppwait(e, en, s_w2l, pidx)
                    if pidx >= 2:
                        ppwait(e, en, s_w2w, pidx - 2)
                    schain(e, en, e.tensor_scalar(sq_scr[:, :], wf[pidx % 2][:, :], swc[:, 1:2], None, ALU.mult))
                    schain(e, en, e.tensor_scalar(sq_scr[:, :], sq_scr[:, :], MAGICR, MAGICR, ALU.add, ALU.subtract))
                    schain(e, en, e.tensor_copy(wq_i[:, :], sq_scr[:, :]))
                    instr = e.tensor_scalar(wq_b[pidx % 2][:, :], wq_i[:, :], 1.0, -1.0,
                                            ALU.min, ALU.max)
                    s_w2q.inc(instr)

            @on("gpsimd")
            def _(e, en):
                for pidx in range(64):
                    wait(e, en, s_w2q, pidx + 1)
                    if pidx >= 2:
                        wait(e, en, s_w2w[pidx % 2], 16 * (pidx // 2))
                    instr = e.dma_start(out=w2q_sp[pidx // 4][:, (pidx % 4) * 2048:(pidx % 4 + 1) * 2048],
                                        in_=wq_b[pidx % 2][:, :])
                    ppissue(e, en, s_w2w, pidx, instr)

            barrier("p2b", extra=[("scalar", s_aw, 16 * 2 * NT), ("gpsimd", s_w2w[0], 16 * 32),
                                  ("gpsimd", s_w2w[1], 16 * 32)])

        phase_mark(5)
        s_smt = sem("p3_smt")

        with contextlib.ExitStack() as sc3:
            E3 = sc3.enter_context
            xq1_fm = E3(nc.sbuf_tensor("xq1_fm", [128, KD * L], BF16))
            h_t = E3(nc.sbuf_tensor("h_t", [128, 2048], F32))
            wq_i = E3(nc.sbuf_tensor("wq_i3", [128, 2048], I32))
            h_b = [E3(nc.sbuf_tensor(f"h_b{i}", [128, 2048], BF16)) for i in range(2)]
            hsq = E3(nc.sbuf_tensor("hsq", [128, 2048], BF16))
            inv1_rep = E3(nc.sbuf_tensor("inv1_rep", [128, 2048], F32))
            rhf = E3(nc.sbuf_tensor("rhf", [1, 2048], F32))
            rha = E3(nc.sbuf_tensor("rha", [1, 2048], F32))
            rhb = E3(nc.sbuf_tensor("rhb", [1, 2048], BF16))
            rlb = E3(nc.sbuf_tensor("rlb", [1, 2048], BF16))

            s_xq1l = sem("p3_xq1l")
            s_w1l = ppsem("p3_w1l")
            s_w1q = sem("p3_w1q")
            s_iv = sem("p3_iv")
            s_ivr = sem("p3_ivr")
            s_mm3 = sem("p3_mm")
            s_hv = sem("p3_hv")
            s_hs = sem("p3_hs")
            s_hw = ppsem("p3_hw")
            s_hsq = sem("p3_hsq")

            @on("gpsimd")
            def _(e, en):
                instr = e.dma_start(out=xq1_fm[:, :].rearrange("p (k t) -> p k t", k=KD),
                                    in_=xq1_sp.rearrange("k p t -> p k t"))
                s_xq1l.dinc(instr)

            @on("sync")
            def _(e, en):
                for ot in range(NOT_F):
                    if ot >= 2:
                        wait(e, en, s_w1q, ot - 1)
                        wait(e, en, s_w1l[ot % 2], 16 * (ot // 2))
                    instr = e.dma_start(out=wf[ot % 2][:, :].rearrange("p (k c2) -> p k c2", k=KD),
                                        in_=w1T[ot])
                    ppissue(e, en, s_w1l, ot, instr)

            @on("vector")
            def _(e, en):
                instr = e.tensor_scalar(ctmp[:, :], camax[:, :], swc[:, 2:3], None, ALU.mult)
                s_iv.inc(instr)

            @on("tensor")
            def _(e, en):
                wait(e, en, s_iv, 1)
                instr = e.transpose(ps[4][0:16, 0:128], ctmp[:, 0:16], identf_sb[:, :])
                s_iv.inc(instr)

            @on("vector")
            def _(e, en):
                wait(e, en, s_iv, 2)
                instr = e.tensor_copy(sq_scr[0:16, 0:128], ps[4][0:16, 0:128])
                s_iv.inc(instr)

            @on("gpsimd")
            def _(e, en):
                wait(e, en, s_iv, 3)
                instr = e.dma_start(out=rhf[0:1, :].rearrange("o (t p) -> o t p", t=16),
                                    in_=sq_scr[0:16, 0:128])
                sem("p3_ivd").inc(instr, 16)

            @on("vector")
            def _(e, en):
                wait(e, en, sem("p3_ivd"), 16)
                schain(e, en, e.tensor_copy(rhb[0:1, :], rhf[0:1, :]))
                schain(e, en, e.tensor_copy(rha[0:1, :], rhb[0:1, :]))
                schain(e, en, e.tensor_tensor(rha[0:1, :], rhf[0:1, :], rha[0:1, :], ALU.subtract))
                instr = e.tensor_copy(rlb[0:1, :], rha[0:1, :])
                s_iv.inc(instr)

            @on("tensor")
            def _(e, en):
                wait(e, en, s_iv, 4)
                for tg in range(4):
                    if tg >= 1:
                        wait(e, en, s_ivr, tg)
                    e.matmul(ps[4][:, 0:512], ones_r_b[0:1, :], rhb[0:1, tg * 512:(tg + 1) * 512],
                             start=True, stop=False, skip_group_check=True)
                    instr = e.matmul(ps[4][:, 0:512], ones_r_b[0:1, :], rlb[0:1, tg * 512:(tg + 1) * 512],
                                     start=False, stop=True, skip_group_check=True)
                    s_iv.inc(instr)

            @on("vector")
            def _(e, en):
                for tg in range(4):
                    wait(e, en, s_iv, 5 + tg)
                    instr = e.tensor_copy(inv1_rep[:, tg * 512:(tg + 1) * 512], ps[4][:, 0:512])
                    s_ivr.inc(instr)

            def v_hevac(e, en, ot):
                if ot >= 1:
                    wait(e, en, s_hs, ot)
                for tg in range(4):
                    wait(e, en, s_mm3, 4 * ot + tg + 1)
                    wait(e, en, s_ivr, 4)
                    instr = e.scalar_tensor_tensor(h_t[:, tg * 512:(tg + 1) * 512], ps[tg][:, :],
                                                   acol[:, 0:1], inv1_rep[:, tg * 512:(tg + 1) * 512],
                                                   ALU.bypass, ALU.mult)
                    s_hv.inc(instr)

            @on("vector")
            def _(e, en):
                for ot in range(NOT_F):
                    ppwait(e, en, s_w1l, ot)
                    if ot >= 2:
                        wait(e, en, s_mm3, 4 * (ot - 1))
                    schain(e, en, e.tensor_scalar(sq_scr[:, :], wf[ot % 2][:, :], swc[:, 0:1], None, ALU.mult))
                    schain(e, en, e.tensor_scalar(sq_scr[:, :], sq_scr[:, :], MAGICR, MAGICR, ALU.add, ALU.subtract))
                    schain(e, en, e.tensor_copy(wq_i[:, :], sq_scr[:, :]))
                    instr = e.tensor_scalar(wq_b[ot % 2][:, :], wq_i[:, :], 1.0, -1.0, ALU.min, ALU.max)
                    s_w1q.inc(instr)
                    if ot >= 1:
                        v_hevac(e, en, ot - 1)
                v_hevac(e, en, NOT_F - 1)

            @on("tensor")
            def _(e, en):
                wait(e, en, s_xq1l, 16)
                for ot in range(NOT_F):
                    wait(e, en, s_w1q, ot + 1)
                    for tg in range(4):
                        if ot >= 1:
                            wait(e, en, s_hv, 4 * (ot - 1) + tg + 1)
                        for kt in range(KD):
                            instr = e.matmul(ps[tg][:, :],
                                             wq_b[ot % 2][:, kt * 128:(kt + 1) * 128],
                                             bass.AP(xq1_fm, kt * L + tg * 512, [[KD * L, 128], [1, 512]]),
                                             start=(kt == 0), stop=(kt == KD - 1))
                        s_mm3.inc(instr)
                    lag = ot - 2
                    if lag >= 0:
                        wait(e, en, s_hsq, lag + 1)
                        for tg in range(4):
                            e.matmul(ps[4][32 * tg:32 * tg + 1, 0:512], ones_c_b[:, 0:1],
                                     h_b[lag % 2][:, tg * 512:(tg + 1) * 512],
                                     start=(lag == 0), stop=(lag == NOT_F - 1),
                                     tile_position=(0, 32 * tg), skip_group_check=True)
                            instr = e.matmul(ps[5][32 * tg:32 * tg + 1, 0:512], ones_c_b[:, 0:1],
                                             hsq[:, tg * 512:(tg + 1) * 512],
                                             start=(lag == 0), stop=(lag == NOT_F - 1),
                                             tile_position=(0, 32 * tg), skip_group_check=True)
                        s_smt.inc(instr)
                for lag in (NOT_F - 2, NOT_F - 1):
                    wait(e, en, s_hsq, lag + 1)
                    for tg in range(4):
                        e.matmul(ps[4][32 * tg:32 * tg + 1, 0:512], ones_c_b[:, 0:1],
                                 h_b[lag % 2][:, tg * 512:(tg + 1) * 512],
                                 start=False, stop=(lag == NOT_F - 1),
                                 tile_position=(0, 32 * tg), skip_group_check=True)
                        instr = e.matmul(ps[5][32 * tg:32 * tg + 1, 0:512], ones_c_b[:, 0:1],
                                         hsq[:, tg * 512:(tg + 1) * 512],
                                         start=False, stop=(lag == NOT_F - 1),
                                         tile_position=(0, 32 * tg), skip_group_check=True)
                    s_smt.inc(instr)

            @on("scalar")
            def _(e, en):
                for ot in range(NOT_F):
                    wait(e, en, s_hv, 4 * (ot + 1))
                    if ot >= 2:
                        ppwait(e, en, s_hw, ot - 2)
                        wait(e, en, s_smt, ot - 1)
                    schain(e, en, e.activation(h_t[:, :], h_t[:, :], AF.Silu, bias=bcols[:, ot:ot + 1]))
                    instr = e.activation(h_b[ot % 2][:, :], h_t[:, :], AF.Gelu)
                    s_hs.inc(instr)
                    wait(e, en, s_hs, ot + 1)
                    instr = e.dma_start(out=h_sp[ot], in_=h_b[ot % 2][:, :])
                    ppissue(e, en, s_hw, ot, instr)

            @on("gpsimd")
            def _(e, en):
                for ot in range(NOT_F):
                    wait(e, en, s_hs, ot + 1)
                    if ot >= 1:
                        wait(e, en, s_smt, ot)
                    instr = e.tensor_tensor(hsq[:, :], h_b[ot % 2][:, :], h_b[ot % 2][:, :], ALU.mult)
                    s_hsq.inc(instr)

            barrier("p3a", extra=[("scalar", s_hw[0], 16 * NOT_F // 2),
                                  ("scalar", s_hw[1], 16 * NOT_F // 2)])

        phase_mark(6)
        row_c2h = E(nc.sbuf_tensor("row_c2h", [1, 2048], BF16))
        row_c2l = E(nc.sbuf_tensor("row_c2l", [1, 2048], BF16))
        row_iv2b = E(nc.sbuf_tensor("row_iv2b", [1, 2048], BF16))

        with contextlib.ExitStack() as sc4:
            E4 = sc4.enter_context
            lnh_t = [E4(nc.sbuf_tensor(f"lnh_t{i}", [128, 2048], BF16)) for i in range(2)]
            stg4 = E4(nc.sbuf_tensor("stg4", [128, 1024], F32))
            t_bf = E4(nc.sbuf_tensor("t_bf", [128, 2048], BF16))
            mu_rep = E4(nc.sbuf_tensor("mu_rep", [128, 2048], BF16))
            rs3_rep = E4(nc.sbuf_tensor("rs3_rep", [128, 2048], BF16))
            rmax = E4(nc.sbuf_tensor("rmax", [128, 2048], F32))
            hsq2 = E4(nc.sbuf_tensor("hsq2", [128, 2048], BF16))
            row_mu = E4(nc.sbuf_tensor("row_mu", [1, 2048], F32))
            row_v = E4(nc.sbuf_tensor("row_v", [1, 2048], F32))
            rhb2 = E4(nc.sbuf_tensor("rhb2", [1, 2048], BF16))
            rlb2 = E4(nc.sbuf_tensor("rlb2", [1, 2048], BF16))
            rsc = E4(nc.sbuf_tensor("rsc", [1, 2048], F32))

            s_r3 = sem("p4_r")
            s_mr = sem("p4_mr")
            s_hl = ppsem("p4_hl")
            s_ln = sem("p4_ln")
            s_lnw = ppsem("p4_lnw")
            s_sq2 = sem("p4_sq2")
            s_smt2 = sem("p4_smt2")
            s_s2e = sem("p4_s2e")
            s_q2 = sem("p4_q2")

            @on("vector")
            def _(e, en):
                wait(e, en, s_smt, NOT_F)
                for g in range(4):
                    schain(e, en, e.tensor_copy(stg4[32 * g:32 * g + 1, 0:512],
                                  ps[4][32 * g:32 * g + 1, 0:512]))
                    instr = e.tensor_copy(stg4[32 * g:32 * g + 1, 512:1024],
                                          ps[5][32 * g:32 * g + 1, 0:512])
                    if g < 3:
                        schain(e, en, instr)
                sem("p4_stg").inc(instr)
                wait(e, en, sem("p4_flt"), 32)
                schain(e, en, e.tensor_scalar(row_mu[0:1, :], row_mu[0:1, :], 1.0 / FF, None, ALU.mult))
                schain(e, en, e.tensor_scalar(row_v[0:1, :], row_v[0:1, :], 1.0 / FF, None, ALU.mult))
                schain(e, en, e.tensor_tensor(rsc[0:1, :], row_mu[0:1, :], row_mu[0:1, :], ALU.mult))
                schain(e, en, e.tensor_tensor(row_v[0:1, :], row_v[0:1, :], rsc[0:1, :], ALU.subtract))
                instr = e.tensor_scalar(row_v[0:1, :], row_v[0:1, :], 1.0, LN_EPS, ALU.mult, ALU.add)
                s_r3.inc(instr)

            @on("gpsimd")
            def _(e, en):
                wait(e, en, sem("p4_stg"), 1)
                sem("p4_flt").dinc(e.dma_start(out=row_mu[0:1, :].rearrange("o (g c) -> o g c", g=4),
                                   in_=bass.AP(stg4, 0, [[32 * 1024, 4], [1, 512]])))
                sem("p4_flt").dinc(e.dma_start(out=row_v[0:1, :].rearrange("o (g c) -> o g c", g=4),
                                   in_=bass.AP(stg4, 512, [[32 * 1024, 4], [1, 512]])))

            @on("scalar")
            def _(e, en):
                wait(e, en, s_r3, 1)
                instr = e.activation(row_v[0:1, :], row_v[0:1, :], AF.Sqrt)
                s_r3.inc(instr)

            @on("vector")
            def _(e, en):
                wait(e, en, s_r3, 2)
                schain(e, en, e.reciprocal(row_v[0:1, :], row_v[0:1, :]))
                schain(e, en, e.tensor_copy(rhb2[0:1, :], row_mu[0:1, :]))
                schain(e, en, e.tensor_copy(rsc[0:1, :], rhb2[0:1, :]))
                schain(e, en, e.tensor_tensor(rsc[0:1, :], row_mu[0:1, :], rsc[0:1, :], ALU.subtract))
                schain(e, en, e.tensor_copy(rlb2[0:1, :], rsc[0:1, :]))
                instr = e.nop()
                s_r3.inc(instr)

            @on("tensor")
            def _(e, en):
                wait(e, en, s_r3, 3)
                for tg in range(4):
                    if tg >= 1:
                        wait(e, en, s_mr, tg)
                    e.matmul(ps[4][:, 0:512], ones_r_b[0:1, :], rhb2[0:1, tg * 512:(tg + 1) * 512],
                             start=True, stop=False, skip_group_check=True)
                    instr = e.matmul(ps[4][:, 0:512], ones_r_b[0:1, :], rlb2[0:1, tg * 512:(tg + 1) * 512],
                                     start=False, stop=True, skip_group_check=True)
                    s_r3.inc(instr)

            @on("vector")
            def _(e, en):
                for tg in range(4):
                    wait(e, en, s_r3, 4 + tg)
                    instr = e.tensor_scalar(mu_rep[:, tg * 512:(tg + 1) * 512], ps[4][:, 0:512],
                                            -1.0, None, ALU.mult)
                    s_mr.inc(instr)
                instr = e.tensor_copy(rhb2[0:1, :], row_v[0:1, :])
                s_mr.inc(instr)

            @on("tensor")
            def _(e, en):
                wait(e, en, s_mr, 5)
                for tg in range(4):
                    if tg >= 1:
                        wait(e, en, s_mr, 5 + tg)
                    instr = e.matmul(ps[4][:, 0:512], ones_r_b[0:1, :], rhb2[0:1, tg * 512:(tg + 1) * 512],
                                     start=True, stop=True, skip_group_check=True)
                    s_r3.inc(instr)

            @on("vector")
            def _(e, en):
                for tg in range(4):
                    wait(e, en, s_r3, 8 + tg)
                    instr = e.tensor_copy(rs3_rep[:, tg * 512:(tg + 1) * 512], ps[4][:, 0:512])
                    s_mr.inc(instr)

            @on("sync")
            def _(e, en):
                for ot in range(NOT_F):
                    if ot >= 2:
                        ppwait(e, en, s_lnw, ot - 2)
                        wait(e, en, s_sq2, ot - 1)
                        wait(e, en, s_hl[ot % 2], 16 * (ot // 2))
                    instr = e.dma_start(out=lnh_t[ot % 2][:, :], in_=h_sp[ot])
                    ppissue(e, en, s_hl, ot, instr)

            @on("vector")
            def _(e, en):
                schain(e, en, e.memset(rmax[:, :], 0.0))
                wait(e, en, s_mr, 9)
                for ot in range(NOT_F):
                    ppwait(e, en, s_hl, ot)
                    if ot >= 1:
                        wait(e, en, s_ln, ot)
                    schain(e, en, e.tensor_tensor(t_bf[:, :], lnh_t[ot % 2][:, :], mu_rep[:, :], ALU.add))
                    schain(e, en, e.tensor_tensor(t_bf[:, :], t_bf[:, :], rs3_rep[:, :], ALU.mult))
                    instr = e.tensor_scalar(lnh_t[ot % 2][:, :], t_bf[:, :], gcols[:, ot:ot + 1],
                                            lbcols[:, ot:ot + 1], ALU.mult, ALU.add)
                    s_ln.inc(instr)
                    if ot >= 1:
                        wait(e, en, s_sq2, ot)
                        wait(e, en, sem("p4_rmx"), ot - 1)
                        instr = e.tensor_tensor(rmax[:, :], rmax[:, :], hsq2[:, :], ALU.max)
                        sem("p4_rmx").inc(instr)
                wait(e, en, s_sq2, NOT_F)
                wait(e, en, sem("p4_rmx"), NOT_F - 1)
                instr = e.tensor_tensor(rmax[:, :], rmax[:, :], hsq2[:, :], ALU.max)
                sem("p4_rmxf").inc(instr)

            @on("scalar")
            def _(e, en):
                for ot in range(NOT_F):
                    wait(e, en, s_ln, ot + 1)
                    if ot >= 2:
                        wait(e, en, s_lnw[ot % 2], 16 * (ot // 2))
                    instr = e.dma_start(out=lnh_sp[ot], in_=lnh_t[ot % 2][:, :])
                    ppissue(e, en, s_lnw, ot, instr)

            @on("gpsimd")
            def _(e, en):
                for ot in range(NOT_F):
                    wait(e, en, s_ln, ot + 1)
                    if ot >= 1:
                        wait(e, en, s_smt2, ot)
                        wait(e, en, sem("p4_rmx"), ot)
                    instr = e.tensor_tensor(hsq2[:, :], lnh_t[ot % 2][:, :], lnh_t[ot % 2][:, :],
                                            ALU.mult)
                    s_sq2.inc(instr)

            @on("tensor")
            def _(e, en):
                for ot in range(NOT_F):
                    wait(e, en, s_sq2, ot + 1)
                    for tg in range(4):
                        instr = e.matmul(ps[5][32 * tg:32 * tg + 1, 0:512], ones_c_b[:, 0:1],
                                         hsq2[:, tg * 512:(tg + 1) * 512],
                                         start=(ot == 0), stop=(ot == NOT_F - 1),
                                         tile_position=(0, 32 * tg), skip_group_check=True)
                    s_smt2.inc(instr)

            @on("vector")
            def _(e, en):
                wait(e, en, sem("p4_rmxf"), 1)
                instr = e.nop()
                s_q2.inc(instr)

            @on("tensor")
            def _(e, en):
                wait(e, en, s_q2, 1)
                for j in range(NT):
                    bank = ps[2] if j % 2 == 0 else ps[3]
                    if j >= 2:
                        wait(e, en, sem("p4_rme"), j - 1)
                    instr = e.transpose(bank[:, 0:128], rmax[:, j * 128:(j + 1) * 128],
                                        identf_sb[:, :])
                    sem("p4_rmt").inc(instr)

            @on("vector")
            def _(e, en):
                for j in range(NT):
                    wait(e, en, sem("p4_rmt"), j + 1)
                    bank = ps[2] if j % 2 == 0 else ps[3]
                    instr = e.tensor_reduce(ctmp[:, j:j + 1], bank[:, 0:128], AX.X, ALU.max)
                    sem("p4_rme").inc(instr)

            @on("tensor")
            def _(e, en):
                wait(e, en, sem("p4_rme"), NT)
                instr = e.transpose(ps[2][0:16, 0:128], ctmp[:, 0:16], identf_sb[:, :])
                sem("p4_rmt").inc(instr)

            @on("vector")
            def _(e, en):
                wait(e, en, sem("p4_rmt"), NT + 1)
                instr = e.tensor_copy(stg4[0:16, 0:128], ps[2][0:16, 0:128])
                sem("p4_rme").inc(instr)

            @on("gpsimd")
            def _(e, en):
                wait(e, en, sem("p4_rme"), NT + 1)
                instr = e.dma_start(out=row_v[0:1, :].rearrange("o (t p) -> o t p", t=16),
                                    in_=stg4[0:16, 0:128])
                sem("p4_flt3").inc(instr, 16)

            @on("vector")
            def _(e, en):
                wait(e, en, s_smt2, NOT_F)
                for g in range(4):
                    instr = e.tensor_copy(stg4[32 * g:32 * g + 1, 512:1024],
                                          ps[5][32 * g:32 * g + 1, 0:512])
                sem("p4_stg2").inc(instr)
                wait(e, en, sem("p4_flt2"), 16)
                instr = e.tensor_scalar(rsc[0:1, :], rsc[0:1, :], 1.0 / FF, LN_EPS, ALU.mult, ALU.add)
                s_q2.inc(instr)

            @on("gpsimd")
            def _(e, en):
                wait(e, en, sem("p4_stg2"), 1)
                instr = e.dma_start(out=rsc[0:1, :].rearrange("o (g c) -> o g c", g=4),
                                    in_=bass.AP(stg4, 512, [[32 * 1024, 4], [1, 512]]))
                sem("p4_flt2").inc(instr, 16)

            @on("scalar")
            def _(e, en):
                wait(e, en, s_q2, 2)
                instr = e.activation(rsc[0:1, :], rsc[0:1, :], AF.Sqrt)
                s_q2.inc(instr)

            @on("scalar")
            def _(e, en):
                wait(e, en, sem("p4_flt3"), 16)
                instr = e.activation(row_v[0:1, :], row_v[0:1, :], AF.Sqrt)
                sem("p4_rmsq").inc(instr)

            @on("vector")
            def _(e, en):
                wait(e, en, s_q2, 3)
                wait(e, en, sem("p4_rmsq"), 1)
                schain(e, en, e.reciprocal(rsc[0:1, :], rsc[0:1, :]))
                schain(e, en, e.tensor_tensor(row_mu[0:1, :], row_v[0:1, :], rsc[0:1, :], ALU.mult))
                schain(e, en, e.tensor_scalar(row_mu[0:1, :], row_mu[0:1, :], 1e-5, None, ALU.max))
                schain(e, en, e.tensor_scalar(row_v[0:1, :], row_mu[0:1, :], swc[0:1, 3:4], None, ALU.mult))
                schain(e, en, e.tensor_copy(row_iv2b[0:1, :], row_v[0:1, :]))
                schain(e, en, e.reciprocal(row_mu[0:1, :], row_mu[0:1, :]))
                schain(e, en, e.tensor_scalar(row_mu[0:1, :], row_mu[0:1, :], 127.0, None, ALU.mult))
                schain(e, en, e.tensor_tensor(row_mu[0:1, :], row_mu[0:1, :], rsc[0:1, :], ALU.mult))
                schain(e, en, e.tensor_copy(row_c2h[0:1, :], row_mu[0:1, :]))
                schain(e, en, e.tensor_copy(rsc[0:1, :], row_c2h[0:1, :]))
                schain(e, en, e.tensor_tensor(rsc[0:1, :], row_mu[0:1, :], rsc[0:1, :], ALU.subtract))
                instr = e.tensor_copy(row_c2l[0:1, :], rsc[0:1, :])
                s_q2.inc(instr)

            barrier("p3b", extra=[("scalar", s_lnw[0], 16 * NOT_F // 2),
                                  ("scalar", s_lnw[1], 16 * NOT_F // 2),
                                  ("vector", s_q2, 4)])

        phase_mark(7)
        with contextlib.ExitStack() as sc5:
            E5 = sc5.enter_context
            hq_sb = E5(nc.sbuf_tensor("hq_sb", [128, KF * 512], BF16))
            lnh_in = E5(nc.sbuf_tensor("lnh_in", [128, 512], BF16))
            hq_f = E5(nc.sbuf_tensor("hq_f", [128, 512], F32))
            hq_i = E5(nc.sbuf_tensor("hq_i", [128, 512], I32))
            c2_rep = E5(nc.sbuf_tensor("c2_rep", [128, 2048], F32))
            inv2_rep = E5(nc.sbuf_tensor("inv2_rep", [128, 2048], F32))
            w2qb = [E5(nc.sbuf_tensor(f"w2qb{i}", [128, KF * 128], BF16)) for i in range(2)]
            oasm = E5(nc.sbuf_tensor("oasm", [128, 4 * 2048], BF16))
            fout = E5(nc.sbuf_tensor("fout", [128, 2048], F32))
            _aacc = E5(nc.sbuf_tensor("aacc_in", [128, DM], BF16))
            aacc_in = [_aacc, _aacc]
            fo_b = E5(nc.sbuf_tensor("fo_b", [128, 512], BF16))

            s_c2p = sem("p5_c2")
            s_c2e = sem("p5_c2e")
            s_hql = sem("p5_hql")
            s_hq = sem("p5_hq")
            s_w2r = ppsem("p5_w2r")
            s_mm4 = sem("p5_mm")
            s_fv = sem("p5_fv")
            s_ftr = sem("p5_ftr")
            s_oa = sem("p5_oa")
            s_aal = sem("p5_aal")
            s_fadd = sem("p5_fadd")
            s_outw = sem("p5_out")

            @on("tensor")
            def _(e, en):
                s_c2p.inc(e.nop())
                for tg in range(4):
                    if tg >= 1:
                        wait(e, en, s_c2e, tg)
                    e.matmul(ps[4][:, 0:512], ones_r_b[0:1, :], row_c2h[0:1, tg * 512:(tg + 1) * 512],
                             start=True, stop=False, skip_group_check=True)
                    instr = e.matmul(ps[4][:, 0:512], ones_r_b[0:1, :], row_c2l[0:1, tg * 512:(tg + 1) * 512],
                                     start=False, stop=True, skip_group_check=True)
                    s_c2p.inc(instr)
                for tg in range(4):
                    wait(e, en, s_c2e, 4 + tg)
                    instr = e.matmul(ps[4][:, 0:512], ones_r_b[0:1, :], row_iv2b[0:1, tg * 512:(tg + 1) * 512],
                                     start=True, stop=True, skip_group_check=True)
                    s_c2p.inc(instr)

            @on("vector")
            def _(e, en):
                for tg in range(4):
                    wait(e, en, s_c2p, 2 + tg)
                    instr = e.tensor_copy(c2_rep[:, tg * 512:(tg + 1) * 512], ps[4][:, 0:512])
                    s_c2e.inc(instr)
                for tg in range(4):
                    wait(e, en, s_c2p, 6 + tg)
                    instr = e.tensor_copy(inv2_rep[:, tg * 512:(tg + 1) * 512], ps[4][:, 0:512])
                    s_c2e.inc(instr)

            for tg in range(4):
                @on("sync")
                def _(e, en, tg=tg):
                    for kt in range(KF):
                        i = tg * KF + kt
                        if i >= 1:
                            wait(e, en, s_hq, i)
                        instr = e.dma_start(out=lnh_in[:, :],
                                            in_=lnh_sp[kt][:, tg * 512:(tg + 1) * 512])
                        s_hql.dinc(instr)

                @on("vector")
                def _(e, en, tg=tg):
                    wait(e, en, s_c2e, 8)
                    if tg >= 1:
                        wait(e, en, s_fv, tg * OT2)
                    for kt in range(KF):
                        i = tg * KF + kt
                        wait(e, en, s_hql, 16 * (i + 1))
                        schain(e, en, e.tensor_tensor(hq_f[:, :], lnh_in[:, :],
                                        c2_rep[:, tg * 512:(tg + 1) * 512], ALU.mult))
                        schain(e, en, e.tensor_scalar(hq_f[:, :], hq_f[:, :], MAGICR, MAGICR, ALU.add, ALU.subtract))
                        schain(e, en, e.tensor_copy(hq_i[:, :], hq_f[:, :]))
                        instr = e.tensor_copy(hq_sb[:, kt * 512:(kt + 1) * 512], hq_i[:, :])
                        s_hq.inc(instr)

                @on("gpsimd")
                def _(e, en, tg=tg):
                    for o in range(0, OT2, 2):
                        i = tg * OT2 + o
                        if i >= 2:
                            wait(e, en, s_mm4, i - 1)
                            wait(e, en, s_w2r[i % 2], 16 * (i // 2))
                        instr = e.dma_start(out=w2qb[o % 2][:, :].rearrange("p (k c2) -> p k c2", k=KF),
                                            in_=w2q_sp[o])
                        ppissue(e, en, s_w2r, i, instr)

                @on("scalar")
                def _(e, en, tg=tg):
                    for o in range(1, OT2, 2):
                        i = tg * OT2 + o
                        if i >= 2:
                            wait(e, en, s_mm4, i - 1)
                            wait(e, en, s_w2r[i % 2], 16 * (i // 2))
                        instr = e.dma_start(out=w2qb[o % 2][:, :].rearrange("p (k c2) -> p k c2", k=KF),
                                            in_=w2q_sp[o])
                        ppissue(e, en, s_w2r, i, instr)

                def te_ftr(e, en, tg, o):
                    i = tg * OT2 + o
                    wait(e, en, s_fv, i + 1)
                    if i >= 1:
                        wait(e, en, s_oa, i)
                    for j in range(4):
                        instr = e.transpose(psA[:, j * 128:(j + 1) * 128],
                                            fo_b[:, j * 128:(j + 1) * 128], identb_sb[:, :])
                    s_ftr.inc(instr)

                def v_oa(e, en, tg, o):
                    i = tg * OT2 + o
                    wait(e, en, s_ftr, i + 1)
                    instr = e.tensor_copy(
                        bass.AP(oasm, o * 128, [[4 * 2048, 128], [2048, 4], [1, 128]]),
                        psA[:, 0:512].rearrange("p (j c2) -> p j c2", j=4))
                    s_oa.inc(instr)

                @on("tensor")
                def _(e, en, tg=tg):
                    wait(e, en, s_hq, (tg + 1) * KF)
                    for o in range(OT2):
                        i = tg * OT2 + o
                        ppwait(e, en, s_w2r, i)
                        if i >= 1:
                            wait(e, en, s_fv, i)
                        for kt in range(KF):
                            instr = e.matmul(ps[0][:, :], w2qb[o % 2][:, kt * 128:(kt + 1) * 128],
                                             bass.AP(hq_sb, kt * 512, [[KF * 512, 128], [1, 512]]),
                                             start=(kt == 0), stop=(kt == KF - 1))
                        s_mm4.inc(instr)
                        if o >= 1:
                            te_ftr(e, en, tg, o - 1)
                    te_ftr(e, en, tg, OT2 - 1)

                @on("vector")
                def _(e, en, tg=tg):
                    for o in range(OT2):
                        i = tg * OT2 + o
                        wait(e, en, s_mm4, i + 1)
                        if i >= 1:
                            wait(e, en, s_ftr, i)
                        schain(e, en, e.tensor_tensor(hq_f[:, :], ps[0][:, :],
                                        inv2_rep[:, tg * 512:(tg + 1) * 512], ALU.mult))
                        instr = e.tensor_scalar(fo_b[:, :], hq_f[:, :], b2cols[:, o:o + 1],
                                                None, ALU.add)
                        s_fv.inc(instr)
                        if o >= 1:
                            v_oa(e, en, tg, o - 1)
                    v_oa(e, en, tg, OT2 - 1)

                @on("gpsimd")
                def _(e, en, tg=tg):
                    for j in range(4):
                        t = tg * 4 + j
                        ii = tg * 4 + j
                        if ii >= 1:
                            wait(e, en, s_fadd, ii)
                        instr = e.dma_start(out=aacc_in[j % 2][:, :], in_=aacc_sp[t])
                        s_aal.dinc(instr)

                @on("vector")
                def _(e, en, tg=tg):
                    wait(e, en, s_oa, (tg + 1) * OT2)
                    for j in range(4):
                        ii = tg * 4 + j
                        wait(e, en, s_aal, 16 * (ii + 1))
                        if ii >= 1:
                            wait(e, en, s_outw, 16 * ii)
                        instr = e.tensor_tensor(fout[:, :], oasm[:, j * 2048:(j + 1) * 2048],
                                                aacc_in[j % 2][:, :], ALU.add)
                        s_fadd.inc(instr)

                @on("scalar")
                def _(e, en, tg=tg):
                    for j in range(4):
                        t = tg * 4 + j
                        ii = tg * 4 + j
                        wait(e, en, s_fadd, ii + 1)
                        instr = e.dma_start(out=out[t * 128:(t + 1) * 128, :], in_=fout[:, :])
                        s_outw.dinc(instr)

            @on("sync")
            def _(e, en):
                wait(e, en, sem("p5_out"), 16 * 16)
                e.nop()

    return nc


def _consts():
    m = np.full((128, 512), MASK_NEG, dtype=np.float32)
    pp = np.arange(128)[:, None]
    ll = np.arange(256)[None, :]
    m[:, 0:256] = np.where(pp <= ll, 0.0, MASK_NEG)
    m[:, 256:512] = np.where((128 + pp) <= ll, 0.0, MASK_NEG)
    return dict(identb=np.eye(128, dtype=ml_dtypes.bfloat16),
                identf=np.eye(128, dtype=np.float32),
                onesrow_b=np.ones((1, 128), dtype=ml_dtypes.bfloat16),
                onescol_b=np.ones((128, 1), dtype=ml_dtypes.bfloat16),
                onesrow_f=np.ones((1, 128), dtype=np.float32),
                maskt=m,
                sel2in=np.concatenate([
                    np.concatenate([np.ones((1, 128), np.float32), np.zeros((1, 128), np.float32)], 1),
                    np.concatenate([np.zeros((1, 128), np.float32), np.ones((1, 128), np.float32)], 1)], 0))


_NC = None


def kernel(L0_attn_u, L0_attn_W_in, L0_attn_conv_w, L0_attn_conv_b, L0_attn_A_log,
           L0_attn_D, L0_attn_dt_bias, L0_attn_norm_w, L0_attn_W_out,
           L0_ffn_w1, L0_ffn_b1, L0_ffn_ln_g, L0_ffn_ln_b, L0_ffn_w2, L0_ffn_b2):
    global _NC
    if _NC is None:
        _NC = build_nc()
    nc = _NC

    win_pad = np.zeros((NOT_IN * 128, DM), dtype=np.float32)
    win_pad[:8512] = L0_attn_W_in
    winT = np.ascontiguousarray(win_pad.T.reshape(KD, 128, NOT_IN, 128).transpose(2, 1, 0, 3))
    w1T = np.ascontiguousarray(L0_ffn_w1.T.reshape(KD, 128, NOT_F, 128).transpose(2, 1, 0, 3))
    w2T = np.ascontiguousarray(L0_ffn_w2.T.reshape(KF, 128, OT2, 128).transpose(2, 1, 0, 3))
    woutT = np.ascontiguousarray(L0_attn_W_out.T)

    colmaj = lambda v, k: np.ascontiguousarray(np.asarray(v).reshape(k, 128).T)
    shared = dict(winT=winT, w1T=w1T, w2T=w2T, woutT=woutT,
                  convw=np.ascontiguousarray(
                      L0_attn_conv_w.reshape(NOT_XBC, 128, 4).transpose(1, 0, 2).reshape(128, NOT_XBC * 4)),
                  convb=colmaj(L0_attn_conv_b, NOT_XBC),
                  a_log=np.ascontiguousarray(L0_attn_A_log.reshape(64, 1)),
                  d_in=np.ascontiguousarray(L0_attn_D.reshape(1, 64)),
                  dt_bias=np.ascontiguousarray(L0_attn_dt_bias.reshape(64, 1)),
                  norm_w=colmaj(L0_attn_norm_w, 32), b1=colmaj(L0_ffn_b1, NOT_F),
                  ln_g=colmaj(L0_ffn_ln_g, NOT_F), ln_b=colmaj(L0_ffn_ln_b, NOT_F),
                  b2=colmaj(L0_ffn_b2, OT2), **_consts())
    in_maps = []
    for b in range(8):
        m = dict(shared)
        m["u"] = np.ascontiguousarray(L0_attn_u[b])
        in_maps.append(m)

    res = run_bass_kernel_spmd(nc, in_maps, core_ids=list(range(8)))
    return np.stack([np.asarray(res.results[b]["out"], dtype=np.float32) for b in range(8)])




# version 07 (final)
# speedup vs baseline: 1.0051x; 1.0051x over previous
import os
import contextlib
import numpy as np
import ml_dtypes
import concourse.bass as bass
import concourse.mybir as mybir
from concourse.bass_utils import run_bass_kernel_spmd

F32 = mybir.dt.float32
BF16 = mybir.dt.bfloat16
I32 = mybir.dt.int32
AF = mybir.ActivationFunctionType
ALU = mybir.AluOpType
AX = mybir.AxisListType

L, DM, DIN, H, Q = 2048, 2048, 4096, 64, 256
NCH, NT, KD = L // Q, L // 128, DM // 128
NOT_Z, NOT_XBC = 32, 34
NOT_IN = NOT_Z + NOT_XBC + 1
FF, NOT_F, KF, OT2 = 8192, 64, 64, 16
RMS_EPS = 1.1920929e-07
LN_EPS = 1e-5
MASK_NEG = -1e30
MAGICR = 12582912.0
ENGS = ("sync", "scalar", "vector", "tensor", "gpsimd")
_PH = int(os.environ.get("MK_PH", "9"))
_SKIP = set(x for x in os.environ.get("MK_SKIP", "").split(",") if x)


def build_nc():
    nc = bass.Bass("TRN2", target_bir_lowering=False, detect_race_conditions=False)

    u = nc.dram_tensor("u", [L, DM], F32, kind="ExternalInput").ap()
    winT = nc.dram_tensor("winT", [NOT_IN, 128, KD, 128], F32, kind="ExternalInput").ap()
    w1T = nc.dram_tensor("w1T", [NOT_F, 128, KD, 128], F32, kind="ExternalInput").ap()
    w2T = nc.dram_tensor("w2T", [OT2, 128, KF, 128], F32, kind="ExternalInput").ap()
    woutT = nc.dram_tensor("woutT", [DIN, DM], F32, kind="ExternalInput").ap()
    convw = nc.dram_tensor("convw", [128, NOT_XBC * 4], F32, kind="ExternalInput").ap()
    convb = nc.dram_tensor("convb", [128, NOT_XBC], F32, kind="ExternalInput").ap()
    a_log = nc.dram_tensor("a_log", [64, 1], F32, kind="ExternalInput").ap()
    d_in = nc.dram_tensor("d_in", [1, 64], F32, kind="ExternalInput").ap()
    dt_bias = nc.dram_tensor("dt_bias", [64, 1], F32, kind="ExternalInput").ap()
    norm_w = nc.dram_tensor("norm_w", [128, 32], F32, kind="ExternalInput").ap()
    b1 = nc.dram_tensor("b1", [128, NOT_F], F32, kind="ExternalInput").ap()
    ln_g = nc.dram_tensor("ln_g", [128, NOT_F], F32, kind="ExternalInput").ap()
    ln_b = nc.dram_tensor("ln_b", [128, NOT_F], F32, kind="ExternalInput").ap()
    b2 = nc.dram_tensor("b2", [128, OT2], F32, kind="ExternalInput").ap()
    identb = nc.dram_tensor("identb", [128, 128], BF16, kind="ExternalInput").ap()
    identf = nc.dram_tensor("identf", [128, 128], F32, kind="ExternalInput").ap()
    onesrow_b = nc.dram_tensor("onesrow_b", [1, 128], BF16, kind="ExternalInput").ap()
    onescol_b = nc.dram_tensor("onescol_b", [128, 1], BF16, kind="ExternalInput").ap()
    onesrow_f = nc.dram_tensor("onesrow_f", [1, 128], F32, kind="ExternalInput").ap()
    maskt = nc.dram_tensor("maskt", [128, 512], F32, kind="ExternalInput").ap()
    sel2in = nc.dram_tensor("sel2in", [2, 256], F32, kind="ExternalInput").ap()

    out = nc.dram_tensor("out", [L, DM], F32, kind="ExternalOutput").ap()
    _SCRK = "ExternalOutput" if os.environ.get("MK_DBGOUT") else "Internal"
    xq1_sp = nc.dram_tensor("xq1_sp", [KD, 128, L], BF16, kind=_SCRK).ap()
    z_sp = nc.dram_tensor("z_sp", [NOT_Z, 128, L], BF16, kind=_SCRK).ap()
    xfm_sp = nc.dram_tensor("xfm_sp", [32, 128, L], BF16, kind=_SCRK).ap()
    xtm_sp = nc.dram_tensor("xtm_sp", [32, 128, 2048], BF16, kind=_SCRK).ap()
    yg_sp = nc.dram_tensor("yg_sp", [NCH, 128, 8192], BF16, kind=_SCRK).ap()
    aacc_sp = nc.dram_tensor("aacc_sp", [NT, 128, DM], BF16, kind=_SCRK).ap()
    h_sp = nc.dram_tensor("h_sp", [NOT_F, 128, L], BF16, kind=_SCRK).ap()
    lnh_sp = nc.dram_tensor("lnh_sp", [NOT_F, 128, L], BF16, kind=_SCRK).ap()
    w2q_sp = nc.dram_tensor("w2q_sp", [OT2, 128, KF * 128], BF16, kind=_SCRK).ap()

    stack = contextlib.ExitStack()
    with stack:
        E = stack.enter_context

        wf = [E(nc.sbuf_tensor(f"wf{i}", [128, 2048], F32)) for i in range(2)]
        wq_b = [E(nc.sbuf_tensor(f"wq_b{i}", [128, 2048], BF16)) for i in range(2)]
        sq_scr = E(nc.sbuf_tensor("sq_scr", [128, 2048], F32))
        csum = E(nc.sbuf_tensor("csum", [128, 16], F32))
        csq = E(nc.sbuf_tensor("csq", [128, 16], F32))
        cmean = E(nc.sbuf_tensor("cmean", [128, 16], F32))
        crstd = E(nc.sbuf_tensor("crstd", [128, 16], F32))
        cr2 = E(nc.sbuf_tensor("cr2", [128, 16], F32))
        camax = E(nc.sbuf_tensor("camax", [128, 16], F32))
        cr2s1 = E(nc.sbuf_tensor("cr2s1", [128, 16], F32))
        ctmp = E(nc.sbuf_tensor("ctmp", [128, 16], F32))
        acol = E(nc.sbuf_tensor("acol", [128, 4], F32))
        swc = E(nc.sbuf_tensor("swc", [128, 4], F32))
        absacc = E(nc.sbuf_tensor("absacc", [128, 2], F32))
        pcol = E(nc.sbuf_tensor("pcol", [64, 4], F32))
        d_row = E(nc.sbuf_tensor("d_row", [1, 64], F32))
        d_rep = E(nc.sbuf_tensor("d_rep", [128, 64], F32))
        normw_sb = E(nc.sbuf_tensor("normw_sb", [128, 32], F32))
        bcols = E(nc.sbuf_tensor("bcols", [128, NOT_F], F32))
        gcols = E(nc.sbuf_tensor("gcols", [128, NOT_F], F32))
        lbcols = E(nc.sbuf_tensor("lbcols", [128, NOT_F], F32))
        b2cols = E(nc.sbuf_tensor("b2cols", [128, OT2], F32))
        identb_sb = E(nc.sbuf_tensor("identb_sb", [128, 128], BF16))
        identf_sb = E(nc.sbuf_tensor("identf_sb", [128, 128], F32))
        ones_r_b = E(nc.sbuf_tensor("ones_r_b", [1, 128], BF16))
        ones_c_b = E(nc.sbuf_tensor("ones_c_b", [128, 1], BF16))
        ones_r_f = E(nc.sbuf_tensor("ones_r_f", [1, 128], F32))
        sel2 = E(nc.sbuf_tensor("sel2", [2, 256], F32))
        mask_sb = E(nc.sbuf_tensor("mask_sb", [128, 512], F32))

        ps = [E(nc.psum_tensor(f"ps{i}", [128, 512], F32)) for i in range(6)]
        psA = E(nc.psum_tensor("psA", [128, 1024], BF16))
        psB = E(nc.psum_tensor("psB", [128, 1024], BF16))

        class Phys:
            __slots__ = ("h", "n", "name")
            def __init__(self, name):
                self.h = E(nc.semaphore(name)); self.n = 0; self.name = name

        class Sem:
            __slots__ = ("p", "base", "lname")
            def __init__(self, phys, lname):
                self.p = phys; self.base = phys.n; self.lname = lname
            @property
            def n(self):
                return self.p.n - self.base
            def inc(self, instr, k=1):
                instr.then_inc(self.p.h, k); self.p.n += k
            def dinc(self, instr):
                self.inc(instr, 16)

        SEMS = {}
        _pool = []
        _pool_next = {}
        SW_SEMS = {"p0_xq1w", "p1_wld", "p1_zw", "p1_xwa", "p1_xwb", "p1_xtw", "p2_ch", "p2_row",
                   "p2_ygw", "pb_wo", "pb_ygla", "pb_yglb", "pb_u2a", "pb_u2b", "pb_aw", "pb_w2wa", "pb_w2wb",
                   "p3_xq1l", "p3_ivd", "p3_hwa", "p3_hwb", "p4_lnwa", "p4_lnwb", "p4_flt", "p4_flt2",
                   "p4_flt3", "p5_w2ra", "p5_w2rb", "p5_aal", "p5_out"}
        def sem(name):
            if name in SEMS:
                return SEMS[name]
            if (name.startswith("bar_") or name.startswith("chn_")
                    or name in ("pre", "prep") or name in SW_SEMS):
                SEMS[name] = Sem(Phys(name), name)
                return SEMS[name]
            pref = name.split("_")[0]
            idx = _pool_next.get(pref, 0)
            _pool_next[pref] = idx + 1
            while len(_pool) <= idx:
                _pool.append(Phys(f"pool{len(_pool)}"))
            SEMS[name] = Sem(_pool[idx], name)
            return SEMS[name]

        waited = {}
        def wait(eng, engname, s, val=None):
            v = (s.p.n) if val is None else (s.base + val)
            if v <= s.base and val is not None and val <= 0:
                return
            if v <= 0:
                return
            k = (engname, s.p.name)
            if waited.get(k, -1) >= v:
                return
            eng.wait_ge(s.p.h, v)
            waited[k] = v

        def ppsem(name):
            return [sem(name + "a"), sem(name + "b")]

        def ppwait(e, en, ss, t):
            wait(e, en, ss[t % 2], 16 * (t // 2 + 1))

        def ppissue(e, en, ss, t, instr):
            ss[t % 2].inc(instr, 16)

        blk = E(nc.Block())
        _chain = sem_holder = {}
        def schain(e, engname, instr):
            cs = sem("chain_" + engname) if False else None
            s_ = SEMS.get("chn_" + engname)
            if s_ is None:
                s_ = sem("chn_" + engname)
            s_.inc(instr)
            e.wait_ge(s_.p.h, s_.p.n)
        _cur = [0]

        def phase_mark(k):
            _cur[0] = k

        def on(engname):
            def deco(f):
                if _cur[0] > _PH:
                    return
                getattr(blk, engname)(lambda e: f(e, engname))
            return deco

        def barrier(name, extra=()):
            s = sem("bar_" + name)
            for engname in ENGS:
                @on(engname)
                def _(e, en, engname=engname):
                    for (wn, ws, wv) in extra:
                        if wn == engname:
                            wait(e, en, ws, wv)
                    s.inc(e.nop())
            for engname in ENGS:
                @on(engname)
                def _(e, en):
                    wait(e, en, s, 5)
            return s

        s_pre = sem("pre")
        s_prep = sem("prep")

        @on("sync")
        def _(e, en):
            for (dst, src) in ((identb_sb[:, :], identb), (identf_sb[:, :], identf),
                               (ones_r_b[:, :], onesrow_b), (ones_c_b[:, :], onescol_b),
                               (ones_r_f[:, :], onesrow_f), (mask_sb[:, :], maskt),
                               (sel2[:, :], sel2in), (pcol[:, 0:1], a_log),
                               (pcol[:, 1:2], dt_bias), (d_row[:, :], d_in),
                               (normw_sb[:, :], norm_w), (bcols[:, :], b1),
                               (gcols[:, :], ln_g), (lbcols[:, :], ln_b),
                               (b2cols[:, :], b2)):
                s_pre.dinc(e.dma_start(out=dst, in_=src))

        @on("scalar")
        def _(e, en):
            wait(e, en, s_pre)
            schain(e, en, e.activation(pcol[:, 2:3], pcol[:, 0:1], AF.Exp))
            instr = e.mul(pcol[:, 2:3], pcol[:, 2:3], -1.0)
            s_prep.inc(instr)

        @on("tensor")
        def _(e, en):
            wait(e, en, s_pre)
            instr = e.matmul(ps[0][:, 0:64], ones_r_f[0:1, :], d_row[0:1, :], start=True, stop=True)
            s_prep.inc(instr)

        @on("vector")
        def _(e, en):
            wait(e, en, s_prep, 2)
            e.memset(acol[:, :], 0.0)
            e.memset(absacc[:, :], 0.0)
            instr = e.tensor_copy(d_rep[:, :], ps[0][:, 0:64])
            s_prep.inc(instr)

        barrier("pro")

        phase_mark(1)
        scspan = contextlib.ExitStack()
        Esp = scspan.enter_context
        B_fm = Esp(nc.sbuf_tensor("B_fm", [128, L], BF16))
        C_fm = Esp(nc.sbuf_tensor("C_fm", [128, L], BF16))
        B_tm = Esp(nc.sbuf_tensor("B_tm", [128, NT * 128], BF16))
        dt_fm = Esp(nc.sbuf_tensor("dt_fm", [64, L], F32))
        acum_fm = Esp(nc.sbuf_tensor("acum_fm", [64, L], F32))
        acum_hi = Esp(nc.sbuf_tensor("acum_hi", [64, L], BF16))
        acum_lo = Esp(nc.sbuf_tensor("acum_lo", [64, L], BF16))
        acum_tm = Esp(nc.sbuf_tensor("acum_tm", [128, NT * 64], F32))
        nacum_tm = Esp(nc.sbuf_tensor("nacum_tm", [128, NT * 64], F32))
        dt_tm = Esp(nc.sbuf_tensor("dt_tm", [128, NT * 64], F32))
        dtdec_fm = Esp(nc.sbuf_tensor("dtdec_fm", [64, Q], F32))
        dtdec_tm = Esp(nc.sbuf_tensor("dtdec_tm", [128, 2 * 64], F32))

        with contextlib.ExitStack() as sc0:
            scx = contextlib.ExitStack()
            E0 = scx.enter_context
            xn_fm = E0(nc.sbuf_tensor("xn_fm", [128, KD * L], BF16))
            trst = E0(nc.sbuf_tensor("trst", [128, 2048], BF16))
            sc0i = contextlib.ExitStack()
            E0i = sc0i.enter_context
            u_tl = [E0i(nc.sbuf_tensor(f"u_tl{i}", [128, DM], F32)) for i in range(2)]
            xn_tm = E0i(nc.sbuf_tensor("xn_tm", [128, DM], F32))
            xn_tm_b = E0i(nc.sbuf_tensor("xn_tm_b", [128, DM], BF16))
            xq_tm_i = E0i(nc.sbuf_tensor("xq_tm_i", [128, DM], I32))
            xq_tm_b = E0i(nc.sbuf_tensor("xq_tm_b", [128, DM], BF16))

            s_u = ppsem("p0_u")
            s_st0 = sem("p0_st")
            s_sq0 = sem("p0_sq")
            s_c0 = sem("p0_c")
            s_u2p = ppsem("p0_u2")
            s_x2 = sem("p0_x2")
            s_tr0 = sem("p0_tr")
            s_ev0 = sem("p0_ev")
            s_xq1w = sem("p0_xq1w")

            @on("sync")
            def _(e, en):
                for t in range(NT):
                    if t >= 2:
                        wait(e, en, s_st0, t - 1)
                        wait(e, en, sem("p0_sqs"), t - 1)
                        wait(e, en, s_u[t % 2], 16 * (t // 2))
                    instr = e.dma_start(out=u_tl[t % 2][:, :], in_=u[t * 128:(t + 1) * 128, :])
                    ppissue(e, en, s_u, t, instr)
                for t in range(NT):
                    wait(e, en, s_c0, 3)
                    if t >= 2:
                        wait(e, en, s_x2, 2 * (t - 1))
                        wait(e, en, s_u2p[t % 2], 16 * (t // 2))
                    instr = e.dma_start(out=u_tl[t % 2][:, :], in_=u[t * 128:(t + 1) * 128, :])
                    ppissue(e, en, s_u2p, t, instr)

            @on("vector")
            def _(e, en):
                for t in range(NT):
                    ppwait(e, en, s_u, t)
                    wait(e, en, sem("p0_sqs"), t)
                    instr = e.tensor_reduce(csum[:, t:t + 1], u_tl[t % 2][:, :], AX.X, ALU.add)
                    s_st0.inc(instr)
                wait(e, en, sem("p0_sqs"), NT)
                wait(e, en, s_st0, NT)
                schain(e, en, e.tensor_scalar(cmean[:, :], csum[:, :], 1.0 / DM, None, ALU.mult))
                schain(e, en, e.tensor_scalar(ctmp[:, :], csq[:, :], 1.0 / DM, None, ALU.mult))
                schain(e, en, e.scalar_tensor_tensor(csq[:, :], cmean[:, :], acol[:, 0:1], cmean[:, :],
                                       ALU.bypass, ALU.mult))
                schain(e, en, e.tensor_tensor(csq[:, :], ctmp[:, :], csq[:, :], ALU.subtract))
                instr = e.tensor_scalar(csq[:, :], csq[:, :], 1.0, LN_EPS, ALU.mult, ALU.add)
                s_c0.inc(instr)
                wait(e, en, s_sq0, 1)
                schain(e, en, e.reciprocal(crstd[:, :], ctmp[:, :]))
                schain(e, en, e.scalar_tensor_tensor(cr2[:, :], crstd[:, :], acol[:, 0:1], crstd[:, :],
                                       ALU.bypass, ALU.mult))
                instr = e.tensor_scalar(cr2[:, :], cr2[:, :], -LN_EPS, 1.0 + LN_EPS, ALU.mult, ALU.add)
                s_c0.inc(instr)
                wait(e, en, s_sq0, 2)
                instr = e.reciprocal(cr2[:, :], ctmp[:, :])
                s_c0.inc(instr)
                for t in range(NT):
                    ppwait(e, en, s_u2p, t)
                    ut = u_tl[t % 2]
                    schain(e, en, e.tensor_scalar(xn_tm[:, :], ut[:, :], cmean[:, t:t + 1], crstd[:, t:t + 1],
                                    ALU.subtract, ALU.mult))
                    schain(e, en, e.tensor_reduce(ctmp[:, t:t + 1], xn_tm[:, :], AX.X, ALU.max,
                                    apply_absolute_value=True))
                    schain(e, en, e.tensor_scalar(camax[:, t:t + 1], ctmp[:, t:t + 1], cr2[:, t:t + 1], 1e-5,
                                    ALU.mult, ALU.max))
                    schain(e, en, e.reciprocal(ctmp[:, t:t + 1], camax[:, t:t + 1]))
                    schain(e, en, e.scalar_tensor_tensor(cr2s1[:, t:t + 1], ctmp[:, t:t + 1], cr2[:, t:t + 1],
                                           acol[:, 0:1], ALU.mult, ALU.add))
                    schain(e, en, e.tensor_scalar(cr2s1[:, t:t + 1], cr2s1[:, t:t + 1], 127.0, None, ALU.mult))
                    schain(e, en, e.tensor_scalar(sq_scr[:, :], xn_tm[:, :], cr2s1[:, t:t + 1], None, ALU.mult))
                    schain(e, en, e.tensor_scalar(sq_scr[:, :], sq_scr[:, :], MAGICR, MAGICR, ALU.add, ALU.subtract))
                    schain(e, en, e.tensor_copy(xq_tm_i[:, :], sq_scr[:, :]))
                    if t >= 1:
                        wait(e, en, s_tr0, 4 * t)
                    e.tensor_copy(xq_tm_b[:, :], xq_tm_i[:, :])
                    instr = e.tensor_copy(xn_tm_b[:, :], xn_tm[:, :])
                    s_x2.inc(instr, 2)
                    wait(e, en, s_tr0, 4 * t + 2)
                    e.tensor_copy(
                        bass.AP(xn_fm, t * 128, [[KD * L, 128], [L, 8], [1, 128]]),
                        psA[:, :].rearrange("p (k c) -> p k c", k=8))
                    instr = e.tensor_copy(
                        bass.AP(xn_fm, 8 * L + t * 128, [[KD * L, 128], [L, 8], [1, 128]]),
                        psB[:, :].rearrange("p (k c) -> p k c", k=8))
                    s_ev0.inc(instr)
                    wait(e, en, s_tr0, 4 * t + 4)
                    if t >= 1:
                        wait(e, en, s_xq1w, 16 * t)
                    e.tensor_copy(trst[:, 0:1024], psA[:, :])
                    instr = e.tensor_copy(trst[:, 1024:2048], psB[:, :])
                    s_ev0.inc(instr)

            @on("scalar")
            def _(e, en):
                for t in range(NT):
                    ppwait(e, en, s_u, t)
                    wait(e, en, sem("p0_sqs"), t)
                    instr = e.activation(sq_scr[:, :], u_tl[t % 2][:, :], AF.Square,
                                         accum_out=csq[:, t:t + 1])
                    sem("p0_sqs").inc(instr)
                wait(e, en, s_c0, 1)
                instr = e.activation(ctmp[:, :], csq[:, :], AF.Sqrt)
                s_sq0.inc(instr)
                wait(e, en, s_c0, 2)
                instr = e.activation(ctmp[:, :], cr2[:, :], AF.Sqrt)
                s_sq0.inc(instr)
                for t in range(NT):
                    wait(e, en, s_ev0, 2 * t + 2)
                    wait(e, en, s_xq1w, 16 * t)
                    instr = e.dma_start(
                        out=xq1_sp.rearrange("k p t -> p k t")[:, :, t * 128:(t + 1) * 128],
                        in_=trst[:, :].rearrange("p (k c) -> p k c", k=16))
                    s_xq1w.dinc(instr)

            @on("tensor")
            def _(e, en):
                for t in range(NT):
                    wait(e, en, s_x2, 2 * t + 2)
                    if t >= 1:
                        wait(e, en, s_ev0, 2 * t)
                    for half, tgt in ((0, psA), (1, psB)):
                        for j in range(8):
                            k = half * 8 + j
                            instr = e.transpose(tgt[:, j * 128:(j + 1) * 128],
                                                xn_tm_b[:, k * 128:(k + 1) * 128], identb_sb[:, :])
                        s_tr0.inc(instr)
                    wait(e, en, s_ev0, 2 * t + 1)
                    for half, tgt in ((0, psA), (1, psB)):
                        for j in range(8):
                            k = half * 8 + j
                            instr = e.transpose(tgt[:, j * 128:(j + 1) * 128],
                                                xq_tm_b[:, k * 128:(k + 1) * 128], identb_sb[:, :])
                        s_tr0.inc(instr)

            barrier("p0", extra=[("scalar", s_xq1w, 16 * NT)])
            sc0i.close()

            phase_mark(2)
            with contextlib.ExitStack() as sc1:
                sc1m = contextlib.ExitStack()
                E1 = sc1m.enter_context
                wb = E1(nc.sbuf_tensor("wb", [128, KD * 128], BF16))
                convbuf = E1(nc.sbuf_tensor("convbuf", [128, 2051], F32))
                convout = E1(nc.sbuf_tensor("convout", [128, DM], F32))
                xfm_t = [E1(nc.sbuf_tensor(f"xfm_t{i}", [128, DM], BF16)) for i in range(2)]
                convw_sb = E1(nc.sbuf_tensor("convw_sb", [128, NOT_XBC * 4], F32))
                convb_sb = E1(nc.sbuf_tensor("convb_sb", [128, NOT_XBC], F32))
                dtraw = E1(nc.sbuf_tensor("dtraw", [64, L], F32))
                scrA = sq_scr
                zst = E1(nc.sbuf_tensor("zst", [128, DM], BF16))

                s_wld = sem("p1_wld")
                s_mm1 = sem("p1_mm")
                s_ev1 = sem("p1_ev")
                s_zw = sem("p1_zw")
                s_cv = sem("p1_cv")
                s_cvs = sem("p1_cvs")
                s_xw = ppsem("p1_xw")
                s_xtw = sem("p1_xtw")
                s_tr1 = sem("p1_tr")
                s_te1 = sem("p1_te")
                s_ps1 = ppsem("p1_ps")
                s_pa1 = sem("p1_pa")
                s_dt = sem("p1_dt")
                s_dtt = sem("p1_dtt")
                s_att = sem("p1_att")
                s_ate = sem("p1_ate")
                s_sw = sem("p1_sw")
                NSLAB = 128

                @on("sync")
                def _(e, en):
                    if "nocp" not in _SKIP:
                        sem("p1_cb").dinc(e.dma_start(out=convw_sb[:, :], in_=convw))
                        sem("p1_cb").dinc(e.dma_start(out=convb_sb[:, :], in_=convb))
                    for i in range(NSLAB if "prescan" not in _SKIP else 0):
                        if i >= 2:
                            wait(e, en, s_pa1, i - 1)
                            wait(e, en, s_ps1[i % 2], 16 * (i // 2))
                        if i < 64:
                            src = w1T[i]
                        else:
                            j = i - 64
                            src = w2T[j // 4][:, (j % 4) * 16:(j % 4 + 1) * 16]
                        instr = e.dma_start(out=wf[i % 2][:, :].rearrange("p (k c) -> p k c", k=KD),
                                            in_=src)
                        ppissue(e, en, s_ps1, i, instr)

                if "nogpcast" not in _SKIP:
                    @on("gpsimd")
                    def _(e, en):
                        for ot in range(NOT_IN if "noload" not in _SKIP else 0):
                            if ot >= 1 and "nomm" not in _SKIP:
                                wait(e, en, s_mm1, 4 * ot)
                            instr = e.dma_start(out=wb[:, :].rearrange("p (k c) -> p k c", k=KD),
                                                in_=winT[ot])
                            s_wld.dinc(instr)
                else:
                    @on("gpsimd")
                    def _(e, en):
                        for ot in range(NOT_IN if "noload" not in _SKIP else 0):
                            if ot >= 1 and "nomm" not in _SKIP:
                                wait(e, en, sem("p1_wcst"), ot)
                            instr = e.dma_start(out=sq_scr[:, :].rearrange("p (k c) -> p k c", k=KD),
                                                in_=winT[ot])
                            s_wld.dinc(instr)

                    @on("vector")
                    def _(e, en):
                        for ot in range(NOT_IN if "nomm" not in _SKIP else 0):
                            wait(e, en, s_wld, 16 * (ot + 1))
                            if ot >= 1:
                                wait(e, en, s_mm1, 4 * ot)
                            instr = e.tensor_copy(wb[:, :], sq_scr[:, :])
                            sem("p1_wcst").inc(instr)

                @on("vector")
                def _(e, en):
                    for i in range(NSLAB if "prescan" not in _SKIP else 0):
                        ppwait(e, en, s_ps1, i)
                        col = 0 if i < 64 else 1
                        wait(e, en, s_pa1, i)
                        schain(e, en, e.tensor_reduce(acol[:, 2:3], wf[i % 2][:, :], AX.X, ALU.add,
                                        apply_absolute_value=True))
                        instr = e.tensor_tensor(absacc[:, col:col + 1], absacc[:, col:col + 1],
                                                acol[:, 2:3], ALU.add)
                        s_pa1.inc(instr)

                def pe_xpose(e, en, o):
                    wait(e, en, s_cvs, o + 1)
                    if o >= 1:
                        wait(e, en, s_te1, o)
                    for half, tgt in ((0, psA), (1, psB)):
                        for j in range(8):
                            k = half * 8 + j
                            instr = e.transpose(tgt[:, j * 128:(j + 1) * 128],
                                                xfm_t[o % 2][:, k * 128:(k + 1) * 128],
                                                identb_sb[:, :])
                        s_tr1.inc(instr)

                @on("tensor")
                def _(e, en):
                    for ot in range(NOT_IN if "nomm" not in _SKIP else 0):
                        wait(e, en, s_wld, 16 * (ot + 1))
                        for tg in range(4):
                            if ot >= 1:
                                wait(e, en, s_ev1, 4 * (ot - 1) + tg + 1)
                            for kt in range(KD):
                                instr = e.matmul(ps[tg][:, :],
                                                 wb[:, kt * 128:(kt + 1) * 128],
                                                 bass.AP(xn_fm, kt * L + tg * 512,
                                                         [[KD * L, 128], [1, 512]]),
                                                 start=(kt == 0), stop=(kt == KD - 1))
                            s_mm1.inc(instr)
                        if "conv" not in _SKIP and "nomm" not in _SKIP and ot >= 34:
                            pe_xpose(e, en, ot - 34)

                @on("scalar")
                def _(e, en):
                    for ot in range(NOT_Z if "nomm" not in _SKIP else 0):
                        for tg in range(4):
                            wait(e, en, s_mm1, 4 * ot + tg + 1)
                            if ot >= 1 and "zspill" not in _SKIP:
                                wait(e, en, s_zw, 16 * ot)
                            instr = e.activation(zst[:, tg * 512:(tg + 1) * 512], ps[tg][:, :], AF.Silu)
                            s_ev1.inc(instr)
                        if "zspill" not in _SKIP:
                            wait(e, en, s_ev1, 4 * (ot + 1))
                            instr = e.dma_start(out=z_sp[ot], in_=zst[:, :])
                            s_zw.dinc(instr)
                    for o in range(NOT_XBC if "conv" not in _SKIP else 0):
                        wait(e, en, s_cv, o + 1)
                        if o >= 2:
                            wait(e, en, s_te1, o - 1)
                            ppwait(e, en, s_xw, o - 2)
                        instr = e.activation(xfm_t[o % 2][:, :], convout[:, :], AF.Silu,
                                             bias=convb_sb[:, o:o + 1])
                        s_cvs.inc(instr)
                        if o < 32:
                            wait(e, en, s_cvs, o + 1)
                            if o >= 2:
                                wait(e, en, s_xw[o % 2], 16 * (o // 2))
                            instr = e.dma_start(out=xfm_sp[o], in_=xfm_t[o % 2][:, :])
                            ppissue(e, en, s_xw, o, instr)
                            wait(e, en, s_te1, o + 1)
                            wait(e, en, s_xtw, 16 * o)
                            instr = e.dma_start(out=xtm_sp[o], in_=trst[:, :])
                            s_xtw.dinc(instr)

                def v_tb_evac(e, en, o):
                    if o < 32:
                        wait(e, en, s_tr1, 2 * o + 2)
                        if o >= 1:
                            wait(e, en, s_xtw, 16 * o)
                        e.tensor_copy(trst[:, 0:1024], psA[:, :])
                        instr = e.tensor_copy(trst[:, 1024:2048], psB[:, :])
                        s_te1.inc(instr)
                    elif o == 32:
                        e.tensor_copy(B_fm[:, :], xfm_t[o % 2][:, :])
                        wait(e, en, s_tr1, 2 * 32 + 2)
                        e.tensor_copy(B_tm[:, 0:1024], psA[:, :])
                        instr = e.tensor_copy(B_tm[:, 1024:2048], psB[:, :])
                        s_te1.inc(instr)
                    else:
                        wait(e, en, s_cvs, 34)
                        instr = e.tensor_copy(C_fm[:, :], xfm_t[o % 2][:, :])
                        s_te1.inc(instr)

                @on("vector")
                def _(e, en):
                    schain(e, en, e.memset(convbuf[:, 0:3], 0.0))
                    for ot in range(NOT_Z, NOT_IN if "nomm" not in _SKIP else NOT_Z):
                        o = ot - NOT_Z
                        if ot == NOT_Z:
                            wait(e, en, s_ev1, 4 * NOT_Z)
                        if o >= 1:
                            wait(e, en, s_cv, o)
                        for tg in range(4):
                            wait(e, en, s_mm1, 4 * ot + tg + 1)
                            if ot < NOT_Z + NOT_XBC:
                                instr = e.tensor_copy(convbuf[:, 3 + tg * 512:3 + (tg + 1) * 512],
                                                      ps[tg][:, :])
                            else:
                                instr = e.tensor_copy(dtraw[:, tg * 512:(tg + 1) * 512], ps[tg][0:64, :])
                            s_ev1.inc(instr)
                        if ot < NOT_Z + NOT_XBC and "conv" not in _SKIP:
                            wait(e, en, sem("p1_cb"), 32)
                            wait(e, en, s_ev1, 4 * (ot + 1))
                            if o >= 1:
                                wait(e, en, s_cvs, o)
                            schain(e, en, e.tensor_scalar(convout[:, :], convbuf[:, 0:DM],
                                            convw_sb[:, o * 4:o * 4 + 1], None, ALU.mult))
                            for k in range(1, 4):
                                instr = e.scalar_tensor_tensor(convout[:, :], convbuf[:, k:k + DM],
                                                               convw_sb[:, o * 4 + k:o * 4 + k + 1],
                                                               convout[:, :], ALU.mult, ALU.add)
                                if k < 3:
                                    schain(e, en, instr)
                            s_cv.inc(instr)
                        if "conv" not in _SKIP and o >= 1 and (o - 1) <= 33:
                            v_tb_evac(e, en, o - 1)

                @on("scalar")
                def _(e, en):
                    if "dt" in _SKIP:
                        return
                    wait(e, en, s_ev1, 4 * NOT_IN)
                    schain(e, en, e.activation(scrA[0:64, :], dtraw[:, :], AF.Abs, bias=pcol[:, 1:2]))
                    schain(e, en, e.activation(scrA[0:64, :], scrA[0:64, :], AF.Exp, scale=-1.0))
                    schain(e, en, e.activation(scrA[0:64, :], scrA[0:64, :], AF.Ln, bias=1.0))
                    instr = e.activation(dt_fm[:, :], dtraw[:, :], AF.Relu, bias=pcol[:, 1:2])
                    s_dt.inc(instr)

                @on("vector")
                def _(e, en):
                    if "dt" in _SKIP:
                        return
                    wait(e, en, s_dt, 1)
                    schain(e, en, e.tensor_tensor(dt_fm[:, :], dt_fm[:, :], scrA[0:64, :], ALU.add))
                    wait(e, en, s_prep, 3)
                    schain(e, en, e.tensor_scalar(scrA[0:64, :], dt_fm[:, :], pcol[:, 2:3], None, ALU.mult))
                    for c in range(NCH):
                        instr = e.tensor_tensor_scan(acum_fm[:, c * Q:(c + 1) * Q], scrA[0:64, c * Q:(c + 1) * Q],
                                             scrA[0:64, c * Q:(c + 1) * Q], 0.0, ALU.add, ALU.bypass)
                    schain(e, en, instr)
                    schain(e, en, e.tensor_copy(acum_hi[:, :], acum_fm[:, :]))
                    schain(e, en, e.tensor_copy(scrA[0:64, :], acum_hi[:, :]))
                    instr = e.tensor_tensor(acum_lo[:, :], acum_fm[:, :], scrA[0:64, :], ALU.subtract)
                    s_dt.inc(instr)

                @on("tensor")
                def _(e, en):
                    if "dt" in _SKIP:
                        return
                    wait(e, en, s_dt, 2)
                    for i2 in range(2 * NT):
                        lt = i2 % NT
                        src = dt_fm if i2 < NT else acum_fm
                        bank = ps[4] if (i2 % 2 == 0) else ps[5]
                        if i2 >= 2:
                            wait(e, en, s_ate, i2 - 1)
                        instr = e.transpose(bank[:, 0:64], src[:, lt * 128:(lt + 1) * 128],
                                            identf_sb[0:64, 0:64])
                        s_att.inc(instr)

                @on("vector")
                def _(e, en):
                    if "dt" in _SKIP:
                        return
                    for i2 in range(2 * NT):
                        lt = i2 % NT
                        wait(e, en, s_att, i2 + 1)
                        bank = ps[4] if (i2 % 2 == 0) else ps[5]
                        if i2 < NT:
                            instr = e.tensor_copy(dt_tm[:, lt * 64:(lt + 1) * 64], bank[:, 0:64])
                        else:
                            e.tensor_copy(acum_tm[:, lt * 64:(lt + 1) * 64], bank[:, 0:64])
                            instr = e.tensor_scalar(nacum_tm[:, lt * 64:(lt + 1) * 64], bank[:, 0:64],
                                                    -1.0, None, ALU.mult)
                        s_ate.inc(instr)

                @on("tensor")
                def _(e, en):
                    if "prescan" in _SKIP or "dt" in _SKIP:
                        return
                    wait(e, en, s_pa1, NSLAB)
                    wait(e, en, s_ate, 2 * NT)
                    instr = e.transpose(ps[4][0:2, 64:192], absacc[:, 0:2], identf_sb[:, :])
                    s_sw.inc(instr)

                @on("vector")
                def _(e, en):
                    if "prescan" in _SKIP or "dt" in _SKIP:
                        return
                    wait(e, en, s_sw, 1)
                    schain(e, en, e.tensor_reduce(acol[0:2, 3:4], ps[4][0:2, 64:192], AX.X, ALU.add))
                    schain(e, en, e.tensor_scalar(acol[0:2, 3:4], acol[0:2, 3:4], 1.0 / (FF * DM), 1e-5,
                                    ALU.mult, ALU.max))
                    instr = e.reciprocal(acol[0:2, 3:4], acol[0:2, 3:4])
                    s_sw.inc(instr)

                @on("tensor")
                def _(e, en):
                    if "prescan" in _SKIP or "dt" in _SKIP:
                        return
                    wait(e, en, s_sw, 2)
                    e.matmul(ps[4][:, 66:67], sel2[:, 0:128], acol[0:2, 3:4], start=True, stop=False,
                             skip_group_check=True)
                    instr = e.matmul(ps[4][:, 67:68], sel2[:, 128:256], acol[0:2, 3:4],
                                     start=False, stop=True, skip_group_check=True)
                    s_sw.inc(instr)

                @on("vector")
                def _(e, en):
                    if "prescan" in _SKIP or "dt" in _SKIP:
                        return
                    wait(e, en, s_sw, 3)
                    schain(e, en, e.tensor_copy(swc[:, 0:1], ps[4][:, 66:67]))
                    schain(e, en, e.tensor_copy(swc[:, 1:2], ps[4][:, 67:68]))
                    schain(e, en, e.tensor_scalar(swc[:, 2:3], swc[:, 0:1], 127.0, None, ALU.mult))
                    schain(e, en, e.tensor_scalar(swc[:, 3:4], swc[:, 1:2], 127.0, None, ALU.mult))
                    schain(e, en, e.reciprocal(swc[:, 2:3], swc[:, 2:3]))
                    instr = e.reciprocal(swc[:, 3:4], swc[:, 3:4])
                    s_sw.inc(instr)

                barrier("p1", extra=(([("scalar", s_xw[0], 16 * 16), ("scalar", s_xw[1], 16 * 16),
                                      ("scalar", s_xtw, 16 * 32)]) if "conv" not in _SKIP else []) +
                        ([("scalar", s_zw, 16 * NOT_Z)] if "zspill" not in _SKIP else []))
                sc1m.close()
                scx.close()

                phase_mark(3)
                with contextlib.ExitStack() as sc2:
                    E2 = sc2.enter_context
                    xtm_ch = E2(nc.sbuf_tensor("xtm_ch", [128, 8192], BF16))
                    xfm_ch = E2(nc.sbuf_tensor("xfm_ch", [128, 8192], BF16))
                    zs_ch = E2(nc.sbuf_tensor("zs_ch", [128, 8192], BF16))
                    yg_sb = E2(nc.sbuf_tensor("yg_sb", [128, 8192], BF16))
                    y_sb = E2(nc.sbuf_tensor("y_sb", [128, 8192], BF16))
                    _achr_hi = E2(nc.sbuf_tensor("achr_hi", [1, 2048], BF16))
                    _achr_lo = E2(nc.sbuf_tensor("achr_lo", [1, 2048], BF16))
                    achr_hi = [_achr_hi, _achr_hi]
                    achr_lo = [_achr_lo, _achr_lo]
                    cbt = E2(nc.sbuf_tensor("cbt", [128, 512], F32))
                    scombs = [E2(nc.sbuf_tensor(f"scomb{i}", [128, 512], F32)) for i in range(2)]
                    esdo = E2(nc.sbuf_tensor("esdo", [128, 256], BF16))
                    arep_sb = E2(nc.sbuf_tensor("arep_sb", [128, 256], F32))
                    cbm = E2(nc.sbuf_tensor("cbm", [128, 512], BF16))
                    cbmt = E2(nc.sbuf_tensor("cbmt", [128, 512], BF16))
                    csdo = E2(nc.sbuf_tensor("csdo", [128, 256], BF16))
                    bdd = E2(nc.sbuf_tensor("bdd", [128, 256], BF16))
                    st_f = E2(nc.sbuf_tensor("st_f", [128, H * 64], F32))
                    st_hb = E2(nc.sbuf_tensor("st_hb", [128, 64], BF16))
                    etrep = E2(nc.sbuf_tensor("etrep", [128, 64], F32))
                    rrow = E2(nc.sbuf_tensor("rrow", [1, Q], F32))
                    rrow_b = E2(nc.sbuf_tensor("rrow_b", [1, Q], BF16))
                    rstd_rep = E2(nc.sbuf_tensor("rstd_rep", [128, Q], F32))
                    ygt = E2(nc.sbuf_tensor("ygt", [128, 256], F32))
                    et_hi = E2(nc.sbuf_tensor("et_hi", [1, 64], BF16))
                    et_lo = E2(nc.sbuf_tensor("et_lo", [1, 64], BF16))
                    lndt_ch = E2(nc.sbuf_tensor("lndt_ch", [128, 128], F32))

                    s_ch = sem("p2_ch")
                    s_row2 = sem("p2_row")
                    s_cb = sem("p2_cb")
                    s_cbe = sem("p2_cbe")
                    s_et = sem("p2_et")
                    s_dd = sem("p2_dd")
                    s_ddt = sem("p2_ddt")
                    s_dde = sem("p2_dde")
                    s_rep = sem("p2_rep")
                    s_cmb = sem("p2_cmb")
                    s_exp = sem("p2_exp")
                    s_cbml = sem("p2_cbm")
                    s_bddp = sem("p2_bdd")
                    s_csd = sem("p2_csd")
                    s_ymm = sem("p2_ymm")
                    s_smm = sem("p2_smm")
                    s_yev = sem("p2_yev")
                    s_stb = sem("p2_stb")
                    s_rec = sem("p2_rec")
                    s_y2 = sem("p2_y2")
                    s_rs = sem("p2_rs")
                    s_rr = sem("p2_rr")
                    s_gt = sem("p2_gt")
                    s_ygw = sem("p2_ygw")

                    @on("vector")
                    def _(e, en):
                        e.memset(st_f[:, :], 0.0)

                    def achr_load(e, en, c, g):
                        if c * 8 + g >= 1:
                            wait(e, en, s_exp, 16 * (c * 8 + g))
                        s_row2.dinc(e.dma_start(
                            out=achr_hi[g % 2][0:1, :].rearrange("o (h q) -> o h q", h=8),
                            in_=acum_hi[g * 8:(g + 1) * 8, c * Q:(c + 1) * Q]))
                        s_row2.dinc(e.dma_start(
                            out=achr_lo[g % 2][0:1, :].rearrange("o (h q) -> o h q", h=8),
                            in_=acum_lo[g * 8:(g + 1) * 8, c * Q:(c + 1) * Q]))

                    def gp_chunk(e, en, c):
                        if c >= 1:
                            wait(e, en, s_ymm, c * H)
                            wait(e, en, s_yev, c * H)
                            wait(e, en, s_ygw, 16 * c)
                            wait(e, en, s_gt, c)
                        for ot in range(32):
                            s_ch.dinc(e.dma_start(out=bass.AP(xtm_ch, ot * 128,
                                                  [[8192, 128], [4096, 2], [1, 128]]),
                                      in_=bass.AP(xtm_sp.tensor, ot * 128 * 2048 + 2 * c * 128,
                                                  [[2048, 128], [128, 2], [1, 128]])))
                            s_ch.dinc(e.dma_start(out=xfm_ch[:, ot * 256:(ot + 1) * 256],
                                      in_=xfm_sp[ot][:, c * Q:(c + 1) * Q]))
                            s_ch.dinc(e.dma_start(out=zs_ch[:, ot * 256:(ot + 1) * 256],
                                      in_=z_sp[ot][:, c * Q:(c + 1) * Q]))
                        for g in range(8):
                            achr_load(e, en, c, g)
                        wait(e, en, s_yev, (c + 1) * H)
                        instr = e.tensor_tensor(yg_sb[:, :], y_sb[:, :], y_sb[:, :], ALU.mult)
                        s_y2.inc(instr)

                    def pe_chunk(e, en, c):
                        if c >= 1:
                            wait(e, en, s_cbe, c)
                        for sblk in range(2):
                            instr = e.matmul(ps[0][:, sblk * 256:(sblk + 1) * 256],
                                             B_fm[:, c * Q + sblk * 128:c * Q + (sblk + 1) * 128],
                                             C_fm[:, c * Q:(c + 1) * Q], start=True, stop=True,
                                             skip_group_check=True)
                        s_cb.inc(instr)
                        wait(e, en, s_et, c)
                        instr = e.transpose(ps[1][0:1, 64:128],
                                            acum_fm[:, c * Q + Q - 1:c * Q + Q],
                                            identf_sb[0:64, 0:64])
                        s_cb.inc(instr)
                        wait(e, en, sem("p2_ehl"), c + 1)
                        e.matmul(ps[1][:, 0:64], ones_r_b[0:1, :], et_hi[0:1, :],
                                 start=True, stop=False, skip_group_check=True)
                        instr = e.matmul(ps[1][:, 0:64], ones_r_b[0:1, :], et_lo[0:1, :],
                                         start=False, stop=True, skip_group_check=True)
                        s_cb.inc(instr)
                        wait(e, en, s_dd, 2 * c + 2)
                        if c >= 1:
                            wait(e, en, s_dde, c)
                            wait(e, en, s_gt, c)
                        for j in range(2):
                            instr = e.transpose(ps[2][:, j * 64:(j + 1) * 64],
                                                dtdec_fm[:, j * 128:(j + 1) * 128],
                                                identf_sb[0:64, 0:64])
                        s_ddt.inc(instr)
                        def pe_rep(h):
                            hd = c * H + h
                            g = h // 8
                            hg = h % 8
                            rbank = ps[3] if h % 2 == 0 else ps[0]
                            if hd >= 2:
                                wait(e, en, s_cmb, hd - 1)
                            if h == 1:
                                wait(e, en, s_cbe, c + 1)
                            wait(e, en, s_row2, (c * 8 + g + 1) * 32)
                            e.matmul(rbank[:, 0:256], ones_r_b[0:1, :],
                                     achr_hi[g % 2][0:1, hg * Q:(hg + 1) * Q],
                                     start=True, stop=False, skip_group_check=True)
                            instr = e.matmul(rbank[:, 0:256], ones_r_b[0:1, :],
                                             achr_lo[g % 2][0:1, hg * Q:(hg + 1) * Q],
                                             start=False, stop=True, skip_group_check=True)
                            s_rep.inc(instr)

                        def pe_ydiag(h):
                            hd = c * H + h
                            wait(e, en, s_ch, (c + 1) * 16 * 96)
                            wait(e, en, s_cbml, hd + 1)
                            wait(e, en, s_csd, hd + 1)
                            wait(e, en, s_stb, hd + 1)
                            wait(e, en, s_bddp, hd + 1)
                            if hd >= 1:
                                wait(e, en, s_yev, hd)
                            e.matmul(ps[5][0:64, 0:256], xtm_ch[:, h * 64:(h + 1) * 64],
                                     cbmt[:, 0:256], start=True, stop=False)
                            e.matmul(ps[5][0:64, 0:256],
                                     xtm_ch[:, 4096 + h * 64:4096 + (h + 1) * 64],
                                     cbmt[:, 256:512], start=False, stop=False)
                            instr = e.matmul(ps[5][0:64, 0:256], st_hb[:, :], csdo[:, :],
                                             start=False, stop=True)
                            s_ymm.inc(instr)
                            if hd >= 1:
                                wait(e, en, s_rec, hd)
                            e.matmul(ps[4][:, 0:64], bdd[:, 0:128],
                                     xtm_ch[:, h * 64:(h + 1) * 64], start=True, stop=False)
                            instr = e.matmul(ps[4][:, 0:64], bdd[:, 128:256],
                                             xtm_ch[:, 4096 + h * 64:4096 + (h + 1) * 64],
                                             start=False, stop=True)
                            s_smm.inc(instr)

                        for h in range(H):
                            pe_rep(h)
                            if h >= 1:
                                pe_ydiag(h - 1)
                        pe_ydiag(H - 1)
                        wait(e, en, s_y2, c + 1)
                        if c >= 1:
                            wait(e, en, s_rs, 2 * c)
                        for ob in range(32):
                            instr = e.matmul(ps[2][0:1, 0:256], ones_c_b[:, 0:1],
                                             yg_sb[:, ob * 256:(ob + 1) * 256],
                                             start=(ob == 0), stop=(ob == 31),
                                             skip_group_check=True)
                        s_rs.inc(instr)
                        wait(e, en, s_rr, 3 * c + 3)
                        instr = e.matmul(ps[2][:, 256:512], ones_r_b[0:1, :], rrow_b[0:1, :],
                                         start=True, stop=True, skip_group_check=True)
                        s_rs.inc(instr)

                    def v_chunk(e, en, c):
                        wait(e, en, s_cb, 3 * c + 1)
                        instr = e.tensor_copy(cbt[:, :], ps[0][:, :])
                        s_cbe.inc(instr)
                        wait(e, en, s_cb, 3 * c + 2)
                        schain(e, en, e.tensor_copy(et_hi[0:1, :], ps[1][0:1, 64:128]))
                        instr = e.tensor_tensor(et_lo[0:1, :], ps[1][0:1, 64:128],
                                                et_hi[0:1, :], ALU.subtract)
                        sem("p2_ehl").inc(instr)
                        wait(e, en, s_dd, 2 * c + 1)
                        instr = e.tensor_tensor(dtdec_fm[:, :], dtdec_fm[:, :],
                                                dt_fm[:, c * Q:(c + 1) * Q], ALU.mult)
                        s_dd.inc(instr)
                        wait(e, en, s_ddt, c + 1)
                        instr = e.tensor_copy(dtdec_tm[:, :], ps[2][:, 0:128])
                        s_dde.inc(instr)
                        def v_a(h):
                            hd = c * H + h
                            rbank = ps[3] if h % 2 == 0 else ps[0]
                            wait(e, en, s_rep, hd + 1)
                            if hd >= 2:
                                wait(e, en, s_exp, 2 * (hd - 1))
                            for j in range(2):
                                schain(e, en, e.scalar_tensor_tensor(
                                    scombs[h % 2][:, j * 256:(j + 1) * 256], rbank[:, 0:256],
                                    nacum_tm[:, (2 * c + j) * 64 + h:(2 * c + j) * 64 + h + 1],
                                    mask_sb[:, j * 256:(j + 1) * 256], ALU.add, ALU.add))
                            if hd >= 1:
                                wait(e, en, s_exp, 2 * hd)
                            instr = e.tensor_copy(arep_sb[:, :], rbank[:, 0:256])
                            s_cmb.inc(instr)

                        def v_b(h):
                            hd = c * H + h
                            wait(e, en, s_exp, 2 * hd + 1)
                            wait(e, en, s_cbe, c + 1)
                            if hd >= 1:
                                wait(e, en, s_ymm, hd)
                            instr = e.tensor_tensor(cbmt[:, :], cbm[:, :], cbt[:, :], ALU.mult)
                            s_cbml.inc(instr)
                            wait(e, en, s_exp, 2 * hd + 2)
                            instr = e.tensor_tensor(csdo[:, :], C_fm[:, c * Q:(c + 1) * Q],
                                                    esdo[:, :], ALU.mult)
                            s_csd.inc(instr)
                            instr = e.tensor_copy(st_hb[:, :], st_f[:, h * 64:(h + 1) * 64])
                            s_stb.inc(instr)
                            wait(e, en, s_dde, c + 1)
                            if hd >= 1:
                                wait(e, en, s_smm, hd)
                            for j in range(2):
                                instr = e.tensor_scalar(bdd[:, j * 128:(j + 1) * 128],
                                                        B_tm[:, (2 * c + j) * 128:(2 * c + j + 1) * 128],
                                                        dtdec_tm[:, j * 64 + h:j * 64 + h + 1],
                                                        None, ALU.mult)
                            s_bddp.inc(instr)

                        def v_c(h):
                            hd = c * H + h
                            wait(e, en, s_ymm, hd + 1)
                            ot = h // 2
                            half = h % 2
                            instr = e.scalar_tensor_tensor(
                                y_sb[half * 64:(half + 1) * 64, ot * 256:(ot + 1) * 256],
                                xfm_ch[half * 64:(half + 1) * 64, ot * 256:(ot + 1) * 256],
                                d_rep[half * 64:(half + 1) * 64, h:h + 1],
                                ps[5][0:64, 0:256], ALU.mult, ALU.add)
                            s_yev.inc(instr)
                            wait(e, en, s_smm, hd + 1)
                            wait(e, en, s_et, c + 1)
                            instr = e.scalar_tensor_tensor(st_f[:, h * 64:(h + 1) * 64],
                                                           st_f[:, h * 64:(h + 1) * 64],
                                                           etrep[:, h:h + 1], ps[4][:, 0:64],
                                                           ALU.mult, ALU.add)
                            s_rec.inc(instr)

                        for h in range(H):
                            v_a(h)
                            if h >= 1:
                                v_b(h - 1)
                            if h >= 2:
                                v_c(h - 2)
                        v_b(H - 1)
                        v_c(H - 2)
                        v_c(H - 1)
                        wait(e, en, s_rs, 2 * c + 1)
                        instr = e.tensor_scalar(rrow[0:1, :], ps[2][0:1, 0:256], 1.0 / DIN,
                                                RMS_EPS, ALU.mult, ALU.add)
                        s_rr.inc(instr)
                        wait(e, en, s_rr, 3 * c + 2)
                        schain(e, en, e.reciprocal(rrow[0:1, :], rrow[0:1, :]))
                        instr = e.tensor_copy(rrow_b[0:1, :], rrow[0:1, :])
                        s_rr.inc(instr)
                        wait(e, en, s_rs, 2 * c + 2)
                        schain(e, en, e.tensor_copy(rstd_rep[:, :], ps[2][:, 256:512]))
                        for ob in range(32):
                            schain(e, en, e.scalar_tensor_tensor(ygt[:, :], y_sb[:, ob * 256:(ob + 1) * 256],
                                                   normw_sb[:, ob:ob + 1], rstd_rep[:, :],
                                                   ALU.mult, ALU.mult))
                            instr = e.tensor_tensor(yg_sb[:, ob * 256:(ob + 1) * 256], ygt[:, :],
                                                    zs_ch[:, ob * 256:(ob + 1) * 256], ALU.mult)
                            if ob < 31:
                                schain(e, en, instr)
                        s_gt.inc(instr)

                    def act_chunk(e, en, c):
                        if c >= 1:
                            wait(e, en, s_exp, 2 * c * H)
                        instr = e.activation(lndt_ch[:, :], dt_tm[:, 2 * c * 64:(2 * c + 2) * 64], AF.Ln)
                        sem("p2_ldt").inc(instr)
                        wait(e, en, s_cb, 3 * c + 3)
                        instr = e.activation(etrep[:, :], ps[1][:, 0:64], AF.Exp,
                                             bias=acol[:, 0:1])
                        s_et.inc(instr)
                        instr = e.activation(dtdec_fm[:, :], acum_fm[:, c * Q:(c + 1) * Q], AF.Exp,
                                             scale=-1.0, bias=acum_fm[:, c * Q + Q - 1:c * Q + Q])
                        s_dd.inc(instr)
                        for h in range(H):
                            hd = c * H + h
                            wait(e, en, s_cmb, hd + 1)
                            wait(e, en, sem("p2_ldt"), c + 1)
                            if hd >= 1:
                                wait(e, en, s_cbml, hd)
                            for j in range(2):
                                instr = e.activation(cbm[:, j * 256:(j + 1) * 256],
                                                     scombs[h % 2][:, j * 256:(j + 1) * 256], AF.Exp,
                                                     bias=lndt_ch[:, j * 64 + h:j * 64 + h + 1])
                                if j == 0:
                                    schain(e, en, instr)
                            s_exp.inc(instr)
                            if hd >= 1:
                                wait(e, en, s_csd, hd)
                            instr = e.activation(esdo[:, :], arep_sb[:, :], AF.Exp,
                                                 bias=acol[:, 0:1])
                            s_exp.inc(instr)
                        wait(e, en, s_rr, 3 * c + 1)
                        instr = e.activation(rrow[0:1, :], rrow[0:1, :], AF.Sqrt)
                        s_rr.inc(instr)
                        wait(e, en, s_gt, c + 1)
                        instr = e.dma_start(out=yg_sp[c], in_=yg_sb[:, :])
                        s_ygw.dinc(instr)

                    @on("gpsimd")
                    def _(e, en):
                        for c in range(NCH):
                            gp_chunk(e, en, c)

                    @on("tensor")
                    def _(e, en):
                        for c in range(NCH):
                            pe_chunk(e, en, c)

                    @on("vector")
                    def _(e, en):
                        for c in range(NCH):
                            v_chunk(e, en, c)

                    @on("scalar")
                    def _(e, en):
                        for c in range(NCH):
                            act_chunk(e, en, c)

                    barrier("p2", extra=[("scalar", s_ygw, 16 * NCH)])
                scspan.close()

        phase_mark(4)
        with contextlib.ExitStack() as scb:
            Eb = scb.enter_context
            wout_h = Eb(nc.sbuf_tensor("wout_h", [128, 32 * 1024], BF16))
            yg_in = [Eb(nc.sbuf_tensor(f"yg_in{i}", [128, 8192], BF16)) for i in range(2)]
            att_st = Eb(nc.sbuf_tensor("att_st", [128, 1024], BF16))
            wq_i = Eb(nc.sbuf_tensor("wq_ib", [128, 2048], I32))
            u2_tl = [Eb(nc.sbuf_tensor(f"u2_tl{i}", [128, 1024], F32)) for i in range(2)]

            s_wo = sem("pb_wo")
            s_ygl = ppsem("pb_ygl")
            s_u2 = ppsem("pb_u2")
            s_mm2 = sem("pb_mm")
            s_ev2 = sem("pb_ev")
            s_aw = sem("pb_aw")
            s_w2l = ppsem("pb_w2l")
            s_w2q = sem("pb_w2q")
            s_w2w = ppsem("pb_w2w")

            @on("gpsimd")
            def _(e, en):
                for half in range(2):
                    if half == 1:
                        wait(e, en, s_mm2, 32)
                    instr = e.dma_start(
                        out=wout_h[:, :].rearrange("p (k c2) -> p k c2", k=32),
                        in_=woutT.rearrange("(k p) o -> p k o", p=128)[:, :, half * 1024:(half + 1) * 1024])
                    s_wo.dinc(instr)
                    for c in range(NCH):
                        hc = half * 8 + c
                        if hc >= 2:
                            wait(e, en, s_mm2, 4 * (hc - 1))
                            wait(e, en, s_ygl[hc % 2], 16 * (hc // 2))
                        instr = e.dma_start(out=yg_in[c % 2][:, :], in_=yg_sp[c])
                        ppissue(e, en, s_ygl, hc, instr)

            def aacc_spill(e, en, ht):
                wait(e, en, s_ev2, 2 * (ht + 1))
                instr = e.dma_start(out=aacc_sp[ht % NT][:, (ht // NT) * 1024:(ht // NT + 1) * 1024],
                                    in_=att_st[:, :])
                s_aw.dinc(instr)

            @on("scalar")
            def _(e, en):
                for half in range(2):
                    for t in range(NT):
                        ht = half * NT + t
                        if ht >= 2:
                            wait(e, en, s_ev2, 2 * (ht - 1))
                            wait(e, en, s_u2[ht % 2], 16 * (ht // 2))
                        instr = e.dma_start(out=u2_tl[t % 2][:, :],
                                            in_=u[t * 128:(t + 1) * 128, half * 1024:(half + 1) * 1024])
                        ppissue(e, en, s_u2, ht, instr)
                        if ht >= 1:
                            aacc_spill(e, en, ht - 1)
                aacc_spill(e, en, 2 * NT - 1)

            @on("tensor")
            def _(e, en):
                for half in range(2):
                    wait(e, en, s_wo, 16 * (half + 1))
                    for c in range(NCH):
                        hc = half * 8 + c
                        ppwait(e, en, s_ygl, hc)
                        for tt in range(2):
                            tgl = half * 16 + c * 2 + tt
                            for og in range(2):
                                if tgl >= 1:
                                    wait(e, en, s_ev2, 2 * (tgl - 1) + og + 1)
                                for kt in range(32):
                                    instr = e.matmul(
                                        ps[og][:, :],
                                        yg_in[c % 2][:, kt * 256 + tt * 128:kt * 256 + tt * 128 + 128],
                                        bass.AP(wout_h, kt * 1024 + og * 512, [[32 * 1024, 128], [1, 512]]),
                                        start=(kt == 0), stop=(kt == 31))
                                s_mm2.inc(instr)

            @on("vector")
            def _(e, en):
                for half in range(2):
                    for t in range(NT):
                        ht = half * NT + t
                        ppwait(e, en, s_u2, ht)
                        if ht >= 1:
                            wait(e, en, s_aw, 16 * ht)
                        for og in range(2):
                            wait(e, en, s_mm2, 2 * ht + og + 1)
                            instr = e.tensor_tensor(att_st[:, og * 512:(og + 1) * 512], ps[og][:, :],
                                                    u2_tl[t % 2][:, og * 512:(og + 1) * 512], ALU.add)
                            s_ev2.inc(instr)

            @on("sync")
            def _(e, en):
                for pidx in range(64):
                    if pidx >= 2:
                        wait(e, en, s_w2q, pidx - 1)
                        wait(e, en, s_w2l[pidx % 2], 16 * (pidx // 2))
                    instr = e.dma_start(out=wf[pidx % 2][:, :].rearrange("p (k c2) -> p k c2", k=KD),
                                        in_=w2T[pidx // 4][:, (pidx % 4) * 16:(pidx % 4 + 1) * 16])
                    ppissue(e, en, s_w2l, pidx, instr)

            @on("vector")
            def _(e, en):
                for pidx in range(64):
                    ppwait(e, en, s_w2l, pidx)
                    if pidx >= 2:
                        ppwait(e, en, s_w2w, pidx - 2)
                    schain(e, en, e.tensor_scalar(sq_scr[:, :], wf[pidx % 2][:, :], swc[:, 1:2], None, ALU.mult))
                    schain(e, en, e.tensor_scalar(sq_scr[:, :], sq_scr[:, :], MAGICR, MAGICR, ALU.add, ALU.subtract))
                    schain(e, en, e.tensor_copy(wq_i[:, :], sq_scr[:, :]))
                    instr = e.tensor_scalar(wq_b[pidx % 2][:, :], wq_i[:, :], 1.0, -1.0,
                                            ALU.min, ALU.max)
                    s_w2q.inc(instr)

            @on("gpsimd")
            def _(e, en):
                for pidx in range(64):
                    wait(e, en, s_w2q, pidx + 1)
                    if pidx >= 2:
                        wait(e, en, s_w2w[pidx % 2], 16 * (pidx // 2))
                    instr = e.dma_start(out=w2q_sp[pidx // 4][:, (pidx % 4) * 2048:(pidx % 4 + 1) * 2048],
                                        in_=wq_b[pidx % 2][:, :])
                    ppissue(e, en, s_w2w, pidx, instr)

            barrier("p2b", extra=[("scalar", s_aw, 16 * 2 * NT), ("gpsimd", s_w2w[0], 16 * 32),
                                  ("gpsimd", s_w2w[1], 16 * 32)])

        phase_mark(5)
        s_smt = sem("p3_smt")

        with contextlib.ExitStack() as sc3:
            E3 = sc3.enter_context
            xq1_fm = E3(nc.sbuf_tensor("xq1_fm", [128, KD * L], BF16))
            h_t = E3(nc.sbuf_tensor("h_t", [128, 2048], F32))
            wq_i = E3(nc.sbuf_tensor("wq_i3", [128, 2048], I32))
            h_b = [E3(nc.sbuf_tensor(f"h_b{i}", [128, 2048], BF16)) for i in range(2)]
            hsq = E3(nc.sbuf_tensor("hsq", [128, 2048], BF16))
            inv1_rep = E3(nc.sbuf_tensor("inv1_rep", [128, 2048], F32))
            rhf = E3(nc.sbuf_tensor("rhf", [1, 2048], F32))
            rha = E3(nc.sbuf_tensor("rha", [1, 2048], F32))
            rhb = E3(nc.sbuf_tensor("rhb", [1, 2048], BF16))
            rlb = E3(nc.sbuf_tensor("rlb", [1, 2048], BF16))

            s_xq1l = sem("p3_xq1l")
            s_w1l = ppsem("p3_w1l")
            s_w1q = sem("p3_w1q")
            s_iv = sem("p3_iv")
            s_ivr = sem("p3_ivr")
            s_mm3 = sem("p3_mm")
            s_hv = sem("p3_hv")
            s_hs = sem("p3_hs")
            s_hw = ppsem("p3_hw")
            s_hsq = sem("p3_hsq")

            @on("gpsimd")
            def _(e, en):
                instr = e.dma_start(out=xq1_fm[:, :].rearrange("p (k t) -> p k t", k=KD),
                                    in_=xq1_sp.rearrange("k p t -> p k t"))
                s_xq1l.dinc(instr)

            @on("sync")
            def _(e, en):
                for ot in range(NOT_F):
                    if ot >= 2:
                        wait(e, en, s_w1q, ot - 1)
                        wait(e, en, s_w1l[ot % 2], 16 * (ot // 2))
                    instr = e.dma_start(out=wf[ot % 2][:, :].rearrange("p (k c2) -> p k c2", k=KD),
                                        in_=w1T[ot])
                    ppissue(e, en, s_w1l, ot, instr)

            @on("vector")
            def _(e, en):
                instr = e.tensor_scalar(ctmp[:, :], camax[:, :], swc[:, 2:3], None, ALU.mult)
                s_iv.inc(instr)

            @on("tensor")
            def _(e, en):
                wait(e, en, s_iv, 1)
                instr = e.transpose(ps[4][0:16, 0:128], ctmp[:, 0:16], identf_sb[:, :])
                s_iv.inc(instr)

            @on("vector")
            def _(e, en):
                wait(e, en, s_iv, 2)
                instr = e.tensor_copy(sq_scr[0:16, 0:128], ps[4][0:16, 0:128])
                s_iv.inc(instr)

            @on("gpsimd")
            def _(e, en):
                wait(e, en, s_iv, 3)
                instr = e.dma_start(out=rhf[0:1, :].rearrange("o (t p) -> o t p", t=16),
                                    in_=sq_scr[0:16, 0:128])
                sem("p3_ivd").inc(instr, 16)

            @on("vector")
            def _(e, en):
                wait(e, en, sem("p3_ivd"), 16)
                schain(e, en, e.tensor_copy(rhb[0:1, :], rhf[0:1, :]))
                schain(e, en, e.tensor_copy(rha[0:1, :], rhb[0:1, :]))
                schain(e, en, e.tensor_tensor(rha[0:1, :], rhf[0:1, :], rha[0:1, :], ALU.subtract))
                instr = e.tensor_copy(rlb[0:1, :], rha[0:1, :])
                s_iv.inc(instr)

            @on("tensor")
            def _(e, en):
                wait(e, en, s_iv, 4)
                for tg in range(4):
                    if tg >= 1:
                        wait(e, en, s_ivr, tg)
                    e.matmul(ps[4][:, 0:512], ones_r_b[0:1, :], rhb[0:1, tg * 512:(tg + 1) * 512],
                             start=True, stop=False, skip_group_check=True)
                    instr = e.matmul(ps[4][:, 0:512], ones_r_b[0:1, :], rlb[0:1, tg * 512:(tg + 1) * 512],
                                     start=False, stop=True, skip_group_check=True)
                    s_iv.inc(instr)

            @on("vector")
            def _(e, en):
                for tg in range(4):
                    wait(e, en, s_iv, 5 + tg)
                    instr = e.tensor_copy(inv1_rep[:, tg * 512:(tg + 1) * 512], ps[4][:, 0:512])
                    s_ivr.inc(instr)

            def v_hevac(e, en, ot):
                if ot >= 1:
                    wait(e, en, s_hs, ot)
                for tg in range(4):
                    wait(e, en, s_mm3, 4 * ot + tg + 1)
                    wait(e, en, s_ivr, 4)
                    instr = e.scalar_tensor_tensor(h_t[:, tg * 512:(tg + 1) * 512], ps[tg][:, :],
                                                   acol[:, 0:1], inv1_rep[:, tg * 512:(tg + 1) * 512],
                                                   ALU.bypass, ALU.mult)
                    s_hv.inc(instr)

            @on("vector")
            def _(e, en):
                for ot in range(NOT_F):
                    ppwait(e, en, s_w1l, ot)
                    if ot >= 2:
                        wait(e, en, s_mm3, 4 * (ot - 1))
                    schain(e, en, e.tensor_scalar(sq_scr[:, :], wf[ot % 2][:, :], swc[:, 0:1], None, ALU.mult))
                    schain(e, en, e.tensor_scalar(sq_scr[:, :], sq_scr[:, :], MAGICR, MAGICR, ALU.add, ALU.subtract))
                    schain(e, en, e.tensor_copy(wq_i[:, :], sq_scr[:, :]))
                    instr = e.tensor_scalar(wq_b[ot % 2][:, :], wq_i[:, :], 1.0, -1.0, ALU.min, ALU.max)
                    s_w1q.inc(instr)
                    if ot >= 1:
                        v_hevac(e, en, ot - 1)
                v_hevac(e, en, NOT_F - 1)

            @on("tensor")
            def _(e, en):
                wait(e, en, s_xq1l, 16)
                for ot in range(NOT_F):
                    wait(e, en, s_w1q, ot + 1)
                    for tg in range(4):
                        if ot >= 1:
                            wait(e, en, s_hv, 4 * (ot - 1) + tg + 1)
                        for kt in range(KD):
                            instr = e.matmul(ps[tg][:, :],
                                             wq_b[ot % 2][:, kt * 128:(kt + 1) * 128],
                                             bass.AP(xq1_fm, kt * L + tg * 512, [[KD * L, 128], [1, 512]]),
                                             start=(kt == 0), stop=(kt == KD - 1))
                        s_mm3.inc(instr)
                    lag = ot - 2
                    if lag >= 0:
                        wait(e, en, s_hsq, lag + 1)
                        for tg in range(4):
                            e.matmul(ps[4][32 * tg:32 * tg + 1, 0:512], ones_c_b[:, 0:1],
                                     h_b[lag % 2][:, tg * 512:(tg + 1) * 512],
                                     start=(lag == 0), stop=(lag == NOT_F - 1),
                                     tile_position=(0, 32 * tg), skip_group_check=True)
                            instr = e.matmul(ps[5][32 * tg:32 * tg + 1, 0:512], ones_c_b[:, 0:1],
                                             hsq[:, tg * 512:(tg + 1) * 512],
                                             start=(lag == 0), stop=(lag == NOT_F - 1),
                                             tile_position=(0, 32 * tg), skip_group_check=True)
                        s_smt.inc(instr)
                for lag in (NOT_F - 2, NOT_F - 1):
                    wait(e, en, s_hsq, lag + 1)
                    for tg in range(4):
                        e.matmul(ps[4][32 * tg:32 * tg + 1, 0:512], ones_c_b[:, 0:1],
                                 h_b[lag % 2][:, tg * 512:(tg + 1) * 512],
                                 start=False, stop=(lag == NOT_F - 1),
                                 tile_position=(0, 32 * tg), skip_group_check=True)
                        instr = e.matmul(ps[5][32 * tg:32 * tg + 1, 0:512], ones_c_b[:, 0:1],
                                         hsq[:, tg * 512:(tg + 1) * 512],
                                         start=False, stop=(lag == NOT_F - 1),
                                         tile_position=(0, 32 * tg), skip_group_check=True)
                    s_smt.inc(instr)

            @on("scalar")
            def _(e, en):
                for ot in range(NOT_F):
                    wait(e, en, s_hv, 4 * (ot + 1))
                    if ot >= 2:
                        ppwait(e, en, s_hw, ot - 2)
                        wait(e, en, s_smt, ot - 1)
                    schain(e, en, e.activation(h_t[:, :], h_t[:, :], AF.Silu, bias=bcols[:, ot:ot + 1]))
                    instr = e.activation(h_b[ot % 2][:, :], h_t[:, :], AF.Gelu)
                    s_hs.inc(instr)
                    wait(e, en, s_hs, ot + 1)
                    instr = e.dma_start(out=h_sp[ot], in_=h_b[ot % 2][:, :])
                    ppissue(e, en, s_hw, ot, instr)

            @on("gpsimd")
            def _(e, en):
                for ot in range(NOT_F):
                    wait(e, en, s_hs, ot + 1)
                    if ot >= 1:
                        wait(e, en, s_smt, ot)
                    instr = e.tensor_tensor(hsq[:, :], h_b[ot % 2][:, :], h_b[ot % 2][:, :], ALU.mult)
                    s_hsq.inc(instr)

            barrier("p3a", extra=[("scalar", s_hw[0], 16 * NOT_F // 2),
                                  ("scalar", s_hw[1], 16 * NOT_F // 2)])

        phase_mark(6)
        row_c2h = E(nc.sbuf_tensor("row_c2h", [1, 2048], BF16))
        row_c2l = E(nc.sbuf_tensor("row_c2l", [1, 2048], BF16))
        row_iv2b = E(nc.sbuf_tensor("row_iv2b", [1, 2048], BF16))

        with contextlib.ExitStack() as sc4:
            E4 = sc4.enter_context
            lnh_t = [E4(nc.sbuf_tensor(f"lnh_t{i}", [128, 2048], BF16)) for i in range(2)]
            stg4 = E4(nc.sbuf_tensor("stg4", [128, 1024], F32))
            t_bf = E4(nc.sbuf_tensor("t_bf", [128, 2048], BF16))
            mu_rep = E4(nc.sbuf_tensor("mu_rep", [128, 2048], BF16))
            rs3_rep = E4(nc.sbuf_tensor("rs3_rep", [128, 2048], BF16))
            rmax = E4(nc.sbuf_tensor("rmax", [128, 2048], F32))
            hsq2 = E4(nc.sbuf_tensor("hsq2", [128, 2048], BF16))
            row_mu = E4(nc.sbuf_tensor("row_mu", [1, 2048], F32))
            row_v = E4(nc.sbuf_tensor("row_v", [1, 2048], F32))
            rhb2 = E4(nc.sbuf_tensor("rhb2", [1, 2048], BF16))
            rlb2 = E4(nc.sbuf_tensor("rlb2", [1, 2048], BF16))
            rsc = E4(nc.sbuf_tensor("rsc", [1, 2048], F32))

            s_r3 = sem("p4_r")
            s_mr = sem("p4_mr")
            s_hl = ppsem("p4_hl")
            s_ln = sem("p4_ln")
            s_lnw = ppsem("p4_lnw")
            s_sq2 = sem("p4_sq2")
            s_smt2 = sem("p4_smt2")
            s_s2e = sem("p4_s2e")
            s_q2 = sem("p4_q2")

            @on("vector")
            def _(e, en):
                wait(e, en, s_smt, NOT_F)
                for g in range(4):
                    schain(e, en, e.tensor_copy(stg4[32 * g:32 * g + 1, 0:512],
                                  ps[4][32 * g:32 * g + 1, 0:512]))
                    instr = e.tensor_copy(stg4[32 * g:32 * g + 1, 512:1024],
                                          ps[5][32 * g:32 * g + 1, 0:512])
                    if g < 3:
                        schain(e, en, instr)
                sem("p4_stg").inc(instr)
                wait(e, en, sem("p4_flt"), 32)
                schain(e, en, e.tensor_scalar(row_mu[0:1, :], row_mu[0:1, :], 1.0 / FF, None, ALU.mult))
                schain(e, en, e.tensor_scalar(row_v[0:1, :], row_v[0:1, :], 1.0 / FF, None, ALU.mult))
                schain(e, en, e.tensor_tensor(rsc[0:1, :], row_mu[0:1, :], row_mu[0:1, :], ALU.mult))
                schain(e, en, e.tensor_tensor(row_v[0:1, :], row_v[0:1, :], rsc[0:1, :], ALU.subtract))
                instr = e.tensor_scalar(row_v[0:1, :], row_v[0:1, :], 1.0, LN_EPS, ALU.mult, ALU.add)
                s_r3.inc(instr)

            @on("gpsimd")
            def _(e, en):
                wait(e, en, sem("p4_stg"), 1)
                sem("p4_flt").dinc(e.dma_start(out=row_mu[0:1, :].rearrange("o (g c) -> o g c", g=4),
                                   in_=bass.AP(stg4, 0, [[32 * 1024, 4], [1, 512]])))
                sem("p4_flt").dinc(e.dma_start(out=row_v[0:1, :].rearrange("o (g c) -> o g c", g=4),
                                   in_=bass.AP(stg4, 512, [[32 * 1024, 4], [1, 512]])))

            @on("scalar")
            def _(e, en):
                wait(e, en, s_r3, 1)
                instr = e.activation(row_v[0:1, :], row_v[0:1, :], AF.Sqrt)
                s_r3.inc(instr)

            @on("vector")
            def _(e, en):
                wait(e, en, s_r3, 2)
                schain(e, en, e.reciprocal(row_v[0:1, :], row_v[0:1, :]))
                schain(e, en, e.tensor_copy(rhb2[0:1, :], row_mu[0:1, :]))
                schain(e, en, e.tensor_copy(rsc[0:1, :], rhb2[0:1, :]))
                schain(e, en, e.tensor_tensor(rsc[0:1, :], row_mu[0:1, :], rsc[0:1, :], ALU.subtract))
                schain(e, en, e.tensor_copy(rlb2[0:1, :], rsc[0:1, :]))
                instr = e.nop()
                s_r3.inc(instr)

            @on("tensor")
            def _(e, en):
                wait(e, en, s_r3, 3)
                for tg in range(4):
                    if tg >= 1:
                        wait(e, en, s_mr, tg)
                    e.matmul(ps[4][:, 0:512], ones_r_b[0:1, :], rhb2[0:1, tg * 512:(tg + 1) * 512],
                             start=True, stop=False, skip_group_check=True)
                    instr = e.matmul(ps[4][:, 0:512], ones_r_b[0:1, :], rlb2[0:1, tg * 512:(tg + 1) * 512],
                                     start=False, stop=True, skip_group_check=True)
                    s_r3.inc(instr)

            @on("vector")
            def _(e, en):
                for tg in range(4):
                    wait(e, en, s_r3, 4 + tg)
                    instr = e.tensor_scalar(mu_rep[:, tg * 512:(tg + 1) * 512], ps[4][:, 0:512],
                                            -1.0, None, ALU.mult)
                    s_mr.inc(instr)
                instr = e.tensor_copy(rhb2[0:1, :], row_v[0:1, :])
                s_mr.inc(instr)

            @on("tensor")
            def _(e, en):
                wait(e, en, s_mr, 5)
                for tg in range(4):
                    if tg >= 1:
                        wait(e, en, s_mr, 5 + tg)
                    instr = e.matmul(ps[4][:, 0:512], ones_r_b[0:1, :], rhb2[0:1, tg * 512:(tg + 1) * 512],
                                     start=True, stop=True, skip_group_check=True)
                    s_r3.inc(instr)

            @on("vector")
            def _(e, en):
                for tg in range(4):
                    wait(e, en, s_r3, 8 + tg)
                    instr = e.tensor_copy(rs3_rep[:, tg * 512:(tg + 1) * 512], ps[4][:, 0:512])
                    s_mr.inc(instr)

            @on("sync")
            def _(e, en):
                for ot in range(NOT_F):
                    if ot >= 2:
                        ppwait(e, en, s_lnw, ot - 2)
                        wait(e, en, s_sq2, ot - 1)
                        wait(e, en, s_hl[ot % 2], 16 * (ot // 2))
                    instr = e.dma_start(out=lnh_t[ot % 2][:, :], in_=h_sp[ot])
                    ppissue(e, en, s_hl, ot, instr)

            @on("vector")
            def _(e, en):
                schain(e, en, e.memset(rmax[:, :], 0.0))
                wait(e, en, s_mr, 9)
                for ot in range(NOT_F):
                    ppwait(e, en, s_hl, ot)
                    if ot >= 1:
                        wait(e, en, s_ln, ot)
                    schain(e, en, e.tensor_tensor(t_bf[:, :], lnh_t[ot % 2][:, :], mu_rep[:, :], ALU.add))
                    schain(e, en, e.tensor_tensor(t_bf[:, :], t_bf[:, :], rs3_rep[:, :], ALU.mult))
                    instr = e.tensor_scalar(lnh_t[ot % 2][:, :], t_bf[:, :], gcols[:, ot:ot + 1],
                                            lbcols[:, ot:ot + 1], ALU.mult, ALU.add)
                    s_ln.inc(instr)
                    if ot >= 1:
                        wait(e, en, s_sq2, ot)
                        wait(e, en, sem("p4_rmx"), ot - 1)
                        instr = e.tensor_tensor(rmax[:, :], rmax[:, :], hsq2[:, :], ALU.max)
                        sem("p4_rmx").inc(instr)
                wait(e, en, s_sq2, NOT_F)
                wait(e, en, sem("p4_rmx"), NOT_F - 1)
                instr = e.tensor_tensor(rmax[:, :], rmax[:, :], hsq2[:, :], ALU.max)
                sem("p4_rmxf").inc(instr)

            @on("scalar")
            def _(e, en):
                for ot in range(NOT_F):
                    wait(e, en, s_ln, ot + 1)
                    if ot >= 2:
                        wait(e, en, s_lnw[ot % 2], 16 * (ot // 2))
                    instr = e.dma_start(out=lnh_sp[ot], in_=lnh_t[ot % 2][:, :])
                    ppissue(e, en, s_lnw, ot, instr)

            @on("gpsimd")
            def _(e, en):
                for ot in range(NOT_F):
                    wait(e, en, s_ln, ot + 1)
                    if ot >= 1:
                        wait(e, en, s_smt2, ot)
                        wait(e, en, sem("p4_rmx"), ot)
                    instr = e.tensor_tensor(hsq2[:, :], lnh_t[ot % 2][:, :], lnh_t[ot % 2][:, :],
                                            ALU.mult)
                    s_sq2.inc(instr)

            @on("tensor")
            def _(e, en):
                for ot in range(NOT_F):
                    wait(e, en, s_sq2, ot + 1)
                    for tg in range(4):
                        instr = e.matmul(ps[5][32 * tg:32 * tg + 1, 0:512], ones_c_b[:, 0:1],
                                         hsq2[:, tg * 512:(tg + 1) * 512],
                                         start=(ot == 0), stop=(ot == NOT_F - 1),
                                         tile_position=(0, 32 * tg), skip_group_check=True)
                    s_smt2.inc(instr)

            @on("vector")
            def _(e, en):
                wait(e, en, sem("p4_rmxf"), 1)
                instr = e.nop()
                s_q2.inc(instr)

            @on("tensor")
            def _(e, en):
                wait(e, en, s_q2, 1)
                for j in range(NT):
                    bank = ps[2] if j % 2 == 0 else ps[3]
                    if j >= 2:
                        wait(e, en, sem("p4_rme"), j - 1)
                    instr = e.transpose(bank[:, 0:128], rmax[:, j * 128:(j + 1) * 128],
                                        identf_sb[:, :])
                    sem("p4_rmt").inc(instr)

            @on("vector")
            def _(e, en):
                for j in range(NT):
                    wait(e, en, sem("p4_rmt"), j + 1)
                    bank = ps[2] if j % 2 == 0 else ps[3]
                    instr = e.tensor_reduce(ctmp[:, j:j + 1], bank[:, 0:128], AX.X, ALU.max)
                    sem("p4_rme").inc(instr)

            @on("tensor")
            def _(e, en):
                wait(e, en, sem("p4_rme"), NT)
                instr = e.transpose(ps[2][0:16, 0:128], ctmp[:, 0:16], identf_sb[:, :])
                sem("p4_rmt").inc(instr)

            @on("vector")
            def _(e, en):
                wait(e, en, sem("p4_rmt"), NT + 1)
                instr = e.tensor_copy(stg4[0:16, 0:128], ps[2][0:16, 0:128])
                sem("p4_rme").inc(instr)

            @on("gpsimd")
            def _(e, en):
                wait(e, en, sem("p4_rme"), NT + 1)
                instr = e.dma_start(out=row_v[0:1, :].rearrange("o (t p) -> o t p", t=16),
                                    in_=stg4[0:16, 0:128])
                sem("p4_flt3").inc(instr, 16)

            @on("vector")
            def _(e, en):
                wait(e, en, s_smt2, NOT_F)
                for g in range(4):
                    instr = e.tensor_copy(stg4[32 * g:32 * g + 1, 512:1024],
                                          ps[5][32 * g:32 * g + 1, 0:512])
                sem("p4_stg2").inc(instr)
                wait(e, en, sem("p4_flt2"), 16)
                instr = e.tensor_scalar(rsc[0:1, :], rsc[0:1, :], 1.0 / FF, LN_EPS, ALU.mult, ALU.add)
                s_q2.inc(instr)

            @on("gpsimd")
            def _(e, en):
                wait(e, en, sem("p4_stg2"), 1)
                instr = e.dma_start(out=rsc[0:1, :].rearrange("o (g c) -> o g c", g=4),
                                    in_=bass.AP(stg4, 512, [[32 * 1024, 4], [1, 512]]))
                sem("p4_flt2").inc(instr, 16)

            @on("scalar")
            def _(e, en):
                wait(e, en, s_q2, 2)
                instr = e.activation(rsc[0:1, :], rsc[0:1, :], AF.Sqrt)
                s_q2.inc(instr)

            @on("scalar")
            def _(e, en):
                wait(e, en, sem("p4_flt3"), 16)
                instr = e.activation(row_v[0:1, :], row_v[0:1, :], AF.Sqrt)
                sem("p4_rmsq").inc(instr)

            @on("vector")
            def _(e, en):
                wait(e, en, s_q2, 3)
                wait(e, en, sem("p4_rmsq"), 1)
                schain(e, en, e.reciprocal(rsc[0:1, :], rsc[0:1, :]))
                schain(e, en, e.tensor_tensor(row_mu[0:1, :], row_v[0:1, :], rsc[0:1, :], ALU.mult))
                schain(e, en, e.tensor_scalar(row_mu[0:1, :], row_mu[0:1, :], 1e-5, None, ALU.max))
                schain(e, en, e.tensor_scalar(row_v[0:1, :], row_mu[0:1, :], swc[0:1, 3:4], None, ALU.mult))
                schain(e, en, e.tensor_copy(row_iv2b[0:1, :], row_v[0:1, :]))
                schain(e, en, e.reciprocal(row_mu[0:1, :], row_mu[0:1, :]))
                schain(e, en, e.tensor_scalar(row_mu[0:1, :], row_mu[0:1, :], 127.0, None, ALU.mult))
                schain(e, en, e.tensor_tensor(row_mu[0:1, :], row_mu[0:1, :], rsc[0:1, :], ALU.mult))
                schain(e, en, e.tensor_copy(row_c2h[0:1, :], row_mu[0:1, :]))
                schain(e, en, e.tensor_copy(rsc[0:1, :], row_c2h[0:1, :]))
                schain(e, en, e.tensor_tensor(rsc[0:1, :], row_mu[0:1, :], rsc[0:1, :], ALU.subtract))
                instr = e.tensor_copy(row_c2l[0:1, :], rsc[0:1, :])
                s_q2.inc(instr)

            barrier("p3b", extra=[("scalar", s_lnw[0], 16 * NOT_F // 2),
                                  ("scalar", s_lnw[1], 16 * NOT_F // 2),
                                  ("vector", s_q2, 4)])

        phase_mark(7)
        with contextlib.ExitStack() as sc5:
            E5 = sc5.enter_context
            hq_sb = E5(nc.sbuf_tensor("hq_sb", [128, KF * 512], BF16))
            lnh_in = E5(nc.sbuf_tensor("lnh_in", [128, 512], BF16))
            hq_f = E5(nc.sbuf_tensor("hq_f", [128, 512], F32))
            hq_i = E5(nc.sbuf_tensor("hq_i", [128, 512], I32))
            c2_rep = E5(nc.sbuf_tensor("c2_rep", [128, 2048], F32))
            inv2_rep = E5(nc.sbuf_tensor("inv2_rep", [128, 2048], F32))
            w2qb = [E5(nc.sbuf_tensor(f"w2qb{i}", [128, KF * 128], BF16)) for i in range(2)]
            oasm = E5(nc.sbuf_tensor("oasm", [128, 4 * 2048], BF16))
            fout = E5(nc.sbuf_tensor("fout", [128, 2048], F32))
            _aacc = E5(nc.sbuf_tensor("aacc_in", [128, DM], BF16))
            aacc_in = [_aacc, _aacc]
            fo_b = E5(nc.sbuf_tensor("fo_b", [128, 512], BF16))

            s_c2p = sem("p5_c2")
            s_c2e = sem("p5_c2e")
            s_hql = sem("p5_hql")
            s_hq = sem("p5_hq")
            s_w2r = ppsem("p5_w2r")
            s_mm4 = sem("p5_mm")
            s_fv = sem("p5_fv")
            s_ftr = sem("p5_ftr")
            s_oa = sem("p5_oa")
            s_aal = sem("p5_aal")
            s_fadd = sem("p5_fadd")
            s_outw = sem("p5_out")

            @on("tensor")
            def _(e, en):
                s_c2p.inc(e.nop())
                for tg in range(4):
                    if tg >= 1:
                        wait(e, en, s_c2e, tg)
                    e.matmul(ps[4][:, 0:512], ones_r_b[0:1, :], row_c2h[0:1, tg * 512:(tg + 1) * 512],
                             start=True, stop=False, skip_group_check=True)
                    instr = e.matmul(ps[4][:, 0:512], ones_r_b[0:1, :], row_c2l[0:1, tg * 512:(tg + 1) * 512],
                                     start=False, stop=True, skip_group_check=True)
                    s_c2p.inc(instr)
                for tg in range(4):
                    wait(e, en, s_c2e, 4 + tg)
                    instr = e.matmul(ps[4][:, 0:512], ones_r_b[0:1, :], row_iv2b[0:1, tg * 512:(tg + 1) * 512],
                                     start=True, stop=True, skip_group_check=True)
                    s_c2p.inc(instr)

            @on("vector")
            def _(e, en):
                for tg in range(4):
                    wait(e, en, s_c2p, 2 + tg)
                    instr = e.tensor_copy(c2_rep[:, tg * 512:(tg + 1) * 512], ps[4][:, 0:512])
                    s_c2e.inc(instr)
                for tg in range(4):
                    wait(e, en, s_c2p, 6 + tg)
                    instr = e.tensor_copy(inv2_rep[:, tg * 512:(tg + 1) * 512], ps[4][:, 0:512])
                    s_c2e.inc(instr)

            for tg in range(4):
                @on("sync")
                def _(e, en, tg=tg):
                    for kt in range(KF):
                        i = tg * KF + kt
                        if i >= 1:
                            wait(e, en, s_hq, i)
                        instr = e.dma_start(out=lnh_in[:, :],
                                            in_=lnh_sp[kt][:, tg * 512:(tg + 1) * 512])
                        s_hql.dinc(instr)

                @on("vector")
                def _(e, en, tg=tg):
                    wait(e, en, s_c2e, 8)
                    if tg >= 1:
                        wait(e, en, s_fv, tg * OT2)
                    for kt in range(KF):
                        i = tg * KF + kt
                        wait(e, en, s_hql, 16 * (i + 1))
                        schain(e, en, e.tensor_tensor(hq_f[:, :], lnh_in[:, :],
                                        c2_rep[:, tg * 512:(tg + 1) * 512], ALU.mult))
                        schain(e, en, e.tensor_scalar(hq_f[:, :], hq_f[:, :], MAGICR, MAGICR, ALU.add, ALU.subtract))
                        schain(e, en, e.tensor_copy(hq_i[:, :], hq_f[:, :]))
                        instr = e.tensor_copy(hq_sb[:, kt * 512:(kt + 1) * 512], hq_i[:, :])
                        s_hq.inc(instr)

                @on("gpsimd")
                def _(e, en, tg=tg):
                    for o in range(OT2):
                        i = tg * OT2 + o
                        if i >= 2:
                            wait(e, en, s_mm4, i - 1)
                            wait(e, en, s_w2r[i % 2], 16 * (i // 2))
                        instr = e.dma_start(out=w2qb[o % 2][:, :].rearrange("p (k c2) -> p k c2", k=KF),
                                            in_=w2q_sp[o])
                        ppissue(e, en, s_w2r, i, instr)

                def te_ftr(e, en, tg, o):
                    i = tg * OT2 + o
                    wait(e, en, s_fv, i + 1)
                    if i >= 1:
                        wait(e, en, s_oa, i)
                    for j in range(4):
                        instr = e.transpose(psA[:, j * 128:(j + 1) * 128],
                                            fo_b[:, j * 128:(j + 1) * 128], identb_sb[:, :])
                    s_ftr.inc(instr)

                def v_oa(e, en, tg, o):
                    i = tg * OT2 + o
                    wait(e, en, s_ftr, i + 1)
                    instr = e.tensor_copy(
                        bass.AP(oasm, o * 128, [[4 * 2048, 128], [2048, 4], [1, 128]]),
                        psA[:, 0:512].rearrange("p (j c2) -> p j c2", j=4))
                    s_oa.inc(instr)

                @on("tensor")
                def _(e, en, tg=tg):
                    wait(e, en, s_hq, (tg + 1) * KF)
                    for o in range(OT2):
                        i = tg * OT2 + o
                        ppwait(e, en, s_w2r, i)
                        if i >= 2:
                            wait(e, en, s_fv, i - 1)
                        for kt in range(KF):
                            instr = e.matmul(ps[o % 2][:, :], w2qb[o % 2][:, kt * 128:(kt + 1) * 128],
                                             bass.AP(hq_sb, kt * 512, [[KF * 512, 128], [1, 512]]),
                                             start=(kt == 0), stop=(kt == KF - 1))
                        s_mm4.inc(instr)
                        if o >= 1:
                            te_ftr(e, en, tg, o - 1)
                    te_ftr(e, en, tg, OT2 - 1)

                @on("vector")
                def _(e, en, tg=tg):
                    for o in range(OT2):
                        i = tg * OT2 + o
                        wait(e, en, s_mm4, i + 1)
                        if i >= 1:
                            wait(e, en, s_ftr, i)
                        schain(e, en, e.tensor_tensor(hq_f[:, :], ps[o % 2][:, :],
                                        inv2_rep[:, tg * 512:(tg + 1) * 512], ALU.mult))
                        instr = e.tensor_scalar(fo_b[:, :], hq_f[:, :], b2cols[:, o:o + 1],
                                                None, ALU.add)
                        s_fv.inc(instr)
                        if o >= 1:
                            v_oa(e, en, tg, o - 1)
                    v_oa(e, en, tg, OT2 - 1)

                @on("gpsimd")
                def _(e, en, tg=tg):
                    for j in range(4):
                        t = tg * 4 + j
                        ii = tg * 4 + j
                        if ii >= 1:
                            wait(e, en, s_fadd, ii)
                        instr = e.dma_start(out=aacc_in[j % 2][:, :], in_=aacc_sp[t])
                        s_aal.dinc(instr)

                @on("vector")
                def _(e, en, tg=tg):
                    wait(e, en, s_oa, (tg + 1) * OT2)
                    for j in range(4):
                        ii = tg * 4 + j
                        wait(e, en, s_aal, 16 * (ii + 1))
                        if ii >= 1:
                            wait(e, en, s_outw, 16 * ii)
                        instr = e.tensor_tensor(fout[:, :], oasm[:, j * 2048:(j + 1) * 2048],
                                                aacc_in[j % 2][:, :], ALU.add)
                        s_fadd.inc(instr)

                @on("scalar")
                def _(e, en, tg=tg):
                    for j in range(4):
                        t = tg * 4 + j
                        ii = tg * 4 + j
                        wait(e, en, s_fadd, ii + 1)
                        instr = e.dma_start(out=out[t * 128:(t + 1) * 128, :], in_=fout[:, :])
                        s_outw.dinc(instr)

            @on("sync")
            def _(e, en):
                wait(e, en, sem("p5_out"), 16 * 16)
                e.nop()

    return nc


def _consts():
    m = np.full((128, 512), MASK_NEG, dtype=np.float32)
    pp = np.arange(128)[:, None]
    ll = np.arange(256)[None, :]
    m[:, 0:256] = np.where(pp <= ll, 0.0, MASK_NEG)
    m[:, 256:512] = np.where((128 + pp) <= ll, 0.0, MASK_NEG)
    return dict(identb=np.eye(128, dtype=ml_dtypes.bfloat16),
                identf=np.eye(128, dtype=np.float32),
                onesrow_b=np.ones((1, 128), dtype=ml_dtypes.bfloat16),
                onescol_b=np.ones((128, 1), dtype=ml_dtypes.bfloat16),
                onesrow_f=np.ones((1, 128), dtype=np.float32),
                maskt=m,
                sel2in=np.concatenate([
                    np.concatenate([np.ones((1, 128), np.float32), np.zeros((1, 128), np.float32)], 1),
                    np.concatenate([np.zeros((1, 128), np.float32), np.ones((1, 128), np.float32)], 1)], 0))


_NC = None


def kernel(L0_attn_u, L0_attn_W_in, L0_attn_conv_w, L0_attn_conv_b, L0_attn_A_log,
           L0_attn_D, L0_attn_dt_bias, L0_attn_norm_w, L0_attn_W_out,
           L0_ffn_w1, L0_ffn_b1, L0_ffn_ln_g, L0_ffn_ln_b, L0_ffn_w2, L0_ffn_b2):
    global _NC
    if _NC is None:
        _NC = build_nc()
    nc = _NC

    win_pad = np.zeros((NOT_IN * 128, DM), dtype=np.float32)
    win_pad[:8512] = L0_attn_W_in
    winT = np.ascontiguousarray(win_pad.T.reshape(KD, 128, NOT_IN, 128).transpose(2, 1, 0, 3))
    w1T = np.ascontiguousarray(L0_ffn_w1.T.reshape(KD, 128, NOT_F, 128).transpose(2, 1, 0, 3))
    w2T = np.ascontiguousarray(L0_ffn_w2.T.reshape(KF, 128, OT2, 128).transpose(2, 1, 0, 3))
    woutT = np.ascontiguousarray(L0_attn_W_out.T)

    colmaj = lambda v, k: np.ascontiguousarray(np.asarray(v).reshape(k, 128).T)
    shared = dict(winT=winT, w1T=w1T, w2T=w2T, woutT=woutT,
                  convw=np.ascontiguousarray(
                      L0_attn_conv_w.reshape(NOT_XBC, 128, 4).transpose(1, 0, 2).reshape(128, NOT_XBC * 4)),
                  convb=colmaj(L0_attn_conv_b, NOT_XBC),
                  a_log=np.ascontiguousarray(L0_attn_A_log.reshape(64, 1)),
                  d_in=np.ascontiguousarray(L0_attn_D.reshape(1, 64)),
                  dt_bias=np.ascontiguousarray(L0_attn_dt_bias.reshape(64, 1)),
                  norm_w=colmaj(L0_attn_norm_w, 32), b1=colmaj(L0_ffn_b1, NOT_F),
                  ln_g=colmaj(L0_ffn_ln_g, NOT_F), ln_b=colmaj(L0_ffn_ln_b, NOT_F),
                  b2=colmaj(L0_ffn_b2, OT2), **_consts())
    in_maps = []
    for b in range(8):
        m = dict(shared)
        m["u"] = np.ascontiguousarray(L0_attn_u[b])
        in_maps.append(m)

    res = run_bass_kernel_spmd(nc, in_maps, core_ids=list(range(8)))
    return np.stack([np.asarray(res.results[b]["out"], dtype=np.float32) for b in range(8)])




# version 10
# speedup vs baseline: 1.0704x; 1.0650x over previous
import os
import contextlib
import numpy as np
import ml_dtypes
import concourse.bass as bass
import concourse.mybir as mybir
from concourse.bass_utils import run_bass_kernel_spmd

F32 = mybir.dt.float32
BF16 = mybir.dt.bfloat16
I32 = mybir.dt.int32
AF = mybir.ActivationFunctionType
ALU = mybir.AluOpType
AX = mybir.AxisListType

L, DM, DIN, H, Q = 2048, 2048, 4096, 64, 256
NCH, NT, KD = L // Q, L // 128, DM // 128
NOT_Z, NOT_XBC = 32, 34
NOT_IN = NOT_Z + NOT_XBC + 1
FF, NOT_F, KF, OT2 = 8192, 64, 64, 16
RMS_EPS = 1.1920929e-07
LN_EPS = 1e-5
MASK_NEG = -1e30
MAGICR = 12582912.0
ENGS = ("sync", "scalar", "vector", "tensor", "gpsimd")
_PH = int(os.environ.get("MK_PH", "9"))
_SKIP = set(x for x in os.environ.get("MK_SKIP", "").split(",") if x)


def build_nc():
    nc = bass.Bass("TRN2", target_bir_lowering=False, detect_race_conditions=False)

    u = nc.dram_tensor("u", [L, DM], F32, kind="ExternalInput").ap()
    winT = nc.dram_tensor("winT", [NOT_IN, 128, KD, 128], F32, kind="ExternalInput").ap()
    w1T = nc.dram_tensor("w1T", [NOT_F, 128, KD, 128], F32, kind="ExternalInput").ap()
    w2T = nc.dram_tensor("w2T", [OT2, 128, KF, 128], F32, kind="ExternalInput").ap()
    woutT = nc.dram_tensor("woutT", [DIN, DM], F32, kind="ExternalInput").ap()
    convw = nc.dram_tensor("convw", [128, NOT_XBC * 4], F32, kind="ExternalInput").ap()
    convb = nc.dram_tensor("convb", [128, NOT_XBC], F32, kind="ExternalInput").ap()
    a_log = nc.dram_tensor("a_log", [64, 1], F32, kind="ExternalInput").ap()
    d_in = nc.dram_tensor("d_in", [1, 64], F32, kind="ExternalInput").ap()
    dt_bias = nc.dram_tensor("dt_bias", [64, 1], F32, kind="ExternalInput").ap()
    norm_w = nc.dram_tensor("norm_w", [128, 32], F32, kind="ExternalInput").ap()
    b1 = nc.dram_tensor("b1", [128, NOT_F], F32, kind="ExternalInput").ap()
    ln_g = nc.dram_tensor("ln_g", [128, NOT_F], F32, kind="ExternalInput").ap()
    ln_b = nc.dram_tensor("ln_b", [128, NOT_F], F32, kind="ExternalInput").ap()
    b2 = nc.dram_tensor("b2", [128, OT2], F32, kind="ExternalInput").ap()
    identb = nc.dram_tensor("identb", [128, 128], BF16, kind="ExternalInput").ap()
    identf = nc.dram_tensor("identf", [128, 128], F32, kind="ExternalInput").ap()
    onesrow_b = nc.dram_tensor("onesrow_b", [1, 128], BF16, kind="ExternalInput").ap()
    onescol_b = nc.dram_tensor("onescol_b", [128, 1], BF16, kind="ExternalInput").ap()
    onesrow_f = nc.dram_tensor("onesrow_f", [1, 128], F32, kind="ExternalInput").ap()
    maskt = nc.dram_tensor("maskt", [128, 512], F32, kind="ExternalInput").ap()
    sel2in = nc.dram_tensor("sel2in", [2, 256], F32, kind="ExternalInput").ap()

    out = nc.dram_tensor("out", [L, DM], F32, kind="ExternalOutput").ap()
    _SCRK = "ExternalOutput" if os.environ.get("MK_DBGOUT") else "Internal"
    xq1_sp = nc.dram_tensor("xq1_sp", [KD, 128, L], BF16, kind=_SCRK).ap()
    z_sp = nc.dram_tensor("z_sp", [NOT_Z, 128, L], BF16, kind=_SCRK).ap()
    xfm_sp = nc.dram_tensor("xfm_sp", [32, 128, L], BF16, kind=_SCRK).ap()
    xtm_sp = nc.dram_tensor("xtm_sp", [32, 128, 2048], BF16, kind=_SCRK).ap()
    yg_sp = nc.dram_tensor("yg_sp", [NCH, 128, 8192], BF16, kind=_SCRK).ap()
    aacc_sp = nc.dram_tensor("aacc_sp", [NT, 128, DM], BF16, kind=_SCRK).ap()
    h_sp = nc.dram_tensor("h_sp", [NOT_F, 128, L], BF16, kind=_SCRK).ap()
    lnh_sp = nc.dram_tensor("lnh_sp", [NOT_F, 128, L], BF16, kind=_SCRK).ap()
    w2q_sp = nc.dram_tensor("w2q_sp", [OT2, 128, KF * 128], BF16, kind=_SCRK).ap()

    stack = contextlib.ExitStack()
    with stack:
        E = stack.enter_context

        wf = [E(nc.sbuf_tensor(f"wf{i}", [128, 2048], F32)) for i in range(2)]
        wq_b = [E(nc.sbuf_tensor(f"wq_b{i}", [128, 2048], BF16)) for i in range(2)]
        sq_scr = E(nc.sbuf_tensor("sq_scr", [128, 2048], F32))
        csum = E(nc.sbuf_tensor("csum", [128, 16], F32))
        csq = E(nc.sbuf_tensor("csq", [128, 16], F32))
        cmean = E(nc.sbuf_tensor("cmean", [128, 16], F32))
        crstd = E(nc.sbuf_tensor("crstd", [128, 16], F32))
        cr2 = E(nc.sbuf_tensor("cr2", [128, 16], F32))
        camax = E(nc.sbuf_tensor("camax", [128, 16], F32))
        cr2s1 = E(nc.sbuf_tensor("cr2s1", [128, 16], F32))
        ctmp = E(nc.sbuf_tensor("ctmp", [128, 16], F32))
        acol = E(nc.sbuf_tensor("acol", [128, 4], F32))
        swc = E(nc.sbuf_tensor("swc", [128, 4], F32))
        absacc = E(nc.sbuf_tensor("absacc", [128, 2], F32))
        pcol = E(nc.sbuf_tensor("pcol", [64, 4], F32))
        d_row = E(nc.sbuf_tensor("d_row", [1, 64], F32))
        d_rep = E(nc.sbuf_tensor("d_rep", [128, 64], F32))
        normw_sb = E(nc.sbuf_tensor("normw_sb", [128, 32], F32))
        bcols = E(nc.sbuf_tensor("bcols", [128, NOT_F], F32))
        gcols = E(nc.sbuf_tensor("gcols", [128, NOT_F], F32))
        lbcols = E(nc.sbuf_tensor("lbcols", [128, NOT_F], F32))
        b2cols = E(nc.sbuf_tensor("b2cols", [128, OT2], F32))
        identb_sb = E(nc.sbuf_tensor("identb_sb", [128, 128], BF16))
        identf_sb = E(nc.sbuf_tensor("identf_sb", [128, 128], F32))
        ones_r_b = E(nc.sbuf_tensor("ones_r_b", [1, 128], BF16))
        ones_c_b = E(nc.sbuf_tensor("ones_c_b", [128, 1], BF16))
        ones_r_f = E(nc.sbuf_tensor("ones_r_f", [1, 128], F32))
        sel2 = E(nc.sbuf_tensor("sel2", [2, 256], F32))
        mask_sb = E(nc.sbuf_tensor("mask_sb", [128, 512], F32))

        ps = [E(nc.psum_tensor(f"ps{i}", [128, 512], F32)) for i in range(6)]
        psA = E(nc.psum_tensor("psA", [128, 1024], BF16))
        psB = E(nc.psum_tensor("psB", [128, 1024], BF16))

        class Phys:
            __slots__ = ("h", "n", "name")
            def __init__(self, name):
                self.h = E(nc.semaphore(name)); self.n = 0; self.name = name

        class Sem:
            __slots__ = ("p", "base", "lname")
            def __init__(self, phys, lname):
                self.p = phys; self.base = phys.n; self.lname = lname
            @property
            def n(self):
                return self.p.n - self.base
            def inc(self, instr, k=1):
                instr.then_inc(self.p.h, k); self.p.n += k
            def dinc(self, instr):
                self.inc(instr, 16)

        SEMS = {}
        _pool = []
        _pool_next = {}
        SW_SEMS = {"p0_xq1w", "p1_wld", "p1_zw", "p1_xwa", "p1_xwb", "p1_xtw", "p2_ch", "p2_row",
                   "p2_ygw", "pb_wo", "pb_ygla", "pb_yglb", "pb_u2a", "pb_u2b", "pb_aw", "pb_w2wa", "pb_w2wb",
                   "p3_xq1l", "p3_ivd", "p3_hwa", "p3_hwb", "p4_lnwa", "p4_lnwb", "p4_flt", "p4_flt2",
                   "p4_flt3", "p5_w2ra", "p5_w2rb", "p5_aal", "p5_out"}
        def sem(name):
            if name in SEMS:
                return SEMS[name]
            if (name.startswith("bar_") or name.startswith("chn_")
                    or name in ("pre", "prep") or name in SW_SEMS):
                SEMS[name] = Sem(Phys(name), name)
                return SEMS[name]
            pref = name.split("_")[0]
            idx = _pool_next.get(pref, 0)
            _pool_next[pref] = idx + 1
            while len(_pool) <= idx:
                _pool.append(Phys(f"pool{len(_pool)}"))
            SEMS[name] = Sem(_pool[idx], name)
            return SEMS[name]

        waited = {}
        def wait(eng, engname, s, val=None):
            v = (s.p.n) if val is None else (s.base + val)
            if v <= s.base and val is not None and val <= 0:
                return
            if v <= 0:
                return
            k = (engname, s.p.name)
            if waited.get(k, -1) >= v:
                return
            eng.wait_ge(s.p.h, v)
            waited[k] = v

        def ppsem(name):
            return [sem(name + "a"), sem(name + "b")]

        def ppwait(e, en, ss, t):
            wait(e, en, ss[t % 2], 16 * (t // 2 + 1))

        def ppissue(e, en, ss, t, instr):
            ss[t % 2].inc(instr, 16)

        blk = E(nc.Block())
        _chain = sem_holder = {}
        def schain(e, engname, instr):
            cs = sem("chain_" + engname) if False else None
            s_ = SEMS.get("chn_" + engname)
            if s_ is None:
                s_ = sem("chn_" + engname)
            s_.inc(instr)
            e.wait_ge(s_.p.h, s_.p.n)
        _cur = [0]

        def phase_mark(k):
            _cur[0] = k

        def on(engname):
            def deco(f):
                if _cur[0] > _PH:
                    return
                getattr(blk, engname)(lambda e: f(e, engname))
            return deco

        def barrier(name, extra=()):
            s = sem("bar_" + name)
            for engname in ENGS:
                @on(engname)
                def _(e, en, engname=engname):
                    for (wn, ws, wv) in extra:
                        if wn == engname:
                            wait(e, en, ws, wv)
                    s.inc(e.nop())
            for engname in ENGS:
                @on(engname)
                def _(e, en):
                    wait(e, en, s, 5)
            return s

        s_pre = sem("pre")
        s_prep = sem("prep")

        @on("sync")
        def _(e, en):
            for (dst, src) in ((identb_sb[:, :], identb), (identf_sb[:, :], identf),
                               (ones_r_b[:, :], onesrow_b), (ones_c_b[:, :], onescol_b),
                               (ones_r_f[:, :], onesrow_f), (mask_sb[:, :], maskt),
                               (sel2[:, :], sel2in), (pcol[:, 0:1], a_log),
                               (pcol[:, 1:2], dt_bias), (d_row[:, :], d_in),
                               (normw_sb[:, :], norm_w), (bcols[:, :], b1),
                               (gcols[:, :], ln_g), (lbcols[:, :], ln_b),
                               (b2cols[:, :], b2)):
                s_pre.dinc(e.dma_start(out=dst, in_=src))

        @on("scalar")
        def _(e, en):
            wait(e, en, s_pre)
            schain(e, en, e.activation(pcol[:, 2:3], pcol[:, 0:1], AF.Exp))
            instr = e.mul(pcol[:, 2:3], pcol[:, 2:3], -1.0)
            s_prep.inc(instr)

        @on("tensor")
        def _(e, en):
            wait(e, en, s_pre)
            instr = e.matmul(ps[0][:, 0:64], ones_r_f[0:1, :], d_row[0:1, :], start=True, stop=True)
            s_prep.inc(instr)

        @on("vector")
        def _(e, en):
            wait(e, en, s_prep, 2)
            e.memset(acol[:, :], 0.0)
            e.memset(absacc[:, :], 0.0)
            instr = e.tensor_copy(d_rep[:, :], ps[0][:, 0:64])
            s_prep.inc(instr)

        barrier("pro")

        phase_mark(1)
        scspan = contextlib.ExitStack()
        Esp = scspan.enter_context
        B_fm = Esp(nc.sbuf_tensor("B_fm", [128, L], BF16))
        C_fm = Esp(nc.sbuf_tensor("C_fm", [128, L], BF16))
        B_tm = Esp(nc.sbuf_tensor("B_tm", [128, NT * 128], BF16))
        dt_fm = Esp(nc.sbuf_tensor("dt_fm", [64, L], F32))
        acum_fm = Esp(nc.sbuf_tensor("acum_fm", [64, L], F32))
        acum_hi = Esp(nc.sbuf_tensor("acum_hi", [64, L], BF16))
        acum_lo = Esp(nc.sbuf_tensor("acum_lo", [64, L], BF16))
        acum_tm = Esp(nc.sbuf_tensor("acum_tm", [128, NT * 64], F32))
        nacum_tm = Esp(nc.sbuf_tensor("nacum_tm", [128, NT * 64], F32))
        dt_tm = Esp(nc.sbuf_tensor("dt_tm", [128, NT * 64], F32))
        dtdec_fm = Esp(nc.sbuf_tensor("dtdec_fm", [64, Q], F32))
        dtdec_tm = Esp(nc.sbuf_tensor("dtdec_tm", [128, 2 * 64], F32))

        with contextlib.ExitStack() as sc0:
            scx = contextlib.ExitStack()
            E0 = scx.enter_context
            xn_fm = E0(nc.sbuf_tensor("xn_fm", [128, KD * L], BF16))
            trst = E0(nc.sbuf_tensor("trst", [128, 2048], BF16))
            sc0i = contextlib.ExitStack()
            E0i = sc0i.enter_context
            u_tl = [E0i(nc.sbuf_tensor(f"u_tl{i}", [128, DM], F32)) for i in range(2)]
            xn_tm = E0i(nc.sbuf_tensor("xn_tm", [128, DM], F32))
            xn_tm_b = E0i(nc.sbuf_tensor("xn_tm_b", [128, DM], BF16))
            xq_tm_i = E0i(nc.sbuf_tensor("xq_tm_i", [128, DM], I32))
            xq_tm_b = E0i(nc.sbuf_tensor("xq_tm_b", [128, DM], BF16))

            s_u = ppsem("p0_u")
            s_st0 = sem("p0_st")
            s_sq0 = sem("p0_sq")
            s_c0 = sem("p0_c")
            s_u2p = ppsem("p0_u2")
            s_x2 = sem("p0_x2")
            s_tr0 = sem("p0_tr")
            s_ev0 = sem("p0_ev")
            s_xq1w = sem("p0_xq1w")

            @on("sync")
            def _(e, en):
                for t in range(NT):
                    if t >= 2:
                        wait(e, en, s_st0, t - 1)
                        wait(e, en, sem("p0_sqs"), t - 1)
                        wait(e, en, s_u[t % 2], 16 * (t // 2))
                    instr = e.dma_start(out=u_tl[t % 2][:, :], in_=u[t * 128:(t + 1) * 128, :])
                    ppissue(e, en, s_u, t, instr)
                for t in range(NT):
                    wait(e, en, s_c0, 3)
                    if t >= 2:
                        wait(e, en, s_x2, 2 * (t - 1))
                        wait(e, en, s_u2p[t % 2], 16 * (t // 2))
                    instr = e.dma_start(out=u_tl[t % 2][:, :], in_=u[t * 128:(t + 1) * 128, :])
                    ppissue(e, en, s_u2p, t, instr)

            @on("vector")
            def _(e, en):
                for t in range(NT):
                    ppwait(e, en, s_u, t)
                    wait(e, en, sem("p0_sqs"), t)
                    instr = e.tensor_reduce(csum[:, t:t + 1], u_tl[t % 2][:, :], AX.X, ALU.add)
                    s_st0.inc(instr)
                wait(e, en, sem("p0_sqs"), NT)
                wait(e, en, s_st0, NT)
                schain(e, en, e.tensor_scalar(cmean[:, :], csum[:, :], 1.0 / DM, None, ALU.mult))
                schain(e, en, e.tensor_scalar(ctmp[:, :], csq[:, :], 1.0 / DM, None, ALU.mult))
                schain(e, en, e.scalar_tensor_tensor(csq[:, :], cmean[:, :], acol[:, 0:1], cmean[:, :],
                                       ALU.bypass, ALU.mult))
                schain(e, en, e.tensor_tensor(csq[:, :], ctmp[:, :], csq[:, :], ALU.subtract))
                instr = e.tensor_scalar(csq[:, :], csq[:, :], 1.0, LN_EPS, ALU.mult, ALU.add)
                s_c0.inc(instr)
                wait(e, en, s_sq0, 1)
                schain(e, en, e.reciprocal(crstd[:, :], ctmp[:, :]))
                schain(e, en, e.scalar_tensor_tensor(cr2[:, :], crstd[:, :], acol[:, 0:1], crstd[:, :],
                                       ALU.bypass, ALU.mult))
                instr = e.tensor_scalar(cr2[:, :], cr2[:, :], -LN_EPS, 1.0 + LN_EPS, ALU.mult, ALU.add)
                s_c0.inc(instr)
                wait(e, en, s_sq0, 2)
                instr = e.reciprocal(cr2[:, :], ctmp[:, :])
                s_c0.inc(instr)
                for t in range(NT):
                    ppwait(e, en, s_u2p, t)
                    ut = u_tl[t % 2]
                    schain(e, en, e.tensor_scalar(xn_tm[:, :], ut[:, :], cmean[:, t:t + 1], crstd[:, t:t + 1],
                                    ALU.subtract, ALU.mult))
                    schain(e, en, e.tensor_reduce(ctmp[:, t:t + 1], xn_tm[:, :], AX.X, ALU.max,
                                    apply_absolute_value=True))
                    schain(e, en, e.tensor_scalar(camax[:, t:t + 1], ctmp[:, t:t + 1], cr2[:, t:t + 1], 1e-5,
                                    ALU.mult, ALU.max))
                    schain(e, en, e.reciprocal(ctmp[:, t:t + 1], camax[:, t:t + 1]))
                    schain(e, en, e.scalar_tensor_tensor(cr2s1[:, t:t + 1], ctmp[:, t:t + 1], cr2[:, t:t + 1],
                                           acol[:, 0:1], ALU.mult, ALU.add))
                    schain(e, en, e.tensor_scalar(cr2s1[:, t:t + 1], cr2s1[:, t:t + 1], 127.0, None, ALU.mult))
                    schain(e, en, e.tensor_scalar(sq_scr[:, :], xn_tm[:, :], cr2s1[:, t:t + 1], None, ALU.mult))
                    schain(e, en, e.tensor_scalar(sq_scr[:, :], sq_scr[:, :], MAGICR, MAGICR, ALU.add, ALU.subtract))
                    schain(e, en, e.tensor_copy(xq_tm_i[:, :], sq_scr[:, :]))
                    if t >= 1:
                        wait(e, en, s_tr0, 4 * t)
                    e.tensor_copy(xq_tm_b[:, :], xq_tm_i[:, :])
                    instr = e.tensor_copy(xn_tm_b[:, :], xn_tm[:, :])
                    s_x2.inc(instr, 2)
                    wait(e, en, s_tr0, 4 * t + 2)
                    e.tensor_copy(
                        bass.AP(xn_fm, t * 128, [[KD * L, 128], [L, 8], [1, 128]]),
                        psA[:, :].rearrange("p (k c) -> p k c", k=8))
                    instr = e.tensor_copy(
                        bass.AP(xn_fm, 8 * L + t * 128, [[KD * L, 128], [L, 8], [1, 128]]),
                        psB[:, :].rearrange("p (k c) -> p k c", k=8))
                    s_ev0.inc(instr)
                    wait(e, en, s_tr0, 4 * t + 4)
                    if t >= 1:
                        wait(e, en, s_xq1w, 16 * t)
                    e.tensor_copy(trst[:, 0:1024], psA[:, :])
                    instr = e.tensor_copy(trst[:, 1024:2048], psB[:, :])
                    s_ev0.inc(instr)

            @on("scalar")
            def _(e, en):
                for t in range(NT):
                    ppwait(e, en, s_u, t)
                    wait(e, en, sem("p0_sqs"), t)
                    instr = e.activation(sq_scr[:, :], u_tl[t % 2][:, :], AF.Square,
                                         accum_out=csq[:, t:t + 1])
                    sem("p0_sqs").inc(instr)
                wait(e, en, s_c0, 1)
                instr = e.activation(ctmp[:, :], csq[:, :], AF.Sqrt)
                s_sq0.inc(instr)
                wait(e, en, s_c0, 2)
                instr = e.activation(ctmp[:, :], cr2[:, :], AF.Sqrt)
                s_sq0.inc(instr)
                for t in range(NT):
                    wait(e, en, s_ev0, 2 * t + 2)
                    wait(e, en, s_xq1w, 16 * t)
                    instr = e.dma_start(
                        out=xq1_sp.rearrange("k p t -> p k t")[:, :, t * 128:(t + 1) * 128],
                        in_=trst[:, :].rearrange("p (k c) -> p k c", k=16))
                    s_xq1w.dinc(instr)

            @on("tensor")
            def _(e, en):
                for t in range(NT):
                    wait(e, en, s_x2, 2 * t + 2)
                    if t >= 1:
                        wait(e, en, s_ev0, 2 * t)
                    for half, tgt in ((0, psA), (1, psB)):
                        for j in range(8):
                            k = half * 8 + j
                            instr = e.transpose(tgt[:, j * 128:(j + 1) * 128],
                                                xn_tm_b[:, k * 128:(k + 1) * 128], identb_sb[:, :])
                        s_tr0.inc(instr)
                    wait(e, en, s_ev0, 2 * t + 1)
                    for half, tgt in ((0, psA), (1, psB)):
                        for j in range(8):
                            k = half * 8 + j
                            instr = e.transpose(tgt[:, j * 128:(j + 1) * 128],
                                                xq_tm_b[:, k * 128:(k + 1) * 128], identb_sb[:, :])
                        s_tr0.inc(instr)

            barrier("p0", extra=[("scalar", s_xq1w, 16 * NT)])
            sc0i.close()

            phase_mark(2)
            with contextlib.ExitStack() as sc1:
                sc1m = contextlib.ExitStack()
                E1 = sc1m.enter_context
                wb = E1(nc.sbuf_tensor("wb", [128, KD * 128], BF16))
                convbuf = E1(nc.sbuf_tensor("convbuf", [128, 2051], F32))
                convout = E1(nc.sbuf_tensor("convout", [128, DM], F32))
                xfm_t = [E1(nc.sbuf_tensor(f"xfm_t{i}", [128, DM], BF16)) for i in range(2)]
                convw_sb = E1(nc.sbuf_tensor("convw_sb", [128, NOT_XBC * 4], F32))
                convb_sb = E1(nc.sbuf_tensor("convb_sb", [128, NOT_XBC], F32))
                dtraw = E1(nc.sbuf_tensor("dtraw", [64, L], F32))
                scrA = sq_scr
                zst = E1(nc.sbuf_tensor("zst", [128, DM], BF16))

                s_wld = sem("p1_wld")
                s_mm1 = sem("p1_mm")
                s_ev1 = sem("p1_ev")
                s_zw = sem("p1_zw")
                s_cv = sem("p1_cv")
                s_cvs = sem("p1_cvs")
                s_xw = ppsem("p1_xw")
                s_xtw = sem("p1_xtw")
                s_tr1 = sem("p1_tr")
                s_te1 = sem("p1_te")
                s_ps1 = ppsem("p1_ps")
                s_pa1 = sem("p1_pa")
                s_dt = sem("p1_dt")
                s_dtt = sem("p1_dtt")
                s_att = sem("p1_att")
                s_ate = sem("p1_ate")
                s_sw = sem("p1_sw")
                NSLAB = 128

                @on("sync")
                def _(e, en):
                    if "nocp" not in _SKIP:
                        sem("p1_cb").dinc(e.dma_start(out=convw_sb[:, :], in_=convw))
                        sem("p1_cb").dinc(e.dma_start(out=convb_sb[:, :], in_=convb))
                    for i in range(NSLAB if "prescan" not in _SKIP else 0):
                        if i >= 2:
                            wait(e, en, s_pa1, i - 1)
                            wait(e, en, s_ps1[i % 2], 16 * (i // 2))
                        if i < 64:
                            src = w1T[i]
                        else:
                            j = i - 64
                            src = w2T[j // 4][:, (j % 4) * 16:(j % 4 + 1) * 16]
                        instr = e.dma_start(out=wf[i % 2][:, :].rearrange("p (k c) -> p k c", k=KD),
                                            in_=src)
                        ppissue(e, en, s_ps1, i, instr)

                if "nogpcast" not in _SKIP:
                    @on("gpsimd")
                    def _(e, en):
                        for ot in range(NOT_IN if "noload" not in _SKIP else 0):
                            if ot >= 1 and "nomm" not in _SKIP:
                                wait(e, en, s_mm1, 4 * ot)
                            instr = e.dma_start(out=wb[:, :].rearrange("p (k c) -> p k c", k=KD),
                                                in_=winT[ot])
                            s_wld.dinc(instr)
                else:
                    @on("gpsimd")
                    def _(e, en):
                        for ot in range(NOT_IN if "noload" not in _SKIP else 0):
                            if ot >= 1 and "nomm" not in _SKIP:
                                wait(e, en, sem("p1_wcst"), ot)
                            instr = e.dma_start(out=sq_scr[:, :].rearrange("p (k c) -> p k c", k=KD),
                                                in_=winT[ot])
                            s_wld.dinc(instr)

                    @on("vector")
                    def _(e, en):
                        for ot in range(NOT_IN if "nomm" not in _SKIP else 0):
                            wait(e, en, s_wld, 16 * (ot + 1))
                            if ot >= 1:
                                wait(e, en, s_mm1, 4 * ot)
                            instr = e.tensor_copy(wb[:, :], sq_scr[:, :])
                            sem("p1_wcst").inc(instr)

                @on("vector")
                def _(e, en):
                    for i in range(NSLAB if "prescan" not in _SKIP else 0):
                        ppwait(e, en, s_ps1, i)
                        col = 0 if i < 64 else 1
                        wait(e, en, s_pa1, i)
                        schain(e, en, e.tensor_reduce(acol[:, 2:3], wf[i % 2][:, :], AX.X, ALU.add,
                                        apply_absolute_value=True))
                        instr = e.tensor_tensor(absacc[:, col:col + 1], absacc[:, col:col + 1],
                                                acol[:, 2:3], ALU.add)
                        s_pa1.inc(instr)

                def pe_xpose(e, en, o):
                    wait(e, en, s_cvs, o + 1)
                    if o >= 1:
                        wait(e, en, s_te1, o)
                    for half, tgt in ((0, psA), (1, psB)):
                        for j in range(8):
                            k = half * 8 + j
                            instr = e.transpose(tgt[:, j * 128:(j + 1) * 128],
                                                xfm_t[o % 2][:, k * 128:(k + 1) * 128],
                                                identb_sb[:, :])
                        s_tr1.inc(instr)

                @on("tensor")
                def _(e, en):
                    for ot in range(NOT_IN if "nomm" not in _SKIP else 0):
                        wait(e, en, s_wld, 16 * (ot + 1))
                        for tg in range(4):
                            if ot >= 1:
                                wait(e, en, s_ev1, 4 * (ot - 1) + tg + 1)
                            for kt in range(KD):
                                instr = e.matmul(ps[tg][:, :],
                                                 wb[:, kt * 128:(kt + 1) * 128],
                                                 bass.AP(xn_fm, kt * L + tg * 512,
                                                         [[KD * L, 128], [1, 512]]),
                                                 start=(kt == 0), stop=(kt == KD - 1))
                            s_mm1.inc(instr)
                        if "conv" not in _SKIP and "nomm" not in _SKIP and ot >= 34:
                            pe_xpose(e, en, ot - 34)

                @on("scalar")
                def _(e, en):
                    for ot in range(NOT_Z if "nomm" not in _SKIP else 0):
                        for tg in range(4):
                            wait(e, en, s_mm1, 4 * ot + tg + 1)
                            if ot >= 1 and "zspill" not in _SKIP:
                                wait(e, en, s_zw, 16 * ot)
                            instr = e.activation(zst[:, tg * 512:(tg + 1) * 512], ps[tg][:, :], AF.Silu)
                            s_ev1.inc(instr)
                        if "zspill" not in _SKIP:
                            wait(e, en, s_ev1, 4 * (ot + 1))
                            instr = e.dma_start(out=z_sp[ot], in_=zst[:, :])
                            s_zw.dinc(instr)
                    for o in range(NOT_XBC if "conv" not in _SKIP else 0):
                        wait(e, en, s_cv, o + 1)
                        if o >= 2:
                            wait(e, en, s_te1, o - 1)
                            ppwait(e, en, s_xw, o - 2)
                        instr = e.activation(xfm_t[o % 2][:, :], convout[:, :], AF.Silu,
                                             bias=convb_sb[:, o:o + 1])
                        s_cvs.inc(instr)
                        if o < 32:
                            wait(e, en, s_cvs, o + 1)
                            if o >= 2:
                                wait(e, en, s_xw[o % 2], 16 * (o // 2))
                            instr = e.dma_start(out=xfm_sp[o], in_=xfm_t[o % 2][:, :])
                            ppissue(e, en, s_xw, o, instr)
                            wait(e, en, s_te1, o + 1)
                            wait(e, en, s_xtw, 16 * o)
                            instr = e.dma_start(out=xtm_sp[o], in_=trst[:, :])
                            s_xtw.dinc(instr)

                def v_tb_evac(e, en, o):
                    if o < 32:
                        wait(e, en, s_tr1, 2 * o + 2)
                        if o >= 1:
                            wait(e, en, s_xtw, 16 * o)
                        e.tensor_copy(trst[:, 0:1024], psA[:, :])
                        instr = e.tensor_copy(trst[:, 1024:2048], psB[:, :])
                        s_te1.inc(instr)
                    elif o == 32:
                        e.tensor_copy(B_fm[:, :], xfm_t[o % 2][:, :])
                        wait(e, en, s_tr1, 2 * 32 + 2)
                        e.tensor_copy(B_tm[:, 0:1024], psA[:, :])
                        instr = e.tensor_copy(B_tm[:, 1024:2048], psB[:, :])
                        s_te1.inc(instr)
                    else:
                        wait(e, en, s_cvs, 34)
                        instr = e.tensor_copy(C_fm[:, :], xfm_t[o % 2][:, :])
                        s_te1.inc(instr)

                @on("vector")
                def _(e, en):
                    schain(e, en, e.memset(convbuf[:, 0:3], 0.0))
                    for ot in range(NOT_Z, NOT_IN if "nomm" not in _SKIP else NOT_Z):
                        o = ot - NOT_Z
                        if ot == NOT_Z:
                            wait(e, en, s_ev1, 4 * NOT_Z)
                        if o >= 1:
                            wait(e, en, s_cv, o)
                        for tg in range(4):
                            wait(e, en, s_mm1, 4 * ot + tg + 1)
                            if ot < NOT_Z + NOT_XBC:
                                instr = e.tensor_copy(convbuf[:, 3 + tg * 512:3 + (tg + 1) * 512],
                                                      ps[tg][:, :])
                            else:
                                instr = e.tensor_copy(dtraw[:, tg * 512:(tg + 1) * 512], ps[tg][0:64, :])
                            s_ev1.inc(instr)
                        if ot < NOT_Z + NOT_XBC and "conv" not in _SKIP:
                            wait(e, en, sem("p1_cb"), 32)
                            wait(e, en, s_ev1, 4 * (ot + 1))
                            if o >= 1:
                                wait(e, en, s_cvs, o)
                            schain(e, en, e.tensor_scalar(convout[:, :], convbuf[:, 0:DM],
                                            convw_sb[:, o * 4:o * 4 + 1], None, ALU.mult))
                            for k in range(1, 4):
                                instr = e.scalar_tensor_tensor(convout[:, :], convbuf[:, k:k + DM],
                                                               convw_sb[:, o * 4 + k:o * 4 + k + 1],
                                                               convout[:, :], ALU.mult, ALU.add)
                                if k < 3:
                                    schain(e, en, instr)
                            s_cv.inc(instr)
                        if "conv" not in _SKIP and o >= 1 and (o - 1) <= 33:
                            v_tb_evac(e, en, o - 1)

                @on("scalar")
                def _(e, en):
                    if "dt" in _SKIP:
                        return
                    wait(e, en, s_ev1, 4 * NOT_IN)
                    schain(e, en, e.activation(scrA[0:64, :], dtraw[:, :], AF.Abs, bias=pcol[:, 1:2]))
                    schain(e, en, e.activation(scrA[0:64, :], scrA[0:64, :], AF.Exp, scale=-1.0))
                    schain(e, en, e.activation(scrA[0:64, :], scrA[0:64, :], AF.Ln, bias=1.0))
                    instr = e.activation(dt_fm[:, :], dtraw[:, :], AF.Relu, bias=pcol[:, 1:2])
                    s_dt.inc(instr)

                @on("vector")
                def _(e, en):
                    if "dt" in _SKIP:
                        return
                    wait(e, en, s_dt, 1)
                    schain(e, en, e.tensor_tensor(dt_fm[:, :], dt_fm[:, :], scrA[0:64, :], ALU.add))
                    wait(e, en, s_prep, 3)
                    schain(e, en, e.tensor_scalar(scrA[0:64, :], dt_fm[:, :], pcol[:, 2:3], None, ALU.mult))
                    for c in range(NCH):
                        instr = e.tensor_tensor_scan(acum_fm[:, c * Q:(c + 1) * Q], scrA[0:64, c * Q:(c + 1) * Q],
                                             scrA[0:64, c * Q:(c + 1) * Q], 0.0, ALU.add, ALU.bypass)
                    schain(e, en, instr)
                    schain(e, en, e.tensor_copy(acum_hi[:, :], acum_fm[:, :]))
                    schain(e, en, e.tensor_copy(scrA[0:64, :], acum_hi[:, :]))
                    instr = e.tensor_tensor(acum_lo[:, :], acum_fm[:, :], scrA[0:64, :], ALU.subtract)
                    s_dt.inc(instr)

                @on("tensor")
                def _(e, en):
                    if "dt" in _SKIP:
                        return
                    wait(e, en, s_dt, 2)
                    for i2 in range(2 * NT):
                        lt = i2 % NT
                        src = dt_fm if i2 < NT else acum_fm
                        bank = ps[4] if (i2 % 2 == 0) else ps[5]
                        if i2 >= 2:
                            wait(e, en, s_ate, i2 - 1)
                        instr = e.transpose(bank[:, 0:64], src[:, lt * 128:(lt + 1) * 128],
                                            identf_sb[0:64, 0:64])
                        s_att.inc(instr)

                @on("vector")
                def _(e, en):
                    if "dt" in _SKIP:
                        return
                    for i2 in range(2 * NT):
                        lt = i2 % NT
                        wait(e, en, s_att, i2 + 1)
                        bank = ps[4] if (i2 % 2 == 0) else ps[5]
                        if i2 < NT:
                            instr = e.tensor_copy(dt_tm[:, lt * 64:(lt + 1) * 64], bank[:, 0:64])
                        else:
                            e.tensor_copy(acum_tm[:, lt * 64:(lt + 1) * 64], bank[:, 0:64])
                            instr = e.tensor_scalar(nacum_tm[:, lt * 64:(lt + 1) * 64], bank[:, 0:64],
                                                    -1.0, None, ALU.mult)
                        s_ate.inc(instr)

                @on("tensor")
                def _(e, en):
                    if "prescan" in _SKIP or "dt" in _SKIP:
                        return
                    wait(e, en, s_pa1, NSLAB)
                    wait(e, en, s_ate, 2 * NT)
                    instr = e.transpose(ps[4][0:2, 64:192], absacc[:, 0:2], identf_sb[:, :])
                    s_sw.inc(instr)

                @on("vector")
                def _(e, en):
                    if "prescan" in _SKIP or "dt" in _SKIP:
                        return
                    wait(e, en, s_sw, 1)
                    schain(e, en, e.tensor_reduce(acol[0:2, 3:4], ps[4][0:2, 64:192], AX.X, ALU.add))
                    schain(e, en, e.tensor_scalar(acol[0:2, 3:4], acol[0:2, 3:4], 1.0 / (FF * DM), 1e-5,
                                    ALU.mult, ALU.max))
                    instr = e.reciprocal(acol[0:2, 3:4], acol[0:2, 3:4])
                    s_sw.inc(instr)

                @on("tensor")
                def _(e, en):
                    if "prescan" in _SKIP or "dt" in _SKIP:
                        return
                    wait(e, en, s_sw, 2)
                    e.matmul(ps[4][:, 66:67], sel2[:, 0:128], acol[0:2, 3:4], start=True, stop=False,
                             skip_group_check=True)
                    instr = e.matmul(ps[4][:, 67:68], sel2[:, 128:256], acol[0:2, 3:4],
                                     start=False, stop=True, skip_group_check=True)
                    s_sw.inc(instr)

                @on("vector")
                def _(e, en):
                    if "prescan" in _SKIP or "dt" in _SKIP:
                        return
                    wait(e, en, s_sw, 3)
                    schain(e, en, e.tensor_copy(swc[:, 0:1], ps[4][:, 66:67]))
                    schain(e, en, e.tensor_copy(swc[:, 1:2], ps[4][:, 67:68]))
                    schain(e, en, e.tensor_scalar(swc[:, 2:3], swc[:, 0:1], 127.0, None, ALU.mult))
                    schain(e, en, e.tensor_scalar(swc[:, 3:4], swc[:, 1:2], 127.0, None, ALU.mult))
                    schain(e, en, e.reciprocal(swc[:, 2:3], swc[:, 2:3]))
                    instr = e.reciprocal(swc[:, 3:4], swc[:, 3:4])
                    s_sw.inc(instr)

                barrier("p1", extra=(([("scalar", s_xw[0], 16 * 16), ("scalar", s_xw[1], 16 * 16),
                                      ("scalar", s_xtw, 16 * 32)]) if "conv" not in _SKIP else []) +
                        ([("scalar", s_zw, 16 * NOT_Z)] if "zspill" not in _SKIP else []))
                sc1m.close()
                scx.close()

                phase_mark(3)
                with contextlib.ExitStack() as sc2:
                    E2 = sc2.enter_context
                    xtm_ch = E2(nc.sbuf_tensor("xtm_ch", [128, 8192], BF16))
                    xfm_ch = E2(nc.sbuf_tensor("xfm_ch", [128, 8192], BF16))
                    zs_ch = E2(nc.sbuf_tensor("zs_ch", [128, 8192], BF16))
                    yg_sb = E2(nc.sbuf_tensor("yg_sb", [128, 8192], BF16))
                    y_sb = E2(nc.sbuf_tensor("y_sb", [128, 8192], BF16))
                    _achr_hi = E2(nc.sbuf_tensor("achr_hi", [1, 2048], BF16))
                    _achr_lo = E2(nc.sbuf_tensor("achr_lo", [1, 2048], BF16))
                    achr_hi = [_achr_hi, _achr_hi]
                    achr_lo = [_achr_lo, _achr_lo]
                    cbt = E2(nc.sbuf_tensor("cbt", [128, 512], F32))
                    scombs = [E2(nc.sbuf_tensor(f"scomb{i}", [128, 512], F32)) for i in range(2)]
                    esdo = E2(nc.sbuf_tensor("esdo", [128, 256], BF16))
                    arep_sb = E2(nc.sbuf_tensor("arep_sb", [128, 256], F32))
                    cbm = E2(nc.sbuf_tensor("cbm", [128, 512], BF16))
                    cbmt = E2(nc.sbuf_tensor("cbmt", [128, 512], BF16))
                    csdo = E2(nc.sbuf_tensor("csdo", [128, 256], BF16))
                    bdd = E2(nc.sbuf_tensor("bdd", [128, 256], BF16))
                    st_f = E2(nc.sbuf_tensor("st_f", [128, H * 64], F32))
                    st_hb = E2(nc.sbuf_tensor("st_hb", [128, 64], BF16))
                    etrep = E2(nc.sbuf_tensor("etrep", [128, 64], F32))
                    rrow = E2(nc.sbuf_tensor("rrow", [1, Q], F32))
                    rrow_b = E2(nc.sbuf_tensor("rrow_b", [1, Q], BF16))
                    rstd_rep = E2(nc.sbuf_tensor("rstd_rep", [128, Q], F32))
                    ygt = E2(nc.sbuf_tensor("ygt", [128, 256], F32))
                    et_hi = E2(nc.sbuf_tensor("et_hi", [1, 64], BF16))
                    et_lo = E2(nc.sbuf_tensor("et_lo", [1, 64], BF16))
                    lndt_ch = E2(nc.sbuf_tensor("lndt_ch", [128, 128], F32))

                    s_ch = sem("p2_ch")
                    s_row2 = sem("p2_row")
                    s_cb = sem("p2_cb")
                    s_cbe = sem("p2_cbe")
                    s_et = sem("p2_et")
                    s_dd = sem("p2_dd")
                    s_ddt = sem("p2_ddt")
                    s_dde = sem("p2_dde")
                    s_rep = sem("p2_rep")
                    s_cmb = sem("p2_cmb")
                    s_exp = sem("p2_exp")
                    s_cbml = sem("p2_cbm")
                    s_bddp = sem("p2_bdd")
                    s_csd = sem("p2_csd")
                    s_ymm = sem("p2_ymm")
                    s_smm = sem("p2_smm")
                    s_yev = sem("p2_yev")
                    s_stb = sem("p2_stb")
                    s_rec = sem("p2_rec")
                    s_y2 = sem("p2_y2")
                    s_rs = sem("p2_rs")
                    s_rr = sem("p2_rr")
                    s_gt = sem("p2_gt")
                    s_ygw = sem("p2_ygw")

                    @on("vector")
                    def _(e, en):
                        e.memset(st_f[:, :], 0.0)

                    def achr_load(e, en, c, g):
                        if c * 8 + g >= 1:
                            wait(e, en, s_exp, 16 * (c * 8 + g))
                        s_row2.dinc(e.dma_start(
                            out=achr_hi[g % 2][0:1, :].rearrange("o (h q) -> o h q", h=8),
                            in_=acum_hi[g * 8:(g + 1) * 8, c * Q:(c + 1) * Q]))
                        s_row2.dinc(e.dma_start(
                            out=achr_lo[g % 2][0:1, :].rearrange("o (h q) -> o h q", h=8),
                            in_=acum_lo[g * 8:(g + 1) * 8, c * Q:(c + 1) * Q]))

                    def gp_chunk(e, en, c):
                        if c >= 1:
                            wait(e, en, s_ymm, c * H)
                            wait(e, en, s_yev, c * H)
                            wait(e, en, s_ygw, 16 * c)
                            wait(e, en, s_gt, c)
                        for ot in range(32):
                            s_ch.dinc(e.dma_start(out=bass.AP(xtm_ch, ot * 128,
                                                  [[8192, 128], [4096, 2], [1, 128]]),
                                      in_=bass.AP(xtm_sp.tensor, ot * 128 * 2048 + 2 * c * 128,
                                                  [[2048, 128], [128, 2], [1, 128]])))
                            s_ch.dinc(e.dma_start(out=xfm_ch[:, ot * 256:(ot + 1) * 256],
                                      in_=xfm_sp[ot][:, c * Q:(c + 1) * Q]))
                            s_ch.dinc(e.dma_start(out=zs_ch[:, ot * 256:(ot + 1) * 256],
                                      in_=z_sp[ot][:, c * Q:(c + 1) * Q]))
                        for g in range(8):
                            achr_load(e, en, c, g)
                        wait(e, en, s_yev, (c + 1) * H)
                        instr = e.tensor_tensor(yg_sb[:, :], y_sb[:, :], y_sb[:, :], ALU.mult)
                        s_y2.inc(instr)

                    def pe_chunk(e, en, c):
                        if c >= 1:
                            wait(e, en, s_cbe, c)
                        for sblk in range(2):
                            instr = e.matmul(ps[0][:, sblk * 256:(sblk + 1) * 256],
                                             B_fm[:, c * Q + sblk * 128:c * Q + (sblk + 1) * 128],
                                             C_fm[:, c * Q:(c + 1) * Q], start=True, stop=True,
                                             skip_group_check=True)
                        s_cb.inc(instr)
                        wait(e, en, s_et, c)
                        instr = e.transpose(ps[1][0:1, 64:128],
                                            acum_fm[:, c * Q + Q - 1:c * Q + Q],
                                            identf_sb[0:64, 0:64])
                        s_cb.inc(instr)
                        wait(e, en, sem("p2_ehl"), c + 1)
                        e.matmul(ps[1][:, 0:64], ones_r_b[0:1, :], et_hi[0:1, :],
                                 start=True, stop=False, skip_group_check=True)
                        instr = e.matmul(ps[1][:, 0:64], ones_r_b[0:1, :], et_lo[0:1, :],
                                         start=False, stop=True, skip_group_check=True)
                        s_cb.inc(instr)
                        wait(e, en, s_dd, 2 * c + 2)
                        if c >= 1:
                            wait(e, en, s_dde, c)
                            wait(e, en, s_gt, c)
                        for j in range(2):
                            instr = e.transpose(ps[2][:, j * 64:(j + 1) * 64],
                                                dtdec_fm[:, j * 128:(j + 1) * 128],
                                                identf_sb[0:64, 0:64])
                        s_ddt.inc(instr)
                        def pe_rep(h):
                            hd = c * H + h
                            g = h // 8
                            hg = h % 8
                            rbank = ps[3] if h % 2 == 0 else ps[0]
                            if hd >= 2:
                                wait(e, en, s_cmb, hd - 1)
                            if h == 1:
                                wait(e, en, s_cbe, c + 1)
                            wait(e, en, s_row2, (c * 8 + g + 1) * 32)
                            e.matmul(rbank[:, 0:256], ones_r_b[0:1, :],
                                     achr_hi[g % 2][0:1, hg * Q:(hg + 1) * Q],
                                     start=True, stop=False, skip_group_check=True)
                            instr = e.matmul(rbank[:, 0:256], ones_r_b[0:1, :],
                                             achr_lo[g % 2][0:1, hg * Q:(hg + 1) * Q],
                                             start=False, stop=True, skip_group_check=True)
                            s_rep.inc(instr)

                        def pe_ydiag(h):
                            hd = c * H + h
                            wait(e, en, s_ch, (c + 1) * 16 * 96)
                            wait(e, en, s_cbml, hd + 1)
                            wait(e, en, s_csd, hd + 1)
                            wait(e, en, s_stb, hd + 1)
                            wait(e, en, s_bddp, hd + 1)
                            if hd >= 1:
                                wait(e, en, s_yev, hd)
                            e.matmul(ps[5][0:64, 0:256], xtm_ch[:, h * 64:(h + 1) * 64],
                                     cbmt[:, 0:256], start=True, stop=False)
                            e.matmul(ps[5][0:64, 0:256],
                                     xtm_ch[:, 4096 + h * 64:4096 + (h + 1) * 64],
                                     cbmt[:, 256:512], start=False, stop=False)
                            instr = e.matmul(ps[5][0:64, 0:256], st_hb[:, :], csdo[:, :],
                                             start=False, stop=True)
                            s_ymm.inc(instr)
                            if hd >= 1:
                                wait(e, en, s_rec, hd)
                            e.matmul(ps[4][:, 0:64], bdd[:, 0:128],
                                     xtm_ch[:, h * 64:(h + 1) * 64], start=True, stop=False)
                            instr = e.matmul(ps[4][:, 0:64], bdd[:, 128:256],
                                             xtm_ch[:, 4096 + h * 64:4096 + (h + 1) * 64],
                                             start=False, stop=True)
                            s_smm.inc(instr)

                        for h in range(H):
                            pe_rep(h)
                            if h >= 1:
                                pe_ydiag(h - 1)
                        pe_ydiag(H - 1)
                        wait(e, en, s_y2, c + 1)
                        if c >= 1:
                            wait(e, en, s_rs, 2 * c)
                        for ob in range(32):
                            instr = e.matmul(ps[2][0:1, 0:256], ones_c_b[:, 0:1],
                                             yg_sb[:, ob * 256:(ob + 1) * 256],
                                             start=(ob == 0), stop=(ob == 31),
                                             skip_group_check=True)
                        s_rs.inc(instr)
                        wait(e, en, s_rr, 3 * c + 3)
                        instr = e.matmul(ps[2][:, 256:512], ones_r_b[0:1, :], rrow_b[0:1, :],
                                         start=True, stop=True, skip_group_check=True)
                        s_rs.inc(instr)

                    def v_chunk(e, en, c):
                        wait(e, en, s_cb, 3 * c + 1)
                        instr = e.tensor_copy(cbt[:, :], ps[0][:, :])
                        s_cbe.inc(instr)
                        wait(e, en, s_cb, 3 * c + 2)
                        schain(e, en, e.tensor_copy(et_hi[0:1, :], ps[1][0:1, 64:128]))
                        instr = e.tensor_tensor(et_lo[0:1, :], ps[1][0:1, 64:128],
                                                et_hi[0:1, :], ALU.subtract)
                        sem("p2_ehl").inc(instr)
                        wait(e, en, s_dd, 2 * c + 1)
                        instr = e.tensor_tensor(dtdec_fm[:, :], dtdec_fm[:, :],
                                                dt_fm[:, c * Q:(c + 1) * Q], ALU.mult)
                        s_dd.inc(instr)
                        wait(e, en, s_ddt, c + 1)
                        instr = e.tensor_copy(dtdec_tm[:, :], ps[2][:, 0:128])
                        s_dde.inc(instr)
                        def v_a(h):
                            hd = c * H + h
                            rbank = ps[3] if h % 2 == 0 else ps[0]
                            wait(e, en, s_rep, hd + 1)
                            if hd >= 2:
                                wait(e, en, s_exp, 2 * (hd - 1))
                            for j in range(2):
                                schain(e, en, e.scalar_tensor_tensor(
                                    scombs[h % 2][:, j * 256:(j + 1) * 256], rbank[:, 0:256],
                                    nacum_tm[:, (2 * c + j) * 64 + h:(2 * c + j) * 64 + h + 1],
                                    mask_sb[:, j * 256:(j + 1) * 256], ALU.add, ALU.add))
                            if hd >= 1:
                                wait(e, en, s_exp, 2 * hd)
                            instr = e.tensor_copy(arep_sb[:, :], rbank[:, 0:256])
                            s_cmb.inc(instr)

                        def v_b(h):
                            hd = c * H + h
                            wait(e, en, s_exp, 2 * hd + 1)
                            wait(e, en, s_cbe, c + 1)
                            if hd >= 1:
                                wait(e, en, s_ymm, hd)
                            instr = e.tensor_tensor(cbmt[:, :], cbm[:, :], cbt[:, :], ALU.mult)
                            s_cbml.inc(instr)
                            wait(e, en, s_exp, 2 * hd + 2)
                            instr = e.tensor_tensor(csdo[:, :], C_fm[:, c * Q:(c + 1) * Q],
                                                    esdo[:, :], ALU.mult)
                            s_csd.inc(instr)
                            instr = e.tensor_copy(st_hb[:, :], st_f[:, h * 64:(h + 1) * 64])
                            s_stb.inc(instr)
                            wait(e, en, s_dde, c + 1)
                            if hd >= 1:
                                wait(e, en, s_smm, hd)
                            for j in range(2):
                                instr = e.tensor_scalar(bdd[:, j * 128:(j + 1) * 128],
                                                        B_tm[:, (2 * c + j) * 128:(2 * c + j + 1) * 128],
                                                        dtdec_tm[:, j * 64 + h:j * 64 + h + 1],
                                                        None, ALU.mult)
                            s_bddp.inc(instr)

                        def v_c(h):
                            hd = c * H + h
                            wait(e, en, s_ymm, hd + 1)
                            ot = h // 2
                            half = h % 2
                            instr = e.scalar_tensor_tensor(
                                y_sb[half * 64:(half + 1) * 64, ot * 256:(ot + 1) * 256],
                                xfm_ch[half * 64:(half + 1) * 64, ot * 256:(ot + 1) * 256],
                                d_rep[half * 64:(half + 1) * 64, h:h + 1],
                                ps[5][0:64, 0:256], ALU.mult, ALU.add)
                            s_yev.inc(instr)
                            wait(e, en, s_smm, hd + 1)
                            wait(e, en, s_et, c + 1)
                            instr = e.scalar_tensor_tensor(st_f[:, h * 64:(h + 1) * 64],
                                                           st_f[:, h * 64:(h + 1) * 64],
                                                           etrep[:, h:h + 1], ps[4][:, 0:64],
                                                           ALU.mult, ALU.add)
                            s_rec.inc(instr)

                        for h in range(H):
                            v_a(h)
                            if h >= 1:
                                v_b(h - 1)
                            if h >= 2:
                                v_c(h - 2)
                        v_b(H - 1)
                        v_c(H - 2)
                        v_c(H - 1)
                        wait(e, en, s_rs, 2 * c + 1)
                        instr = e.tensor_scalar(rrow[0:1, :], ps[2][0:1, 0:256], 1.0 / DIN,
                                                RMS_EPS, ALU.mult, ALU.add)
                        s_rr.inc(instr)
                        wait(e, en, s_rr, 3 * c + 2)
                        schain(e, en, e.reciprocal(rrow[0:1, :], rrow[0:1, :]))
                        instr = e.tensor_copy(rrow_b[0:1, :], rrow[0:1, :])
                        s_rr.inc(instr)
                        wait(e, en, s_rs, 2 * c + 2)
                        schain(e, en, e.tensor_copy(rstd_rep[:, :], ps[2][:, 256:512]))
                        for ob in range(32):
                            schain(e, en, e.scalar_tensor_tensor(ygt[:, :], y_sb[:, ob * 256:(ob + 1) * 256],
                                                   normw_sb[:, ob:ob + 1], rstd_rep[:, :],
                                                   ALU.mult, ALU.mult))
                            instr = e.tensor_tensor(yg_sb[:, ob * 256:(ob + 1) * 256], ygt[:, :],
                                                    zs_ch[:, ob * 256:(ob + 1) * 256], ALU.mult)
                            if ob < 31:
                                schain(e, en, instr)
                        s_gt.inc(instr)

                    def act_chunk(e, en, c):
                        if c >= 1:
                            wait(e, en, s_exp, 2 * c * H)
                        instr = e.activation(lndt_ch[:, :], dt_tm[:, 2 * c * 64:(2 * c + 2) * 64], AF.Ln)
                        sem("p2_ldt").inc(instr)
                        wait(e, en, s_cb, 3 * c + 3)
                        instr = e.activation(etrep[:, :], ps[1][:, 0:64], AF.Exp,
                                             bias=acol[:, 0:1])
                        s_et.inc(instr)
                        instr = e.activation(dtdec_fm[:, :], acum_fm[:, c * Q:(c + 1) * Q], AF.Exp,
                                             scale=-1.0, bias=acum_fm[:, c * Q + Q - 1:c * Q + Q])
                        s_dd.inc(instr)
                        for h in range(H):
                            hd = c * H + h
                            wait(e, en, s_cmb, hd + 1)
                            wait(e, en, sem("p2_ldt"), c + 1)
                            if hd >= 1:
                                wait(e, en, s_cbml, hd)
                            for j in range(2):
                                instr = e.activation(cbm[:, j * 256:(j + 1) * 256],
                                                     scombs[h % 2][:, j * 256:(j + 1) * 256], AF.Exp,
                                                     bias=lndt_ch[:, j * 64 + h:j * 64 + h + 1])
                                if j == 0:
                                    schain(e, en, instr)
                            s_exp.inc(instr)
                            if hd >= 1:
                                wait(e, en, s_csd, hd)
                            instr = e.activation(esdo[:, :], arep_sb[:, :], AF.Exp,
                                                 bias=acol[:, 0:1])
                            s_exp.inc(instr)
                        wait(e, en, s_rr, 3 * c + 1)
                        instr = e.activation(rrow[0:1, :], rrow[0:1, :], AF.Sqrt)
                        s_rr.inc(instr)
                        wait(e, en, s_gt, c + 1)
                        instr = e.dma_start(out=yg_sp[c], in_=yg_sb[:, :])
                        s_ygw.dinc(instr)

                    @on("gpsimd")
                    def _(e, en):
                        for c in range(NCH):
                            gp_chunk(e, en, c)

                    @on("tensor")
                    def _(e, en):
                        for c in range(NCH):
                            pe_chunk(e, en, c)

                    @on("vector")
                    def _(e, en):
                        for c in range(NCH):
                            v_chunk(e, en, c)

                    @on("scalar")
                    def _(e, en):
                        for c in range(NCH):
                            act_chunk(e, en, c)

                    barrier("p2", extra=[("scalar", s_ygw, 16 * NCH)])
                scspan.close()

        phase_mark(4)
        with contextlib.ExitStack() as scb:
            Eb = scb.enter_context
            wout_h = Eb(nc.sbuf_tensor("wout_h", [128, 32 * 1024], BF16))
            yg_in = [Eb(nc.sbuf_tensor(f"yg_in{i}", [128, 8192], BF16)) for i in range(2)]
            att_st = Eb(nc.sbuf_tensor("att_st", [128, 1024], BF16))
            wq_i = Eb(nc.sbuf_tensor("wq_ib", [128, 2048], I32))
            u2_tl = [Eb(nc.sbuf_tensor(f"u2_tl{i}", [128, 1024], F32)) for i in range(2)]

            s_wo = sem("pb_wo")
            s_ygl = ppsem("pb_ygl")
            s_u2 = ppsem("pb_u2")
            s_mm2 = sem("pb_mm")
            s_ev2 = sem("pb_ev")
            s_aw = sem("pb_aw")
            s_w2l = ppsem("pb_w2l")
            s_w2q = sem("pb_w2q")
            s_w2w = ppsem("pb_w2w")

            @on("gpsimd")
            def _(e, en):
                for half in range(2):
                    if half == 1:
                        wait(e, en, s_mm2, 32)
                    instr = e.dma_start(
                        out=wout_h[:, :].rearrange("p (k c2) -> p k c2", k=32),
                        in_=woutT.rearrange("(k p) o -> p k o", p=128)[:, :, half * 1024:(half + 1) * 1024])
                    s_wo.dinc(instr)
                    for c in range(NCH):
                        hc = half * 8 + c
                        if hc >= 2:
                            wait(e, en, s_mm2, 4 * (hc - 1))
                            wait(e, en, s_ygl[hc % 2], 16 * (hc // 2))
                        instr = e.dma_start(out=yg_in[c % 2][:, :], in_=yg_sp[c])
                        ppissue(e, en, s_ygl, hc, instr)

            def aacc_spill(e, en, ht):
                wait(e, en, s_ev2, 2 * (ht + 1))
                instr = e.dma_start(out=aacc_sp[ht % NT][:, (ht // NT) * 1024:(ht // NT + 1) * 1024],
                                    in_=att_st[:, :])
                s_aw.dinc(instr)

            @on("scalar")
            def _(e, en):
                for half in range(2):
                    for t in range(NT):
                        ht = half * NT + t
                        if ht >= 2:
                            wait(e, en, s_ev2, 2 * (ht - 1))
                            wait(e, en, s_u2[ht % 2], 16 * (ht // 2))
                        instr = e.dma_start(out=u2_tl[t % 2][:, :],
                                            in_=u[t * 128:(t + 1) * 128, half * 1024:(half + 1) * 1024])
                        ppissue(e, en, s_u2, ht, instr)
                        if ht >= 1:
                            aacc_spill(e, en, ht - 1)
                aacc_spill(e, en, 2 * NT - 1)

            @on("tensor")
            def _(e, en):
                for half in range(2):
                    wait(e, en, s_wo, 16 * (half + 1))
                    for c in range(NCH):
                        hc = half * 8 + c
                        ppwait(e, en, s_ygl, hc)
                        for tt in range(2):
                            tgl = half * 16 + c * 2 + tt
                            for og in range(2):
                                if tgl >= 1:
                                    wait(e, en, s_ev2, 2 * (tgl - 1) + og + 1)
                                for kt in range(32):
                                    instr = e.matmul(
                                        ps[og][:, :],
                                        yg_in[c % 2][:, kt * 256 + tt * 128:kt * 256 + tt * 128 + 128],
                                        bass.AP(wout_h, kt * 1024 + og * 512, [[32 * 1024, 128], [1, 512]]),
                                        start=(kt == 0), stop=(kt == 31))
                                s_mm2.inc(instr)

            @on("vector")
            def _(e, en):
                for half in range(2):
                    for t in range(NT):
                        ht = half * NT + t
                        ppwait(e, en, s_u2, ht)
                        if ht >= 1:
                            wait(e, en, s_aw, 16 * ht)
                        for og in range(2):
                            wait(e, en, s_mm2, 2 * ht + og + 1)
                            instr = e.tensor_tensor(att_st[:, og * 512:(og + 1) * 512], ps[og][:, :],
                                                    u2_tl[t % 2][:, og * 512:(og + 1) * 512], ALU.add)
                            s_ev2.inc(instr)

            @on("sync")
            def _(e, en):
                for pidx in range(64):
                    if pidx >= 2:
                        wait(e, en, s_w2q, pidx - 1)
                        wait(e, en, s_w2l[pidx % 2], 16 * (pidx // 2))
                    instr = e.dma_start(out=wf[pidx % 2][:, :].rearrange("p (k c2) -> p k c2", k=KD),
                                        in_=w2T[pidx // 4][:, (pidx % 4) * 16:(pidx % 4 + 1) * 16])
                    ppissue(e, en, s_w2l, pidx, instr)

            @on("vector")
            def _(e, en):
                for pidx in range(64):
                    ppwait(e, en, s_w2l, pidx)
                    if pidx >= 2:
                        ppwait(e, en, s_w2w, pidx - 2)
                    schain(e, en, e.tensor_scalar(sq_scr[:, :], wf[pidx % 2][:, :], swc[:, 1:2], None, ALU.mult))
                    schain(e, en, e.tensor_scalar(sq_scr[:, :], sq_scr[:, :], MAGICR, MAGICR, ALU.add, ALU.subtract))
                    schain(e, en, e.tensor_copy(wq_i[:, :], sq_scr[:, :]))
                    instr = e.tensor_scalar(wq_b[pidx % 2][:, :], wq_i[:, :], 1.0, -1.0,
                                            ALU.min, ALU.max)
                    s_w2q.inc(instr)

            @on("gpsimd")
            def _(e, en):
                for pidx in range(64):
                    wait(e, en, s_w2q, pidx + 1)
                    if pidx >= 2:
                        wait(e, en, s_w2w[pidx % 2], 16 * (pidx // 2))
                    instr = e.dma_start(out=w2q_sp[pidx // 4][:, (pidx % 4) * 2048:(pidx % 4 + 1) * 2048],
                                        in_=wq_b[pidx % 2][:, :])
                    ppissue(e, en, s_w2w, pidx, instr)

            barrier("p2b", extra=[("scalar", s_aw, 16 * 2 * NT), ("gpsimd", s_w2w[0], 16 * 32),
                                  ("gpsimd", s_w2w[1], 16 * 32)])

        phase_mark(5)
        s_smt = sem("p3_smt")

        with contextlib.ExitStack() as sc3:
            E3 = sc3.enter_context
            xq1_fm = E3(nc.sbuf_tensor("xq1_fm", [128, KD * L], BF16))
            h_t = E3(nc.sbuf_tensor("h_t", [128, 2048], F32))
            wq_i = E3(nc.sbuf_tensor("wq_i3", [128, 2048], I32))
            h_b = [E3(nc.sbuf_tensor(f"h_b{i}", [128, 2048], BF16)) for i in range(2)]
            hsq = E3(nc.sbuf_tensor("hsq", [128, 2048], BF16))
            inv1_rep = E3(nc.sbuf_tensor("inv1_rep", [128, 2048], F32))
            rhf = E3(nc.sbuf_tensor("rhf", [1, 2048], F32))
            rha = E3(nc.sbuf_tensor("rha", [1, 2048], F32))
            rhb = E3(nc.sbuf_tensor("rhb", [1, 2048], BF16))
            rlb = E3(nc.sbuf_tensor("rlb", [1, 2048], BF16))

            s_xq1l = sem("p3_xq1l")
            s_w1l = ppsem("p3_w1l")
            s_w1q = sem("p3_w1q")
            s_iv = sem("p3_iv")
            s_ivr = sem("p3_ivr")
            s_mm3 = sem("p3_mm")
            s_hv = sem("p3_hv")
            s_hs = sem("p3_hs")
            s_hw = ppsem("p3_hw")
            s_hsq = sem("p3_hsq")

            @on("gpsimd")
            def _(e, en):
                instr = e.dma_start(out=xq1_fm[:, :].rearrange("p (k t) -> p k t", k=KD),
                                    in_=xq1_sp.rearrange("k p t -> p k t"))
                s_xq1l.dinc(instr)

            @on("sync")
            def _(e, en):
                for ot in range(NOT_F):
                    if ot >= 2:
                        wait(e, en, s_w1q, ot - 1)
                        wait(e, en, s_w1l[ot % 2], 16 * (ot // 2))
                    instr = e.dma_start(out=wf[ot % 2][:, :].rearrange("p (k c2) -> p k c2", k=KD),
                                        in_=w1T[ot])
                    ppissue(e, en, s_w1l, ot, instr)

            @on("vector")
            def _(e, en):
                instr = e.tensor_scalar(ctmp[:, :], camax[:, :], swc[:, 2:3], None, ALU.mult)
                s_iv.inc(instr)

            @on("tensor")
            def _(e, en):
                wait(e, en, s_iv, 1)
                instr = e.transpose(ps[4][0:16, 0:128], ctmp[:, 0:16], identf_sb[:, :])
                s_iv.inc(instr)

            @on("vector")
            def _(e, en):
                wait(e, en, s_iv, 2)
                instr = e.tensor_copy(sq_scr[0:16, 0:128], ps[4][0:16, 0:128])
                s_iv.inc(instr)

            @on("gpsimd")
            def _(e, en):
                wait(e, en, s_iv, 3)
                instr = e.dma_start(out=rhf[0:1, :].rearrange("o (t p) -> o t p", t=16),
                                    in_=sq_scr[0:16, 0:128])
                sem("p3_ivd").inc(instr, 16)

            @on("vector")
            def _(e, en):
                wait(e, en, sem("p3_ivd"), 16)
                schain(e, en, e.tensor_copy(rhb[0:1, :], rhf[0:1, :]))
                schain(e, en, e.tensor_copy(rha[0:1, :], rhb[0:1, :]))
                schain(e, en, e.tensor_tensor(rha[0:1, :], rhf[0:1, :], rha[0:1, :], ALU.subtract))
                instr = e.tensor_copy(rlb[0:1, :], rha[0:1, :])
                s_iv.inc(instr)

            @on("tensor")
            def _(e, en):
                wait(e, en, s_iv, 4)
                for tg in range(4):
                    if tg >= 1:
                        wait(e, en, s_ivr, tg)
                    e.matmul(ps[4][:, 0:512], ones_r_b[0:1, :], rhb[0:1, tg * 512:(tg + 1) * 512],
                             start=True, stop=False, skip_group_check=True)
                    instr = e.matmul(ps[4][:, 0:512], ones_r_b[0:1, :], rlb[0:1, tg * 512:(tg + 1) * 512],
                                     start=False, stop=True, skip_group_check=True)
                    s_iv.inc(instr)

            @on("vector")
            def _(e, en):
                for tg in range(4):
                    wait(e, en, s_iv, 5 + tg)
                    instr = e.tensor_copy(inv1_rep[:, tg * 512:(tg + 1) * 512], ps[4][:, 0:512])
                    s_ivr.inc(instr)

            def v_hevac(e, en, ot):
                if ot >= 1:
                    wait(e, en, s_hs, ot)
                for tg in range(4):
                    wait(e, en, s_mm3, 4 * ot + tg + 1)
                    wait(e, en, s_ivr, 4)
                    instr = e.scalar_tensor_tensor(h_t[:, tg * 512:(tg + 1) * 512], ps[tg][:, :],
                                                   acol[:, 0:1], inv1_rep[:, tg * 512:(tg + 1) * 512],
                                                   ALU.bypass, ALU.mult)
                    s_hv.inc(instr)

            @on("vector")
            def _(e, en):
                for ot in range(NOT_F):
                    ppwait(e, en, s_w1l, ot)
                    if ot >= 2:
                        wait(e, en, s_mm3, 4 * (ot - 1))
                    schain(e, en, e.tensor_scalar(sq_scr[:, :], wf[ot % 2][:, :], swc[:, 0:1], None, ALU.mult))
                    schain(e, en, e.tensor_scalar(sq_scr[:, :], sq_scr[:, :], MAGICR, MAGICR, ALU.add, ALU.subtract))
                    schain(e, en, e.tensor_copy(wq_i[:, :], sq_scr[:, :]))
                    instr = e.tensor_scalar(wq_b[ot % 2][:, :], wq_i[:, :], 1.0, -1.0, ALU.min, ALU.max)
                    s_w1q.inc(instr)
                    if ot >= 1:
                        v_hevac(e, en, ot - 1)
                v_hevac(e, en, NOT_F - 1)

            @on("tensor")
            def _(e, en):
                wait(e, en, s_xq1l, 16)
                for ot in range(NOT_F):
                    wait(e, en, s_w1q, ot + 1)
                    for tg in range(4):
                        if ot >= 1:
                            wait(e, en, s_hv, 4 * (ot - 1) + tg + 1)
                        for kt in range(KD):
                            instr = e.matmul(ps[tg][:, :],
                                             wq_b[ot % 2][:, kt * 128:(kt + 1) * 128],
                                             bass.AP(xq1_fm, kt * L + tg * 512, [[KD * L, 128], [1, 512]]),
                                             start=(kt == 0), stop=(kt == KD - 1))
                        s_mm3.inc(instr)
                    lag = ot - 2
                    if lag >= 0:
                        wait(e, en, s_hsq, lag + 1)
                        for tg in range(4):
                            e.matmul(ps[4][32 * tg:32 * tg + 1, 0:512], ones_c_b[:, 0:1],
                                     h_b[lag % 2][:, tg * 512:(tg + 1) * 512],
                                     start=(lag == 0), stop=(lag == NOT_F - 1),
                                     tile_position=(0, 32 * tg), skip_group_check=True)
                            instr = e.matmul(ps[5][32 * tg:32 * tg + 1, 0:512], ones_c_b[:, 0:1],
                                             hsq[:, tg * 512:(tg + 1) * 512],
                                             start=(lag == 0), stop=(lag == NOT_F - 1),
                                             tile_position=(0, 32 * tg), skip_group_check=True)
                        s_smt.inc(instr)
                for lag in (NOT_F - 2, NOT_F - 1):
                    wait(e, en, s_hsq, lag + 1)
                    for tg in range(4):
                        e.matmul(ps[4][32 * tg:32 * tg + 1, 0:512], ones_c_b[:, 0:1],
                                 h_b[lag % 2][:, tg * 512:(tg + 1) * 512],
                                 start=False, stop=(lag == NOT_F - 1),
                                 tile_position=(0, 32 * tg), skip_group_check=True)
                        instr = e.matmul(ps[5][32 * tg:32 * tg + 1, 0:512], ones_c_b[:, 0:1],
                                         hsq[:, tg * 512:(tg + 1) * 512],
                                         start=False, stop=(lag == NOT_F - 1),
                                         tile_position=(0, 32 * tg), skip_group_check=True)
                    s_smt.inc(instr)

            @on("scalar")
            def _(e, en):
                for ot in range(NOT_F):
                    wait(e, en, s_hv, 4 * (ot + 1))
                    if ot >= 2:
                        ppwait(e, en, s_hw, ot - 2)
                        wait(e, en, s_smt, ot - 1)
                    schain(e, en, e.activation(h_t[:, :], h_t[:, :], AF.Silu, bias=bcols[:, ot:ot + 1]))
                    instr = e.activation(h_b[ot % 2][:, :], h_t[:, :], AF.Gelu)
                    s_hs.inc(instr)
                    wait(e, en, s_hs, ot + 1)
                    instr = e.dma_start(out=h_sp[ot], in_=h_b[ot % 2][:, :])
                    ppissue(e, en, s_hw, ot, instr)

            @on("gpsimd")
            def _(e, en):
                for ot in range(NOT_F):
                    wait(e, en, s_hs, ot + 1)
                    if ot >= 1:
                        wait(e, en, s_smt, ot)
                    instr = e.tensor_tensor(hsq[:, :], h_b[ot % 2][:, :], h_b[ot % 2][:, :], ALU.mult)
                    s_hsq.inc(instr)

            barrier("p3a", extra=[("scalar", s_hw[0], 16 * NOT_F // 2),
                                  ("scalar", s_hw[1], 16 * NOT_F // 2)])

        phase_mark(6)
        row_c2h = E(nc.sbuf_tensor("row_c2h", [1, 2048], BF16))
        row_c2l = E(nc.sbuf_tensor("row_c2l", [1, 2048], BF16))
        row_iv2b = E(nc.sbuf_tensor("row_iv2b", [1, 2048], BF16))

        with contextlib.ExitStack() as sc4:
            E4 = sc4.enter_context
            lnh_t = [E4(nc.sbuf_tensor(f"lnh_t{i}", [128, 2048], BF16)) for i in range(2)]
            stg4 = E4(nc.sbuf_tensor("stg4", [128, 1024], F32))
            t_bf = E4(nc.sbuf_tensor("t_bf", [128, 2048], BF16))
            mu_rep = E4(nc.sbuf_tensor("mu_rep", [128, 2048], BF16))
            rs3_rep = E4(nc.sbuf_tensor("rs3_rep", [128, 2048], BF16))
            rmax = E4(nc.sbuf_tensor("rmax", [128, 2048], F32))
            hsq2 = E4(nc.sbuf_tensor("hsq2", [128, 2048], BF16))
            row_mu = E4(nc.sbuf_tensor("row_mu", [1, 2048], F32))
            row_v = E4(nc.sbuf_tensor("row_v", [1, 2048], F32))
            rhb2 = E4(nc.sbuf_tensor("rhb2", [1, 2048], BF16))
            rlb2 = E4(nc.sbuf_tensor("rlb2", [1, 2048], BF16))
            rsc = E4(nc.sbuf_tensor("rsc", [1, 2048], F32))

            s_r3 = sem("p4_r")
            s_mr = sem("p4_mr")
            s_hl = ppsem("p4_hl")
            s_ln = sem("p4_ln")
            s_lnw = ppsem("p4_lnw")
            s_sq2 = sem("p4_sq2")
            s_smt2 = sem("p4_smt2")
            s_s2e = sem("p4_s2e")
            s_q2 = sem("p4_q2")

            @on("vector")
            def _(e, en):
                wait(e, en, s_smt, NOT_F)
                for g in range(4):
                    schain(e, en, e.tensor_copy(stg4[32 * g:32 * g + 1, 0:512],
                                  ps[4][32 * g:32 * g + 1, 0:512]))
                    instr = e.tensor_copy(stg4[32 * g:32 * g + 1, 512:1024],
                                          ps[5][32 * g:32 * g + 1, 0:512])
                    if g < 3:
                        schain(e, en, instr)
                sem("p4_stg").inc(instr)
                wait(e, en, sem("p4_flt"), 32)
                schain(e, en, e.tensor_scalar(row_mu[0:1, :], row_mu[0:1, :], 1.0 / FF, None, ALU.mult))
                schain(e, en, e.tensor_scalar(row_v[0:1, :], row_v[0:1, :], 1.0 / FF, None, ALU.mult))
                schain(e, en, e.tensor_tensor(rsc[0:1, :], row_mu[0:1, :], row_mu[0:1, :], ALU.mult))
                schain(e, en, e.tensor_tensor(row_v[0:1, :], row_v[0:1, :], rsc[0:1, :], ALU.subtract))
                instr = e.tensor_scalar(row_v[0:1, :], row_v[0:1, :], 1.0, LN_EPS, ALU.mult, ALU.add)
                s_r3.inc(instr)

            @on("gpsimd")
            def _(e, en):
                wait(e, en, sem("p4_stg"), 1)
                sem("p4_flt").dinc(e.dma_start(out=row_mu[0:1, :].rearrange("o (g c) -> o g c", g=4),
                                   in_=bass.AP(stg4, 0, [[32 * 1024, 4], [1, 512]])))
                sem("p4_flt").dinc(e.dma_start(out=row_v[0:1, :].rearrange("o (g c) -> o g c", g=4),
                                   in_=bass.AP(stg4, 512, [[32 * 1024, 4], [1, 512]])))

            @on("scalar")
            def _(e, en):
                wait(e, en, s_r3, 1)
                instr = e.activation(row_v[0:1, :], row_v[0:1, :], AF.Sqrt)
                s_r3.inc(instr)

            @on("vector")
            def _(e, en):
                wait(e, en, s_r3, 2)
                schain(e, en, e.reciprocal(row_v[0:1, :], row_v[0:1, :]))
                schain(e, en, e.tensor_copy(rhb2[0:1, :], row_mu[0:1, :]))
                schain(e, en, e.tensor_copy(rsc[0:1, :], rhb2[0:1, :]))
                schain(e, en, e.tensor_tensor(rsc[0:1, :], row_mu[0:1, :], rsc[0:1, :], ALU.subtract))
                schain(e, en, e.tensor_copy(rlb2[0:1, :], rsc[0:1, :]))
                instr = e.nop()
                s_r3.inc(instr)

            @on("tensor")
            def _(e, en):
                wait(e, en, s_r3, 3)
                for tg in range(4):
                    if tg >= 1:
                        wait(e, en, s_mr, tg)
                    e.matmul(ps[4][:, 0:512], ones_r_b[0:1, :], rhb2[0:1, tg * 512:(tg + 1) * 512],
                             start=True, stop=False, skip_group_check=True)
                    instr = e.matmul(ps[4][:, 0:512], ones_r_b[0:1, :], rlb2[0:1, tg * 512:(tg + 1) * 512],
                                     start=False, stop=True, skip_group_check=True)
                    s_r3.inc(instr)

            @on("vector")
            def _(e, en):
                for tg in range(4):
                    wait(e, en, s_r3, 4 + tg)
                    instr = e.tensor_scalar(mu_rep[:, tg * 512:(tg + 1) * 512], ps[4][:, 0:512],
                                            -1.0, None, ALU.mult)
                    s_mr.inc(instr)
                instr = e.tensor_copy(rhb2[0:1, :], row_v[0:1, :])
                s_mr.inc(instr)

            @on("tensor")
            def _(e, en):
                wait(e, en, s_mr, 5)
                for tg in range(4):
                    if tg >= 1:
                        wait(e, en, s_mr, 5 + tg)
                    instr = e.matmul(ps[4][:, 0:512], ones_r_b[0:1, :], rhb2[0:1, tg * 512:(tg + 1) * 512],
                                     start=True, stop=True, skip_group_check=True)
                    s_r3.inc(instr)

            @on("vector")
            def _(e, en):
                for tg in range(4):
                    wait(e, en, s_r3, 8 + tg)
                    instr = e.tensor_copy(rs3_rep[:, tg * 512:(tg + 1) * 512], ps[4][:, 0:512])
                    s_mr.inc(instr)

            @on("sync")
            def _(e, en):
                for ot in range(NOT_F):
                    if ot >= 2:
                        ppwait(e, en, s_lnw, ot - 2)
                        wait(e, en, s_sq2, ot - 1)
                        wait(e, en, s_hl[ot % 2], 16 * (ot // 2))
                    instr = e.dma_start(out=lnh_t[ot % 2][:, :], in_=h_sp[ot])
                    ppissue(e, en, s_hl, ot, instr)

            @on("vector")
            def _(e, en):
                schain(e, en, e.memset(rmax[:, :], 0.0))
                wait(e, en, s_mr, 9)
                for ot in range(NOT_F):
                    ppwait(e, en, s_hl, ot)
                    if ot >= 1:
                        wait(e, en, s_ln, ot)
                    schain(e, en, e.tensor_tensor(t_bf[:, :], lnh_t[ot % 2][:, :], mu_rep[:, :], ALU.add))
                    schain(e, en, e.tensor_tensor(t_bf[:, :], t_bf[:, :], rs3_rep[:, :], ALU.mult))
                    instr = e.tensor_scalar(lnh_t[ot % 2][:, :], t_bf[:, :], gcols[:, ot:ot + 1],
                                            lbcols[:, ot:ot + 1], ALU.mult, ALU.add)
                    s_ln.inc(instr)
                    if ot >= 1:
                        wait(e, en, s_sq2, ot)
                        wait(e, en, sem("p4_rmx"), ot - 1)
                        instr = e.tensor_tensor(rmax[:, :], rmax[:, :], hsq2[:, :], ALU.max)
                        sem("p4_rmx").inc(instr)
                wait(e, en, s_sq2, NOT_F)
                wait(e, en, sem("p4_rmx"), NOT_F - 1)
                instr = e.tensor_tensor(rmax[:, :], rmax[:, :], hsq2[:, :], ALU.max)
                sem("p4_rmxf").inc(instr)

            @on("scalar")
            def _(e, en):
                for ot in range(NOT_F):
                    wait(e, en, s_ln, ot + 1)
                    if ot >= 2:
                        wait(e, en, s_lnw[ot % 2], 16 * (ot // 2))
                    instr = e.dma_start(out=lnh_sp[ot], in_=lnh_t[ot % 2][:, :])
                    ppissue(e, en, s_lnw, ot, instr)

            @on("gpsimd")
            def _(e, en):
                for ot in range(NOT_F):
                    wait(e, en, s_ln, ot + 1)
                    if ot >= 1:
                        wait(e, en, s_smt2, ot)
                        wait(e, en, sem("p4_rmx"), ot)
                    instr = e.tensor_tensor(hsq2[:, :], lnh_t[ot % 2][:, :], lnh_t[ot % 2][:, :],
                                            ALU.mult)
                    s_sq2.inc(instr)

            @on("tensor")
            def _(e, en):
                for ot in range(NOT_F):
                    wait(e, en, s_sq2, ot + 1)
                    for tg in range(4):
                        instr = e.matmul(ps[5][32 * tg:32 * tg + 1, 0:512], ones_c_b[:, 0:1],
                                         hsq2[:, tg * 512:(tg + 1) * 512],
                                         start=(ot == 0), stop=(ot == NOT_F - 1),
                                         tile_position=(0, 32 * tg), skip_group_check=True)
                    s_smt2.inc(instr)

            @on("vector")
            def _(e, en):
                wait(e, en, sem("p4_rmxf"), 1)
                instr = e.nop()
                s_q2.inc(instr)

            @on("tensor")
            def _(e, en):
                wait(e, en, s_q2, 1)
                for j in range(NT):
                    bank = ps[2] if j % 2 == 0 else ps[3]
                    if j >= 2:
                        wait(e, en, sem("p4_rme"), j - 1)
                    instr = e.transpose(bank[:, 0:128], rmax[:, j * 128:(j + 1) * 128],
                                        identf_sb[:, :])
                    sem("p4_rmt").inc(instr)

            @on("vector")
            def _(e, en):
                for j in range(NT):
                    wait(e, en, sem("p4_rmt"), j + 1)
                    bank = ps[2] if j % 2 == 0 else ps[3]
                    instr = e.tensor_reduce(ctmp[:, j:j + 1], bank[:, 0:128], AX.X, ALU.max)
                    sem("p4_rme").inc(instr)

            @on("tensor")
            def _(e, en):
                wait(e, en, sem("p4_rme"), NT)
                instr = e.transpose(ps[2][0:16, 0:128], ctmp[:, 0:16], identf_sb[:, :])
                sem("p4_rmt").inc(instr)

            @on("vector")
            def _(e, en):
                wait(e, en, sem("p4_rmt"), NT + 1)
                instr = e.tensor_copy(stg4[0:16, 0:128], ps[2][0:16, 0:128])
                sem("p4_rme").inc(instr)

            @on("gpsimd")
            def _(e, en):
                wait(e, en, sem("p4_rme"), NT + 1)
                instr = e.dma_start(out=row_v[0:1, :].rearrange("o (t p) -> o t p", t=16),
                                    in_=stg4[0:16, 0:128])
                sem("p4_flt3").inc(instr, 16)

            @on("vector")
            def _(e, en):
                wait(e, en, s_smt2, NOT_F)
                for g in range(4):
                    instr = e.tensor_copy(stg4[32 * g:32 * g + 1, 512:1024],
                                          ps[5][32 * g:32 * g + 1, 0:512])
                sem("p4_stg2").inc(instr)
                wait(e, en, sem("p4_flt2"), 16)
                instr = e.tensor_scalar(rsc[0:1, :], rsc[0:1, :], 1.0 / FF, LN_EPS, ALU.mult, ALU.add)
                s_q2.inc(instr)

            @on("gpsimd")
            def _(e, en):
                wait(e, en, sem("p4_stg2"), 1)
                instr = e.dma_start(out=rsc[0:1, :].rearrange("o (g c) -> o g c", g=4),
                                    in_=bass.AP(stg4, 512, [[32 * 1024, 4], [1, 512]]))
                sem("p4_flt2").inc(instr, 16)

            @on("scalar")
            def _(e, en):
                wait(e, en, s_q2, 2)
                instr = e.activation(rsc[0:1, :], rsc[0:1, :], AF.Sqrt)
                s_q2.inc(instr)

            @on("scalar")
            def _(e, en):
                wait(e, en, sem("p4_flt3"), 16)
                instr = e.activation(row_v[0:1, :], row_v[0:1, :], AF.Sqrt)
                sem("p4_rmsq").inc(instr)

            @on("vector")
            def _(e, en):
                wait(e, en, s_q2, 3)
                wait(e, en, sem("p4_rmsq"), 1)
                schain(e, en, e.reciprocal(rsc[0:1, :], rsc[0:1, :]))
                schain(e, en, e.tensor_tensor(row_mu[0:1, :], row_v[0:1, :], rsc[0:1, :], ALU.mult))
                schain(e, en, e.tensor_scalar(row_mu[0:1, :], row_mu[0:1, :], 1e-5, None, ALU.max))
                schain(e, en, e.tensor_scalar(row_v[0:1, :], row_mu[0:1, :], swc[0:1, 3:4], None, ALU.mult))
                schain(e, en, e.tensor_copy(row_iv2b[0:1, :], row_v[0:1, :]))
                schain(e, en, e.reciprocal(row_mu[0:1, :], row_mu[0:1, :]))
                schain(e, en, e.tensor_scalar(row_mu[0:1, :], row_mu[0:1, :], 127.0, None, ALU.mult))
                schain(e, en, e.tensor_tensor(row_mu[0:1, :], row_mu[0:1, :], rsc[0:1, :], ALU.mult))
                schain(e, en, e.tensor_copy(row_c2h[0:1, :], row_mu[0:1, :]))
                schain(e, en, e.tensor_copy(rsc[0:1, :], row_c2h[0:1, :]))
                schain(e, en, e.tensor_tensor(rsc[0:1, :], row_mu[0:1, :], rsc[0:1, :], ALU.subtract))
                instr = e.tensor_copy(row_c2l[0:1, :], rsc[0:1, :])
                s_q2.inc(instr)

            barrier("p3b", extra=[("scalar", s_lnw[0], 16 * NOT_F // 2),
                                  ("scalar", s_lnw[1], 16 * NOT_F // 2),
                                  ("vector", s_q2, 4)])

        phase_mark(7)
        with contextlib.ExitStack() as sc5:
            E5 = sc5.enter_context
            hq_sb = E5(nc.sbuf_tensor("hq_sb", [128, KF * 512], BF16))
            lnh_ins = [E5(nc.sbuf_tensor(f"lnh_in{i}", [128, 512], BF16)) for i in range(2)]
            hq_f = E5(nc.sbuf_tensor("hq_f", [128, 512], F32))
            hq_i = E5(nc.sbuf_tensor("hq_i", [128, 512], I32))
            c2_rep = E5(nc.sbuf_tensor("c2_rep", [128, 2048], F32))
            inv2_rep = E5(nc.sbuf_tensor("inv2_rep", [128, 2048], F32))
            w2qb = [E5(nc.sbuf_tensor(f"w2qb{i}", [128, KF * 128], BF16)) for i in range(2)]
            oasm = E5(nc.sbuf_tensor("oasm", [128, 4 * 2048], BF16))
            fout = E5(nc.sbuf_tensor("fout", [128, 2048], F32))
            _aacc = E5(nc.sbuf_tensor("aacc_in", [128, DM], BF16))
            aacc_in = [_aacc, _aacc]
            fo_b = E5(nc.sbuf_tensor("fo_b", [128, 512], BF16))

            s_c2p = sem("p5_c2")
            s_c2e = sem("p5_c2e")
            s_hql = sem("p5_hql")
            s_hq = sem("p5_hq")
            s_w2r = ppsem("p5_w2r")
            s_mm4 = sem("p5_mm")
            s_fv = sem("p5_fv")
            s_ftr = sem("p5_ftr")
            s_oa = sem("p5_oa")
            s_aal = sem("p5_aal")
            s_fadd = sem("p5_fadd")
            s_outw = sem("p5_out")

            @on("tensor")
            def _(e, en):
                s_c2p.inc(e.nop())
                for tg in range(4):
                    if tg >= 1:
                        wait(e, en, s_c2e, tg)
                    e.matmul(ps[4][:, 0:512], ones_r_b[0:1, :], row_c2h[0:1, tg * 512:(tg + 1) * 512],
                             start=True, stop=False, skip_group_check=True)
                    instr = e.matmul(ps[4][:, 0:512], ones_r_b[0:1, :], row_c2l[0:1, tg * 512:(tg + 1) * 512],
                                     start=False, stop=True, skip_group_check=True)
                    s_c2p.inc(instr)
                for tg in range(4):
                    wait(e, en, s_c2e, 4 + tg)
                    instr = e.matmul(ps[4][:, 0:512], ones_r_b[0:1, :], row_iv2b[0:1, tg * 512:(tg + 1) * 512],
                                     start=True, stop=True, skip_group_check=True)
                    s_c2p.inc(instr)

            @on("vector")
            def _(e, en):
                for tg in range(4):
                    wait(e, en, s_c2p, 2 + tg)
                    instr = e.tensor_copy(c2_rep[:, tg * 512:(tg + 1) * 512], ps[4][:, 0:512])
                    s_c2e.inc(instr)
                for tg in range(4):
                    wait(e, en, s_c2p, 6 + tg)
                    instr = e.tensor_copy(inv2_rep[:, tg * 512:(tg + 1) * 512], ps[4][:, 0:512])
                    s_c2e.inc(instr)

            for tg in range(4):
                @on("sync")
                def _(e, en, tg=tg):
                    for kt in range(KF):
                        i = tg * KF + kt
                        if i >= 2:
                            wait(e, en, s_hq, i - 1)
                        instr = e.dma_start(out=lnh_ins[i % 2][:, :],
                                            in_=lnh_sp[kt][:, tg * 512:(tg + 1) * 512])
                        s_hql.dinc(instr)

                @on("vector")
                def _(e, en, tg=tg):
                    wait(e, en, s_c2e, 8)
                    if tg >= 1:
                        wait(e, en, s_fv, tg * OT2)
                    for kt in range(KF):
                        i = tg * KF + kt
                        wait(e, en, s_hql, 16 * (i + 1))
                        schain(e, en, e.tensor_tensor(hq_f[:, :], lnh_ins[i % 2][:, :],
                                        c2_rep[:, tg * 512:(tg + 1) * 512], ALU.mult))
                        schain(e, en, e.tensor_scalar(hq_f[:, :], hq_f[:, :], MAGICR, MAGICR, ALU.add, ALU.subtract))
                        schain(e, en, e.tensor_copy(hq_i[:, :], hq_f[:, :]))
                        instr = e.tensor_copy(hq_sb[:, kt * 512:(kt + 1) * 512], hq_i[:, :])
                        s_hq.inc(instr)

                @on("gpsimd")
                def _(e, en, tg=tg):
                    for o in range(OT2):
                        i = tg * OT2 + o
                        if i >= 2:
                            wait(e, en, s_mm4, i - 1)
                            wait(e, en, s_w2r[i % 2], 16 * (i // 2))
                        instr = e.dma_start(out=w2qb[o % 2][:, :].rearrange("p (k c2) -> p k c2", k=KF),
                                            in_=w2q_sp[o])
                        ppissue(e, en, s_w2r, i, instr)

                def te_ftr(e, en, tg, o):
                    i = tg * OT2 + o
                    wait(e, en, s_fv, i + 1)
                    if i >= 1:
                        wait(e, en, s_oa, i)
                    for j in range(4):
                        instr = e.transpose(psA[:, j * 128:(j + 1) * 128],
                                            fo_b[:, j * 128:(j + 1) * 128], identb_sb[:, :])
                    s_ftr.inc(instr)

                def v_oa(e, en, tg, o):
                    i = tg * OT2 + o
                    wait(e, en, s_ftr, i + 1)
                    instr = e.tensor_copy(
                        bass.AP(oasm, o * 128, [[4 * 2048, 128], [2048, 4], [1, 128]]),
                        psA[:, 0:512].rearrange("p (j c2) -> p j c2", j=4))
                    s_oa.inc(instr)

                @on("tensor")
                def _(e, en, tg=tg):
                    wait(e, en, s_hq, (tg + 1) * KF)
                    for o in range(OT2):
                        i = tg * OT2 + o
                        ppwait(e, en, s_w2r, i)
                        if i >= 2:
                            wait(e, en, s_fv, i - 1)
                        for kt in range(KF):
                            instr = e.matmul(ps[o % 2][:, :], w2qb[o % 2][:, kt * 128:(kt + 1) * 128],
                                             bass.AP(hq_sb, kt * 512, [[KF * 512, 128], [1, 512]]),
                                             start=(kt == 0), stop=(kt == KF - 1))
                        s_mm4.inc(instr)
                        if o >= 1:
                            te_ftr(e, en, tg, o - 1)
                    te_ftr(e, en, tg, OT2 - 1)

                @on("vector")
                def _(e, en, tg=tg):
                    for o in range(OT2):
                        i = tg * OT2 + o
                        wait(e, en, s_mm4, i + 1)
                        if i >= 1:
                            wait(e, en, s_ftr, i)
                        schain(e, en, e.tensor_tensor(hq_f[:, :], ps[o % 2][:, :],
                                        inv2_rep[:, tg * 512:(tg + 1) * 512], ALU.mult))
                        instr = e.tensor_scalar(fo_b[:, :], hq_f[:, :], b2cols[:, o:o + 1],
                                                None, ALU.add)
                        s_fv.inc(instr)
                        if o >= 1:
                            v_oa(e, en, tg, o - 1)
                    v_oa(e, en, tg, OT2 - 1)

                @on("gpsimd")
                def _(e, en, tg=tg):
                    for j in range(4):
                        t = tg * 4 + j
                        ii = tg * 4 + j
                        if ii >= 1:
                            wait(e, en, s_fadd, ii)
                        instr = e.dma_start(out=aacc_in[j % 2][:, :], in_=aacc_sp[t])
                        s_aal.dinc(instr)

                @on("vector")
                def _(e, en, tg=tg):
                    wait(e, en, s_oa, (tg + 1) * OT2)
                    for j in range(4):
                        ii = tg * 4 + j
                        wait(e, en, s_aal, 16 * (ii + 1))
                        if ii >= 1:
                            wait(e, en, s_outw, 16 * ii)
                        instr = e.tensor_tensor(fout[:, :], oasm[:, j * 2048:(j + 1) * 2048],
                                                aacc_in[j % 2][:, :], ALU.add)
                        s_fadd.inc(instr)

                @on("scalar")
                def _(e, en, tg=tg):
                    for j in range(4):
                        t = tg * 4 + j
                        ii = tg * 4 + j
                        wait(e, en, s_fadd, ii + 1)
                        instr = e.dma_start(out=out[t * 128:(t + 1) * 128, :], in_=fout[:, :])
                        s_outw.dinc(instr)

            @on("sync")
            def _(e, en):
                wait(e, en, sem("p5_out"), 16 * 16)
                e.nop()

    return nc


def _consts():
    m = np.full((128, 512), MASK_NEG, dtype=np.float32)
    pp = np.arange(128)[:, None]
    ll = np.arange(256)[None, :]
    m[:, 0:256] = np.where(pp <= ll, 0.0, MASK_NEG)
    m[:, 256:512] = np.where((128 + pp) <= ll, 0.0, MASK_NEG)
    return dict(identb=np.eye(128, dtype=ml_dtypes.bfloat16),
                identf=np.eye(128, dtype=np.float32),
                onesrow_b=np.ones((1, 128), dtype=ml_dtypes.bfloat16),
                onescol_b=np.ones((128, 1), dtype=ml_dtypes.bfloat16),
                onesrow_f=np.ones((1, 128), dtype=np.float32),
                maskt=m,
                sel2in=np.concatenate([
                    np.concatenate([np.ones((1, 128), np.float32), np.zeros((1, 128), np.float32)], 1),
                    np.concatenate([np.zeros((1, 128), np.float32), np.ones((1, 128), np.float32)], 1)], 0))


_NC = None


def kernel(L0_attn_u, L0_attn_W_in, L0_attn_conv_w, L0_attn_conv_b, L0_attn_A_log,
           L0_attn_D, L0_attn_dt_bias, L0_attn_norm_w, L0_attn_W_out,
           L0_ffn_w1, L0_ffn_b1, L0_ffn_ln_g, L0_ffn_ln_b, L0_ffn_w2, L0_ffn_b2):
    global _NC
    if _NC is None:
        _NC = build_nc()
    nc = _NC

    win_pad = np.zeros((NOT_IN * 128, DM), dtype=np.float32)
    win_pad[:8512] = L0_attn_W_in
    winT = np.ascontiguousarray(win_pad.T.reshape(KD, 128, NOT_IN, 128).transpose(2, 1, 0, 3))
    w1T = np.ascontiguousarray(L0_ffn_w1.T.reshape(KD, 128, NOT_F, 128).transpose(2, 1, 0, 3))
    w2T = np.ascontiguousarray(L0_ffn_w2.T.reshape(KF, 128, OT2, 128).transpose(2, 1, 0, 3))
    woutT = np.ascontiguousarray(L0_attn_W_out.T)

    colmaj = lambda v, k: np.ascontiguousarray(np.asarray(v).reshape(k, 128).T)
    shared = dict(winT=winT, w1T=w1T, w2T=w2T, woutT=woutT,
                  convw=np.ascontiguousarray(
                      L0_attn_conv_w.reshape(NOT_XBC, 128, 4).transpose(1, 0, 2).reshape(128, NOT_XBC * 4)),
                  convb=colmaj(L0_attn_conv_b, NOT_XBC),
                  a_log=np.ascontiguousarray(L0_attn_A_log.reshape(64, 1)),
                  d_in=np.ascontiguousarray(L0_attn_D.reshape(1, 64)),
                  dt_bias=np.ascontiguousarray(L0_attn_dt_bias.reshape(64, 1)),
                  norm_w=colmaj(L0_attn_norm_w, 32), b1=colmaj(L0_ffn_b1, NOT_F),
                  ln_g=colmaj(L0_ffn_ln_g, NOT_F), ln_b=colmaj(L0_ffn_ln_b, NOT_F),
                  b2=colmaj(L0_ffn_b2, OT2), **_consts())
    in_maps = []
    for b in range(8):
        m = dict(shared)
        m["u"] = np.ascontiguousarray(L0_attn_u[b])
        in_maps.append(m)

    res = run_bass_kernel_spmd(nc, in_maps, core_ids=list(range(8)))
    return np.stack([np.asarray(res.results[b]["out"], dtype=np.float32) for b in range(8)])


